# version 05
# baseline (speedup 1.0000x reference)
import numpy as np
import ml_dtypes
import concourse.bass as bass
import concourse.mybir as mybir
from concourse.bass_utils import run_bass_kernel_spmd
from contextlib import ExitStack

F32 = mybir.dt.float32
BF16 = mybir.dt.bfloat16
AF = mybir.ActivationFunctionType
ALU = mybir.AluOpType
AX = mybir.AxisListType

B, S, D = 2, 8192, 1024
H = 16
FF, NE = 4096, 8
TOWN, NCH = 2048, 16
NG, GSZ = 16, 4
NPAIR = 8
SC = 1.0 / 32.0
EPS_LN, EPS_RMS = 1e-5, 1e-6


def _pieces(g):
    qlo = 128 * g
    out = []
    p = qlo
    fe = min(2048, (qlo // 512 + 1) * 512)
    if fe > p:
        out.append((p, fe))
        p = fe
    while p < 2048:
        out.append((p, p + 512))
        p += 512
    return out


def _lspans(g):
    qlo = 128 * g
    out = []
    p = qlo
    fe = min(TOWN, (qlo // 256 + 1) * 256)
    if fe > p:
        out.append((p, fe))
        p = fe
    while p < TOWN:
        out.append((p, p + 256))
        p += 256
    return out


class Sem:
    def __init__(self, nc, name):
        self.h = nc.alloc_semaphore(name)
        self.n = 0

    def inc(self, instr, k=1):
        instr.then_inc(self.h, k)
        self.n += k
        return self.n


def _build(dbg, upto, use, rec, sub=""):
    nc = bass.Bass("TRN2", target_bir_lowering=False)
    DT = nc.dram_tensor
    x_own = DT("x_own", [TOWN, D], F32, kind="ExternalInput").ap()
    x_str = DT("x_str", [S, D], F32, kind="ExternalInput").ap()
    wqkv = DT("wqkv", [D, 3 * D], F32, kind="ExternalInput").ap()
    w_o_in = DT("w_o_in", [D, D], F32, kind="ExternalInput").ap()
    rms_bc = DT("rms_bc", [128, D], F32, kind="ExternalInput").ap()
    gate_t = DT("gate_t", [D, NE], F32, kind="ExternalInput").ap()
    b1_t = DT("b1_t", [128, NE * 32], F32, kind="ExternalInput").ap()
    b2_in = DT("b2_in", [NE, D], F32, kind="ExternalInput").ap()
    w1_2d = DT("w1_2d", [NE * D, FF], F32, kind="ExternalInput").ap()
    w2_2d = DT("w2_2d", [NE * FF, D], F32, kind="ExternalInput").ap()
    masks = DT("masks", [4, 128, 128], F32, kind="ExternalInput").ap()
    ident = DT("ident", [128, 128], F32, kind="ExternalInput").ap()
    out_t = DT("out", [TOWN, D], F32, kind="ExternalOutput").ap()
    dkind = "ExternalOutput" if dbg else "Internal"
    xn2t_sp = DT("xn2t_sp", [128, 8 * TOWN], BF16, kind=dkind).ap()
    attnx_sp = DT("attnx_sp", [TOWN, D], F32, kind=dkind).ap()
    l_tab = DT("l_tab", [H, NG, TOWN], F32, kind=dkind).ap()
    l_tab2 = l_tab.rearrange("h g q -> (h g) q")
    if dbg:
        d_qt = DT("d_qt", [128, NPAIR * TOWN], BF16, kind="ExternalOutput").ap()
        d_coef = DT("d_coef", [128, NCH * NE], F32, kind="ExternalOutput").ap()
        d_ot = DT("d_ot", [128, 8 * TOWN], F32, kind="ExternalOutput").ap()

    coeft_sp = DT("coeft_sp", [8, NCH * 128], F32, kind="Internal").ap()
    coef_sp = DT("coef_sp", [128, NCH * NE], F32, kind="Internal").ap()
    ctx = ExitStack()
    E = ctx.enter_context
    maskb = E(nc.sbuf_tensor("maskb", [128, 4 * 128], BF16))
    id_b = E(nc.sbuf_tensor("id_b", [128, 128], BF16))
    eps5 = E(nc.sbuf_tensor("eps5", [128, 1], F32))
    ctxB = ExitStack()
    EB = ctxB.enter_context
    wq_kv = EB(nc.sbuf_tensor("wq_kv", [128, 8 * 2048], BF16, side="right"))
    QT = [EB(nc.sbuf_tensor(f"qt{g}", [128, TOWN], BF16, side="right")) for g in range(NPAIR)]

    soV = Sem(nc, "soV")
    soA = Sem(nc, "soA")
    soT = Sem(nc, "soT")
    sw = Sem(nc, "s_w")
    szf = Sem(nc, "s_zf")
    ssp = Sem(nc, "s_sp")
    sldma = Sem(nc, "s_ldma")
    smc = Sem(nc, "s_mc")
    sout_ = Sem(nc, "s_out")
    sxl = Sem(nc, "s_xl")
    sxsb = [Sem(nc, f"s_xs{i}") for i in range(4)]
    slt = [Sem(nc, f"s_lt{i}") for i in range(2)]
    sxo = [Sem(nc, f"s_xo{i}") for i in range(2)]
    sax = [Sem(nc, f"s_ax{i}") for i in range(2)]
    sw1 = [Sem(nc, f"s_w1{i}") for i in range(2)]
    sw2 = [Sem(nc, f"s_w2{i}") for i in range(2)]
    sat = [Sem(nc, f"s_at{i}") for i in range(2)]
    sot = [Sem(nc, f"s_ot{i}") for i in range(2)]
    slp = [Sem(nc, f"s_lp{i}") for i in range(2)]
    stp = [Sem(nc, f"s_tp{i}") for i in range(2)]

    def cv(v, name, *a, **kw):
        v.wait_ge(soV.h, soV.n)
        ins = getattr(v, name)(*a, **kw)
        soV.inc(ins)
        return ins

    def cs(s, name, *a, **kw):
        s.wait_ge(soA.h, soA.n)
        ins = getattr(s, name)(*a, **kw)
        soA.inc(ins)
        return ins

    def mark(key, sem):
        rec[key] = sem.n

    def wg(eng, sem, key):
        eng.wait_ge(sem.h, use.get(key, 0))

    with ExitStack() as sctx:
        zb = sctx.enter_context(nc.sbuf_tensor("zb", [64, 8192], F32))
        with nc.Block("setup") as blk:
            @blk.gpsimd
            def _(g):
                sw.inc(g.dma_start(wq_kv[:].rearrange("p (c j) -> p c j", c=8),
                                   wqkv[:, D:3 * D].rearrange("(c p) j -> p c j", p=128)), 16)
                sw.inc(g.dma_start(maskb[:].rearrange("p (r q) -> p r q", r=4),
                                   masks.rearrange("r k q -> k r q")), 16)
                sw.inc(g.dma_start(id_b[:], ident), 16)

            @blk.vector
            def _(v):
                cv(v, "memset", zb[:], 0.0)
                cv(v, "memset", eps5[:], EPS_LN)
                mark("zb", soV)

            @blk.sync
            def _(sy):
                wg(sy, soV, "zb")
                lt_flat = l_tab.rearrange("h g q -> (h g q)").rearrange("(a b) -> a b", a=64)
                szf.inc(sy.dma_start(lt_flat[:, :], zb[:]), 16)
                sy.wait_ge(szf.h, 16)

    with ExitStack() as actx:
        Ea = actx.enter_context
        wq_q = Ea(nc.sbuf_tensor("wq_q", [128, 8 * D], BF16))
        rmsb = Ea(nc.sbuf_tensor("rmsb", [128, D], F32))
        coefT = Ea(nc.sbuf_tensor("coefT", [8, NCH * 128], F32))
        id_f = Ea(nc.sbuf_tensor("id_f", [128, 128], F32))
        coef = Ea(nc.sbuf_tensor("coef", [128, NCH * NE], F32))
        eps6 = Ea(nc.sbuf_tensor("eps6", [128, 1], F32))
        ghi = Ea(nc.sbuf_tensor("ghi", [128, 8 * NE], BF16))
        glo = Ea(nc.sbuf_tensor("glo", [128, 8 * NE], BF16))
        gf = Ea(nc.sbuf_tensor("gf", [128, 8 * NE], F32))
        xbuf = [Ea(nc.sbuf_tensor(f"a0x{i}", [128, D], F32)) for i in range(4)]
        xnb = [Ea(nc.sbuf_tensor(f"a0n{i}", [128, D], BF16)) for i in range(2)]
        xn2b = [Ea(nc.sbuf_tensor(f"a0n2{i}", [128, D], BF16)) for i in range(2)]
        lob = [Ea(nc.sbuf_tensor(f"a0lo{i}", [128, D], BF16)) for i in range(2)]
        xnf = Ea(nc.sbuf_tensor("a0xnf", [128, D], F32))
        xrf = Ea(nc.sbuf_tensor("a0xrf", [128, D], F32))
        x2f = Ea(nc.sbuf_tensor("a0x2f", [128, D], F32))
        bst = Ea(nc.sbuf_tensor("a0bst", [128, 48], F32))
        mv = Ea(nc.sbuf_tensor("a0mv", [128, 8], F32))
        xnt = Ea(nc.sbuf_tensor("a0xnt", [128, 8 * 512], BF16))
        x2t = Ea(nc.sbuf_tensor("a0x2t", [128, 8 * 512], BF16))
        lot = [Ea(nc.sbuf_tensor(f"a0lot{i}", [128, 8 * 128], BF16)) for i in range(2)]
        cst = {nm: Ea(nc.sbuf_tensor("a0" + nm, [128, 4], F32))
               for nm in "vv rec ms lnv rstd lnm cc".split()}
        g16 = {nm: Ea(nc.sbuf_tensor("a0g" + nm, [128, 16], F32))
               for nm in "gm ngm gs gr".split()}
        Lsb = Ea(nc.sbuf_tensor("a0L", [128, NCH * NE], F32))
        egs = Ea(nc.sbuf_tensor("a0eg", [128, NE], F32))
        ps_tr = [Ea(nc.psum_tensor(f"a0ptr{i}", [128, 1024], BF16)) for i in range(2)]
        ps_q = [Ea(nc.psum_tensor(f"a0pq{i}", [128, 512], F32)) for i in range(2)]
        ps_g = [Ea(nc.psum_tensor(f"a0pg{i}", [128, NE], F32)) for i in range(2)]
        ps_ct = Ea(nc.psum_tensor("a0pct", [8, 128], F32))

        with nc.Block("a0") as blk:
            @blk.gpsimd
            def _(g):
                sw.inc(g.dma_start(wq_q[:].rearrange("p (c j) -> p c j", c=8),
                                   wqkv[:, 0:D].rearrange("(c p) j -> p c j", p=128)), 16)

            @blk.sync
            def _(sy):
                sxl.inc(sy.dma_start(rmsb[:], rms_bc), 16)
                sxl.inc(sy.dma_start(id_f[:], ident), 16)
                sxl.inc(sy.dma_start(gf[:].rearrange("p (c e) -> p c e", c=8),
                                     gate_t.rearrange("(c p) e -> p c e", p=128)), 16)

                def spills(st):
                    if "nospill" in sub:
                        return
                    wg(sy, soA, ("a0te_ch", st * 4 + 3))
                    for cd in range(8):
                        ssp.inc(sy.dma_start(xn2t_sp[:, cd * TOWN + st * 512:cd * TOWN + (st + 1) * 512],
                                             x2t[:, cd * 512:(cd + 1) * 512]), 16)
                for st in range(4):
                    for i in range(4):
                        ch = st * 4 + i
                        if ch >= 4:
                            wg(sy, soV, ("a0ln", ch - 4))
                        sxsb[ch % 4].inc(sy.dma_start(xbuf[ch % 4][:], x_own[ch * 128:(ch + 1) * 128, :]), 16)
                    if st >= 1:
                        spills(st - 1)
                spills(3)

            @blk.vector
            def _(v):
                c = cst
                cv(v, "memset", eps6[:], EPS_RMS)
                v.wait_ge(sxl.h, 48)
                cv(v, "tensor_copy", ghi[:], gf[:])
                cv(v, "scalar_tensor_tensor", glo[:], ghi[:], -1.0, gf[:], ALU.mult, ALU.add)
                mark("gate_hl", soV)
                for st in range(4):
                    for i in range(4):
                        ch = st * 4 + i
                        v.wait_ge(sxsb[ch % 4].h, 16 * (ch // 4 + 1))
                        cv(v, "bn_stats", bst[:, i * 12:i * 12 + 6], xbuf[ch % 4][:, 0:512])
                        cv(v, "bn_stats", bst[:, i * 12 + 6:i * 12 + 12], xbuf[ch % 4][:, 512:1024])
                    for i in range(4):
                        cv(v, "bn_aggr", mv[:, i * 2:i * 2 + 2], bst[:, i * 12:i * 12 + 12])
                    for i in range(4):
                        cv(v, "tensor_scalar", c["vv"][:, i:i + 1], mv[:, 2 * i + 1:2 * i + 2],
                           EPS_LN, None, ALU.add)
                    for i in range(4):
                        cv(v, "reciprocal", c["rec"][:, i:i + 1], c["vv"][:, i:i + 1])
                    for i in range(4):
                        cv(v, "tensor_mul", c["ms"][:, i:i + 1], mv[:, 2 * i + 1:2 * i + 2],
                           c["rec"][:, i:i + 1])
                    mark(("a0var", st), soV)
                    wg(v, soA, ("a0rstd", st))
                    for i in range(4):
                        ch = st * 4 + i
                        if ch >= 2:
                            wg(v, soT, ("a0tr", ch - 2))
                        cv(v, "tensor_scalar", xnf[:], xbuf[ch % 4][:], mv[:, 2 * i:2 * i + 1],
                           c["rstd"][:, i:i + 1], ALU.subtract, ALU.mult)
                        cv(v, "tensor_copy", xnb[ch % 2][:], xnf[:])
                        cv(v, "tensor_mul", xrf[:], xnf[:], rmsb[:])
                        cv(v, "tensor_scalar", x2f[:], xrf[:], c["cc"][:, i:i + 1], None, ALU.mult)
                        cv(v, "tensor_copy", xn2b[ch % 2][:], x2f[:])
                        cv(v, "scalar_tensor_tensor", lob[ch % 2][:], xn2b[ch % 2][:], -1.0, x2f[:],
                           ALU.mult, ALU.add)
                        mark(("a0ln", ch), soV)

            @blk.tensor
            def _(t):
                if "notr" in sub:
                    return
                t.wait_ge(sw.h, 64)
                wg(t, soV, "gate_hl")
                for st in range(4):
                    for i in range(4):
                        ch = st * 4 + i
                        wg(t, soV, ("a0ln", ch))
                        for sj, src in enumerate((xnb[ch % 2], xn2b[ch % 2], lob[ch % 2])):
                            for j in range(8):
                                ntr = 24 * ch + 8 * sj + j
                                if ntr >= 2:
                                    wg(t, soA, ("a0te", ntr - 2))
                                soT.inc(t.transpose(ps_tr[ntr % 2][:, 0:128],
                                                    src[:, j * 128:(j + 1) * 128], id_b[:]))
                                mark(("a0trj", ntr), soT)
                        mark(("a0tr", ch), soT)
                        if "nogate" in sub:
                            continue
                        wg(t, soA, ("a0te_ch", ch))
                        if ch >= 2:
                            wg(t, soA, ("a0gl", ch - 2))
                        nmm = 0
                        ins = None
                        for cd in range(8):
                            for src2 in range(3):
                                if src2 == 0:
                                    lhsT, rh = x2t[:, cd * 512 + i * 128:cd * 512 + i * 128 + 128], ghi
                                elif src2 == 1:
                                    lhsT, rh = x2t[:, cd * 512 + i * 128:cd * 512 + i * 128 + 128], glo
                                else:
                                    lhsT, rh = lot[ch % 2][:, cd * 128:(cd + 1) * 128], ghi
                                nmm += 1
                                ins = t.matmul(ps_g[ch % 2][:], lhsT, rh[:, cd * NE:(cd + 1) * NE],
                                               start=(nmm == 1), stop=(nmm == 24))
                        soT.inc(ins)
                        mark(("a0gate", ch), soT)
                    for gp in range(8):
                        if "noqt" in sub:
                            break
                        k = st * 8 + gp
                        if k >= 2:
                            wg(t, soA, ("a0qe", k - 2))
                        ins = None
                        for cd in range(8):
                            ins = t.matmul(ps_q[k % 2][:], wq_q[:, cd * D + gp * 128:cd * D + (gp + 1) * 128],
                                           xnt[:, cd * 512:(cd + 1) * 512], start=(cd == 0), stop=(cd == 7))
                        soT.inc(ins)
                        mark(("a0qt", k), soT)
                if "nogate" not in sub:
                    wg(t, soV, ("a0coef", NCH - 1))
                    for ch in range(NCH):
                        if ch >= 1:
                            wg(t, soV, ("a0ctv", ch - 1))
                        soT.inc(t.transpose(ps_ct[:], coef[:, ch * NE:(ch + 1) * NE], id_f[:]))
                        mark(("a0ct", ch), soT)

            @blk.scalar
            def _(s):
                c = cst
                for st in range(4):
                    wg(s, soV, ("a0var", st))
                    for i in range(4):
                        cs(s, "activation", c["lnv"][:, i:i + 1], mv[:, 2 * i + 1:2 * i + 2], AF.Ln, bias=eps5[:])
                    for i in range(4):
                        cs(s, "activation", c["lnm"][:, i:i + 1], c["ms"][:, i:i + 1], AF.Ln, bias=eps6[:])
                    for i in range(4):
                        cs(s, "activation", c["rstd"][:, i:i + 1], c["lnv"][:, i:i + 1], AF.Exp, scale=-0.5)
                    for i in range(4):
                        cs(s, "activation", c["cc"][:, i:i + 1], c["lnm"][:, i:i + 1], AF.Exp, scale=-0.5)
                    mark(("a0rstd", st), soA)
                    if "notr" in sub or "noev" in sub:
                        continue
                    for i in range(4):
                        ch = st * 4 + i
                        if ch >= 2:
                            wg(s, soT, ("a0gate", ch - 2))
                        if i == 0 and st >= 1:
                            wg(s, soT, ("a0qt", st * 8 - 1))
                            s.wait_ge(ssp.h, 128 * st)
                        for sj, dst in enumerate(("xnt", "x2t", "lot")):
                            for j in range(8):
                                ntr = 24 * ch + 8 * sj + j
                                wg(s, soT, ("a0trj", ntr))
                                psl = ps_tr[ntr % 2][:, 0:128]
                                if dst == "xnt":
                                    cs(s, "activation", xnt[:, j * 512 + i * 128:j * 512 + i * 128 + 128],
                                       psl, AF.Copy)
                                elif dst == "x2t":
                                    cs(s, "activation", x2t[:, j * 512 + i * 128:j * 512 + i * 128 + 128],
                                       psl, AF.Copy)
                                else:
                                    cs(s, "activation", lot[ch % 2][:, j * 128:(j + 1) * 128], psl, AF.Copy)
                                mark(("a0te", ntr), soA)
                        mark(("a0te_ch", ch), soA)
                        if "nogate" not in sub:
                            wg(s, soT, ("a0gate", ch))
                            cs(s, "activation", Lsb[:, ch * NE:(ch + 1) * NE], ps_g[ch % 2][:], AF.Copy)
                            mark(("a0gl", ch), soA)
                    for gp in range(8):
                        if "noqt" in sub:
                            break
                        k = st * 8 + gp
                        wg(s, soT, ("a0qt", k))
                        cs(s, "activation", QT[gp][:, st * 512:(st + 1) * 512], ps_q[k % 2][:], AF.Copy)
                        mark(("a0qe", k), soA)
                if "notr" not in sub and "nogate" not in sub:
                    for ch in range(NCH):
                        wg(s, soV, ("a0ngm", ch))
                        cs(s, "activation", egs[:], Lsb[:, ch * NE:(ch + 1) * NE], AF.Exp,
                           bias=g16["ngm"][:, ch:ch + 1], accum_out=g16["gs"][:, ch:ch + 1])
                        mark(("a0ga", ch), soA)

            @blk.vector
            def _(v):
                if "notr" in sub or "nogate" in sub:
                    return
                for ch in range(NCH):
                    wg(v, soA, ("a0gl", ch))
                    cv(v, "reduce_max", g16["gm"][:, ch:ch + 1], Lsb[:, ch * NE:(ch + 1) * NE], axis=AX.X)
                for ch in range(NCH):
                    cv(v, "tensor_scalar", g16["ngm"][:, ch:ch + 1], g16["gm"][:, ch:ch + 1],
                       -1.0, None, ALU.mult)
                    mark(("a0ngm", ch), soV)
                for ch in range(NCH):
                    wg(v, soA, ("a0ga", ch))
                    cv(v, "reciprocal", g16["gr"][:, ch:ch + 1], g16["gs"][:, ch:ch + 1])
                for ch in range(NCH):
                    cv(v, "tensor_scalar", coef[:, ch * NE:(ch + 1) * NE],
                       Lsb[:, ch * NE:(ch + 1) * NE],
                       g16["gm"][:, ch:ch + 1], g16["gr"][:, ch:ch + 1], ALU.is_equal, ALU.mult)
                    mark(("a0coef", ch), soV)
                for ch in range(NCH):
                    wg(v, soT, ("a0ct", ch))
                    cv(v, "tensor_copy", coefT[:, ch * 128:(ch + 1) * 128], ps_ct[:])
                    mark(("a0ctv", ch), soV)

            @blk.sync
            def _(sy):
                if "nogate" not in sub:
                    wg(sy, soV, ("a0ctv", NCH - 1))
                    ssp.inc(sy.dma_start(coeft_sp, coefT[:]), 16)
                    wg(sy, soV, ("a0coef", NCH - 1))
                    ssp.inc(sy.dma_start(coef_sp, coef[:]), 16)
                if dbg and "notr" not in sub:
                    wg(sy, soA, ("a0qe", 31))
                    for gp in range(NPAIR):
                        ssp.inc(sy.dma_start(d_qt[:, gp * TOWN:(gp + 1) * TOWN], QT[gp][:]), 16)
                    wg(sy, soV, ("a0coef", NCH - 1))
                    ssp.inc(sy.dma_start(d_coef, coef[:]), 16)
                sy.wait_ge(ssp.h, ssp.n)

    if upto == "a0":
        with nc.Block("fin0") as blk:
            @blk.sync
            def _(sy):
                sout_.inc(sy.dma_start(out_t, x_own), 16)
                sy.wait_ge(sout_.h, 16)
        ctx.close()
        return nc

    scg = []
    tot = 0
    for g in range(NG):
        tot += H * 4 * len(_pieces(g))
        scg.append(tot)

    with ExitStack() as wctx:
        OTacc = wctx.enter_context(nc.sbuf_tensor("OTacc", [128, 8 * TOWN], F32))
        with ExitStack() as a1c:
            E1 = a1c.enter_context
            xs = [E1(nc.sbuf_tensor(f"a1x{i}", [128, D], F32)) for i in range(4)]
            xnb1 = [E1(nc.sbuf_tensor(f"a1n{i}", [128, D], BF16)) for i in range(2)]
            bst1 = E1(nc.sbuf_tensor("a1bst", [128, 48], F32))
            mv1 = E1(nc.sbuf_tensor("a1mv", [128, 8], F32))
            xnt1 = E1(nc.sbuf_tensor("a1xnt", [128, 8 * 512], BF16))
            ktg = E1(nc.sbuf_tensor("a1kt", [128, 8 * 512], BF16))
            vg = [E1(nc.sbuf_tensor(f"a1v{i}", [128, H * 65], BF16)) for i in range(4)]
            PT = [[E1(nc.sbuf_tensor(f"a1pt{s0}_{k}", [128, TOWN], BF16)) for k in range(4)]
                  for s0 in range(2)]
            tmpl = [E1(nc.sbuf_tensor(f"a1tl{i}", [1, 256], F32)) for i in range(2)]
            c1 = {nm: E1(nc.sbuf_tensor("a1" + nm, [128, 4], F32))
                  for nm in "lnv rstd".split()}
            ps_s = E1(nc.psum_tensor("a1ps", [128, 2048], F32))
            ps_big = E1(nc.psum_tensor("a1pb", [128, 2048], F32))

            nld_tab = {}
            nld = 0
            for g in range(NG):
                for h in range(H):
                    for li in range(len(_lspans(g))):
                        nld_tab[(g, h, li)] = nld
                        nld += 1
            TOT_L = nld

            with nc.Block("a1") as blk:
                @blk.sync
                def _(sy):
                    def transposes(j):
                        if j >= 1:
                            wg(sy, soT, ("a1vm", j - 1, 3))
                        for i in range(4):
                            s_ = 4 * j + i
                            wg(sy, soV, ("a1xnb", s_))
                            for b_ in range(8):
                                stp[s_ % 2].inc(sy.dma_start(
                                    xnt1[:, b_ * 512 + i * 128:b_ * 512 + i * 128 + 128],
                                    xnb1[s_ % 2][:, b_ * 128:(b_ + 1) * 128],
                                    transpose=True), 16)

                    for g in range(NG):
                        for i in range(4):
                            s_ = 4 * g + i
                            if s_ >= 4:
                                wg(sy, soV, ("a1xnb", s_ - 4))
                            sxsb[s_ % 4].inc(sy.dma_start(xs[s_ % 4][:], x_str[s_ * 128:(s_ + 1) * 128, :]), 16)
                        transposes(g)

                @blk.gpsimd
                def _(g_):
                    def lstores(g):
                        for h in range(H):
                            for li, (a, b) in enumerate(_lspans(g)):
                                nl = nld_tab[(g, h, li)]
                                wg(g_, soV, ("a1lev", g, h, li))
                                slp[nl % 2].inc(g_.dma_start(
                                    l_tab2[h * NG + g:h * NG + g + 1, a:b],
                                    tmpl[nl % 2][0:1, 0:b - a]), 16)

                    for g in range(NG):
                        lstores(g)
                    g_.wait_ge(slp[0].h, slp[0].n)
                    g_.wait_ge(slp[1].h, slp[1].n)

                @blk.vector
                def _(v):
                    for i in range(4):
                        cv(v, "memset", vg[i][:].rearrange("p (h j) -> p h j", j=65)[:, :, 64:65], 1.0)

                    def stats(g):
                        for i in range(4):
                            s_ = 4 * g + i
                            v.wait_ge(sxsb[s_ % 4].h, 16 * (4 + s_ // 4 + 1))
                            cv(v, "bn_stats", bst1[:, i * 12:i * 12 + 6], xs[s_ % 4][:, 0:512])
                            cv(v, "bn_stats", bst1[:, i * 12 + 6:i * 12 + 12], xs[s_ % 4][:, 512:1024])
                        for i in range(4):
                            cv(v, "bn_aggr", mv1[:, i * 2:i * 2 + 2], bst1[:, i * 12:i * 12 + 12])
                        mark(("a1var", g), soV)

                    def xnb_one(g, i):
                        if i == 0:
                            wg(v, soA, ("a1rstd", g))
                        s_ = 4 * g + i
                        if s_ >= 2:
                            v.wait_ge(stp[s_ % 2].h, 16 * 8 * (s_ // 2))
                        cv(v, "tensor_scalar", xnb1[s_ % 2][:], xs[s_ % 4][:], mv1[:, 2 * i:2 * i + 1],
                           c1["rstd"][:, i:i + 1], ALU.subtract, ALU.mult)
                        mark(("a1xnb", s_), soV)

                    def kt_ev(j):
                        for p8 in range(8):
                            rb = (p8 % 2) * 512
                            wg(v, soT, ("a1ktm", j, p8))
                            if j >= 1 and p8 == 0:
                                wg(v, soT, ("a1smm", scg[j - 1] - 1))
                            cv(v, "tensor_copy", ktg[:, p8 * 512:(p8 + 1) * 512],
                               ps_big[:, rb:rb + 512])
                            mark(("a1kt", j, p8), soV)

                    def xnb_ev(j):
                        for i in range(4):
                            xnb_one(j, i)
                        kt_ev(j)

                    stats(0)
                    xnb_ev(0)
                    for g in range(NG):
                        for i in range(4):
                            wg(v, soT, ("a1vm", g, i))
                            if g >= 1:
                                wg(v, soT, ("a1pv", g - 1, 15))
                            rb = 1024 * (i % 2)
                            cv(v, "tensor_copy",
                               vg[i][:].rearrange("p (h j) -> p h j", j=65)[:, :, 0:64],
                               ps_big[:, rb:rb + 1024].rearrange("p (h j) -> p h j", j=64))
                            mark(("a1ve", g, i), soV)
                        if g + 1 < NG:
                            stats(g + 1)
                        qlo = 128 * g
                        for h in range(H):
                            wg(v, soT, ("a1pv", g, h))
                            hc, r0 = h // 2, 64 * (h % 2)
                            dstap = OTacc[r0:r0 + 64, hc * TOWN + qlo:(hc + 1) * TOWN]
                            if g == 0:
                                cv(v, "tensor_copy", dstap, ps_big[0:64, qlo:TOWN])
                            else:
                                cv(v, "tensor_add", dstap, dstap, ps_big[0:64, qlo:TOWN])
                            for li, (a, b) in enumerate(_lspans(g)):
                                nl = nld_tab[(g, h, li)]
                                if nl >= 2:
                                    v.wait_ge(slp[nl % 2].h, 16 * (nl // 2))
                                cv(v, "tensor_copy", tmpl[nl % 2][0:1, 0:b - a], ps_big[64:65, a:b])
                                mark(("a1lev", g, h, li), soV)
                            mark(("a1ote", g, h), soV)
                        if g + 1 < NG:
                            xnb_ev(g + 1)

                @blk.tensor
                def _(t):
                    t.wait_ge(sw.h, 64)

                    def kv(j):
                        t.wait_ge(stp[0].h, 16 * 16 * (j + 1))
                        t.wait_ge(stp[1].h, 16 * 16 * (j + 1))
                        if j >= 1:
                            wg(t, soV, ("a1ote", j - 1, 15))
                        for p8 in range(8):
                            rb = (p8 % 2) * 512
                            if p8 >= 2:
                                wg(t, soV, ("a1kt", j, p8 - 2))
                            ins = None
                            for cd in range(8):
                                ins = t.matmul(ps_big[:, rb:rb + 512],
                                               wq_kv[:, cd * 2048 + p8 * 128:cd * 2048 + (p8 + 1) * 128],
                                               xnt1[:, cd * 512:(cd + 1) * 512],
                                               start=(cd == 0), stop=(cd == 7))
                            soT.inc(ins)
                            mark(("a1ktm", j, p8), soT)
                        wg(t, soV, ("a1kt", j, 7))
                        for i in range(4):
                            rb = 1024 * (i % 2)
                            if i >= 2:
                                wg(t, soV, ("a1ve", j, i - 2))
                            ins = None
                            for cd in range(8):
                                lhsT = xnt1[:, cd * 512 + i * 128:cd * 512 + i * 128 + 128]
                                ins = t.matmul(ps_big[:, rb:rb + 512], lhsT,
                                               wq_kv[:, cd * 2048 + 1024:cd * 2048 + 1536],
                                               start=(cd == 0), stop=(cd == 7))
                                ins = t.matmul(ps_big[:, rb + 512:rb + 1024], lhsT,
                                               wq_kv[:, cd * 2048 + 1536:cd * 2048 + 2048],
                                               start=(cd == 0), stop=(cd == 7))
                            soT.inc(ins)
                            mark(("a1vm", j, i), soT)

                    nsc = [0]

                    def scores(g, h):
                        gp, r0 = h // 2, 64 * (h % 2)
                        for k in range(4):
                            for pi, (a, b) in enumerate(_pieces(g)):
                                jj = nsc[0]
                                nsc[0] += 1
                                if jj >= 4:
                                    wg(t, soA, ("a1expj", jj - 4))
                                ins = t.matmul(ps_s[:, (jj % 4) * 512:(jj % 4) * 512 + (b - a)],
                                               ktg[r0:r0 + 64, gp * 512 + k * 128:gp * 512 + (k + 1) * 128],
                                               QT[gp][r0:r0 + 64, a:b], start=True, stop=(pi != 0))
                                if pi == 0:
                                    ins = t.matmul(ps_s[:, (jj % 4) * 512:(jj % 4) * 512 + 128],
                                                   maskb[:, k * 128:(k + 1) * 128], id_b[:],
                                                   start=False, stop=True)
                                soT.inc(ins)
                                mark(("a1smm", jj), soT)

                    def pv(g, h):
                        wg(t, soA, ("a1exp", g, h))
                        if h == 0:
                            wg(t, soV, ("a1ve", g, 3))
                        if h >= 1:
                            wg(t, soV, ("a1ote", g, h - 1))
                        ins = None
                        for (a, b) in _pieces(g):
                            for k in range(4):
                                ins = t.matmul(ps_big[0:65, a:b], vg[k][:, h * 65:(h + 1) * 65],
                                               PT[h % 2][k][:, a:b], start=(k == 0), stop=(k == 3))
                        soT.inc(ins)
                        mark(("a1pv", g, h), soT)

                    kv(0)
                    for g in range(NG):
                        scores(g, 0)
                        for h in range(H):
                            if h + 1 < H:
                                scores(g, h + 1)
                            pv(g, h)
                        if g + 1 < NG:
                            kv(g + 1)

                @blk.scalar
                def _(s):
                    def rstd_chain(g):
                        wg(s, soV, ("a1var", g))
                        for i in range(4):
                            cs(s, "activation", c1["lnv"][:, i:i + 1], mv1[:, 2 * i + 1:2 * i + 2], AF.Ln, bias=eps5[:])
                        for i in range(4):
                            cs(s, "activation", c1["rstd"][:, i:i + 1], c1["lnv"][:, i:i + 1], AF.Exp, scale=-0.5)
                        mark(("a1rstd", g), soA)

                    rstd_chain(0)
                    nsc = 0
                    for g in range(NG):
                        if g + 1 < NG:
                            rstd_chain(g + 1)
                        for h in range(H):
                            if h >= 2:
                                wg(s, soT, ("a1pv", g, h - 2))
                            elif g >= 1:
                                wg(s, soT, ("a1pv", g - 1, 14 + h))
                            for k in range(4):
                                for (a, b) in _pieces(g):
                                    wg(s, soT, ("a1smm", nsc))
                                    cs(s, "activation", PT[h % 2][k][:, a:b],
                                       ps_s[:, (nsc % 4) * 512:(nsc % 4) * 512 + (b - a)],
                                       AF.Exp, scale=SC)
                                    mark(("a1expj", nsc), soA)
                                    nsc += 1
                            mark(("a1exp", g, h), soA)

        ctxB.close()
        with ExitStack() as woc:
            Ew = woc.enter_context
            OTb = Ew(nc.sbuf_tensor("OTb", [128, 8 * TOWN], BF16))
            on16 = Ew(nc.sbuf_tensor("on16", [16, 128], F32))
            wos = Ew(nc.sbuf_tensor("wos", [128, 8 * D], BF16))
            lts = [Ew(nc.sbuf_tensor(f"lts{i}", [16, TOWN], F32)) for i in range(2)]
            rlb = Ew(nc.sbuf_tensor("rlb", [128, TOWN], F32))
            xo = [Ew(nc.sbuf_tensor(f"xo{i}", [128, D], F32)) for i in range(2)]
            axt = [Ew(nc.sbuf_tensor(f"axt{i}", [128, D], F32)) for i in range(2)]
            ps_lb = Ew(nc.psum_tensor("ps_lb", [128, TOWN], F32))
            ps_wo = [Ew(nc.psum_tensor(f"ps_wo{i}", [128, D], F32)) for i in range(2)]

            A0SP = ssp.n

            with nc.Block("wo") as blk:
                @blk.gpsimd
                def _(g_):
                    sw.inc(g_.dma_start(wos[:].rearrange("p (c j) -> p c j", c=8),
                                        w_o_in.rearrange("(c p) j -> p c j", p=128)), 16)

                @blk.sync
                def _(sy):
                    for h in range(H):
                        if h >= 2:
                            wg(sy, soT, ("wolb", h - 2))
                        slt[h % 2].inc(sy.dma_start(lts[h % 2][:], l_tab2[h * NG:(h + 1) * NG, :]), 16)
                    for tc in range(NCH):
                        if tc >= 2:
                            wg(sy, soV, ("woax", tc - 2))
                        sxo[tc % 2].inc(sy.dma_start(xo[tc % 2][:], x_own[tc * 128:(tc + 1) * 128, :]), 16)
                        if tc >= 2:
                            wg(sy, soV, ("woax", tc - 2))
                            sat[tc % 2].inc(sy.dma_start(attnx_sp[(tc - 2) * 128:(tc - 1) * 128, :],
                                                         axt[tc % 2][:]), 16)
                    for tc in range(NCH - 2, NCH):
                        wg(sy, soV, ("woax", tc))
                        sat[tc % 2].inc(sy.dma_start(attnx_sp[tc * 128:(tc + 1) * 128, :], axt[tc % 2][:]), 16)
                    if dbg:
                        wg(sy, soV, ("wootb", H - 1))
                        ssp.inc(sy.dma_start(d_ot, OTacc[:]), 16)
                    sy.wait_ge(ssp.h, ssp.n)
                    sy.wait_ge(sat[0].h, sat[0].n)
                    sy.wait_ge(sat[1].h, sat[1].n)

                @blk.tensor
                def _(t):
                    t.wait_ge(sw.h, 80)
                    wg(t, soV, "on16")
                    for h in range(H):
                        t.wait_ge(slt[h % 2].h, 16 * (h // 2 + 1))
                        if h >= 1:
                            wg(t, soV, ("worl", h - 1))
                        ins = None
                        for pc in range(4):
                            ins = t.matmul(ps_lb[:, pc * 512:(pc + 1) * 512], on16[:],
                                           lts[h % 2][:, pc * 512:(pc + 1) * 512], start=True, stop=True)
                        soT.inc(ins)
                        mark(("wolb", h), soT)
                    for tc in range(NCH):
                        wg(t, soV, ("wootb", H - 1))
                        if tc >= 2:
                            wg(t, soV, ("woax", tc - 2))
                        ins = None
                        for cd in range(8):
                            lhsT = OTb[:, cd * TOWN + tc * 128:cd * TOWN + (tc + 1) * 128]
                            ins = t.matmul(ps_wo[tc % 2][:, 0:512], lhsT, wos[:, cd * D:cd * D + 512],
                                           start=(cd == 0), stop=(cd == 7))
                            ins = t.matmul(ps_wo[tc % 2][:, 512:1024], lhsT, wos[:, cd * D + 512:(cd + 1) * D],
                                           start=(cd == 0), stop=(cd == 7))
                        soT.inc(ins)
                        mark(("womm", tc), soT)

                @blk.vector
                def _(v):
                    cv(v, "memset", on16[:], 1.0)
                    mark("on16", soV)
                    for h in range(H):
                        wg(v, soT, ("wolb", h))
                        cv(v, "reciprocal", rlb[:], ps_lb[:])
                        mark(("worl", h), soV)
                        hc, r0 = h // 2, 64 * (h % 2)
                        cv(v, "tensor_mul", OTb[r0:r0 + 64, hc * TOWN:(hc + 1) * TOWN],
                           OTacc[r0:r0 + 64, hc * TOWN:(hc + 1) * TOWN], rlb[r0:r0 + 64, :])
                        mark(("wootb", h), soV)
                    for tc in range(NCH):
                        wg(v, soT, ("womm", tc))
                        v.wait_ge(sxo[tc % 2].h, 16 * (tc // 2 + 1))
                        if tc >= 2:
                            v.wait_ge(sat[tc % 2].h, 16 * ((tc - 2) // 2 + 1))
                        cv(v, "tensor_add", axt[tc % 2][:], xo[tc % 2][:], ps_wo[tc % 2][:])
                        mark(("woax", tc), soV)

    A1SP = ssp.n
    with ExitStack() as mctx:
        Em = mctx.enter_context
        y_acc = Em(nc.sbuf_tensor("y_acc", [128, NCH * D], F32))
        X2 = Em(nc.sbuf_tensor("mX2", [128, 8 * TOWN], BF16))
        hsl = Em(nc.sbuf_tensor("mh", [128, 8 * TOWN], BF16))
        w2s = [Em(nc.sbuf_tensor(f"mw2{i}", [128, 8 * D], BF16)) for i in range(2)]
        w1t = [Em(nc.sbuf_tensor(f"mw1{i}", [128, 8 * 128], BF16)) for i in range(2)]
        b1s = Em(nc.sbuf_tensor("mb1", [128, NE * 32], F32))
        b2s = Em(nc.sbuf_tensor("mb2", [8, D], F32))
        coefT = Em(nc.sbuf_tensor("mcoefT", [8, NCH * 128], F32))
        coef = Em(nc.sbuf_tensor("mcoef", [128, NCH * NE], F32))
        axt2 = [Em(nc.sbuf_tensor(f"max{i}", [128, D], F32)) for i in range(2)]
        outt = [Em(nc.sbuf_tensor(f"mo{i}", [128, D], F32)) for i in range(2)]
        ps_h = [Em(nc.psum_tensor(f"mph{i}", [128, 1024], F32)) for i in range(2)]
        ps_y = [Em(nc.psum_tensor(f"mpy{i}", [128, 1024], F32)) for i in range(2)]

        with nc.Block("moe") as blk:
            @blk.sync
            def _(sy):
                sy.wait_ge(ssp.h, A1SP)
                sy.wait_ge(sat[0].h, sat[0].n)
                sy.wait_ge(sat[1].h, sat[1].n)
                smc.inc(sy.dma_start(X2[:], xn2t_sp), 16)
                smc.inc(sy.dma_start(b2s[:], b2_in), 16)
                smc.inc(sy.dma_start(b1s[:], b1_t), 16)
                smc.inc(sy.dma_start(coefT[:], coeft_sp), 16)
                smc.inc(sy.dma_start(coef[:], coef_sp), 16)
                for tc in range(NCH):
                    if tc >= 2:
                        wg(sy, soV, ("mfin", tc - 2))
                    sax[tc % 2].inc(sy.dma_start(axt2[tc % 2][:], attnx_sp[tc * 128:(tc + 1) * 128, :]), 16)
                    if tc >= 2:
                        wg(sy, soV, ("mfin", tc - 2))
                        sot[tc % 2].inc(sy.dma_start(out_t[(tc - 2) * 128:(tc - 1) * 128, :], outt[tc % 2][:]), 16)
                for tc in range(NCH - 2, NCH):
                    wg(sy, soV, ("mfin", tc))
                    sot[tc % 2].inc(sy.dma_start(out_t[tc * 128:(tc + 1) * 128, :], outt[tc % 2][:]), 16)
                sy.wait_ge(sot[0].h, 16 * (NCH // 2))
                sy.wait_ge(sot[1].h, 16 * (NCH // 2))

            @blk.gpsimd
            def _(g_):
                ti = 0
                for e in range(NE):
                    for fq in range(4):
                        q = e * 4 + fq
                        if q >= 2:
                            wg(g_, soT, ("mym", q - 2, NCH - 1))
                        r0 = e * FF + fq * 1024
                        sw2[q % 2].inc(g_.dma_start(w2s[q % 2][:].rearrange("p (c j) -> p c j", c=8),
                                                    w2_2d[r0:r0 + 1024, :].rearrange("(c p) j -> p c j", p=128)), 16)
                        for fc in range(8):
                            if ti >= 2:
                                wg(g_, soT, ("mhm", ti - 2, 1))
                            cl = fq * 1024 + fc * 128
                            sw1[ti % 2].inc(g_.dma_start(
                                w1t[ti % 2][:].rearrange("p (c f) -> p c f", c=8),
                                w1_2d[e * D:(e + 1) * D, cl:cl + 128].rearrange("(c p) f -> p c f", p=128)), 16)
                            ti += 1

            @blk.tensor
            def _(t):
                t.wait_ge(smc.h, 80)
                for tc in range(NCH):
                    if tc >= 2:
                        wg(t, soV, ("mye", -1, tc - 2))
                    ins = t.matmul(ps_y[tc % 2][:, 0:512], coefT[:, tc * 128:(tc + 1) * 128],
                                   b2s[:, 0:512], start=True, stop=True)
                    ins = t.matmul(ps_y[tc % 2][:, 512:1024], coefT[:, tc * 128:(tc + 1) * 128],
                                   b2s[:, 512:1024], start=True, stop=True)
                    soT.inc(ins)
                    mark(("myi", tc), soT)
                ti = 0
                for e in range(NE):
                    for fq in range(4):
                        q = e * 4 + fq
                        for fc in range(8):
                            t.wait_ge(sw1[ti % 2].h, 16 * (ti // 2 + 1))
                            for sub in range(2):
                                if ti > 0:
                                    wg(t, soA, ("msi", ti - 1, sub))
                                ins = None
                                for cd in range(8):
                                    co = cd * TOWN + sub * 1024
                                    ins = t.matmul(ps_h[sub][:, 0:512], w1t[ti % 2][:, cd * 128:(cd + 1) * 128],
                                                   X2[:, co:co + 512], start=(cd == 0), stop=(cd == 7))
                                    ins = t.matmul(ps_h[sub][:, 512:1024], w1t[ti % 2][:, cd * 128:(cd + 1) * 128],
                                                   X2[:, co + 512:co + 1024], start=(cd == 0), stop=(cd == 7))
                                soT.inc(ins)
                                mark(("mhm", ti, sub), soT)
                            ti += 1
                        wg(t, soA, ("msi", q * 8 + 7, 1))
                        t.wait_ge(sw2[q % 2].h, 16 * (q // 2 + 1))
                        for tc in range(NCH):
                            if tc >= 2:
                                wg(t, soV, ("mye", q, tc - 2))
                            elif q >= 1:
                                wg(t, soV, ("mye", q - 1, 14 + tc))
                            else:
                                wg(t, soV, ("mye", -1, 14 + tc % 2))
                            ins = None
                            for fc in range(8):
                                lhsT = hsl[:, fc * TOWN + tc * 128:fc * TOWN + (tc + 1) * 128]
                                ins = t.matmul(ps_y[tc % 2][:, 0:512], lhsT, w2s[q % 2][:, fc * D:fc * D + 512],
                                               start=(fc == 0), stop=(fc == 7))
                                ins = t.matmul(ps_y[tc % 2][:, 512:1024], lhsT,
                                               w2s[q % 2][:, fc * D + 512:(fc + 1) * D],
                                               start=(fc == 0), stop=(fc == 7))
                            soT.inc(ins)
                            mark(("mym", q, tc), soT)

            @blk.scalar
            def _(s):
                ti = 0
                for e in range(NE):
                    for fq in range(4):
                        q = e * 4 + fq
                        for fc in range(8):
                            if q >= 1 and fc == 0:
                                wg(s, soT, ("mym", q - 1, NCH - 1))
                            for sub in range(2):
                                wg(s, soT, ("mhm", ti, sub))
                                cs(s, "activation", hsl[:, fc * TOWN + sub * 1024:fc * TOWN + sub * 1024 + 1024],
                                   ps_h[sub][:], AF.Silu,
                                   bias=b1s[:, e * 32 + fq * 8 + fc:e * 32 + fq * 8 + fc + 1])
                                mark(("msi", ti, sub), soA)
                            ti += 1

            @blk.vector
            def _(v):
                for tc in range(NCH):
                    wg(v, soT, ("myi", tc))
                    cv(v, "tensor_copy", y_acc[:, tc * D:(tc + 1) * D], ps_y[tc % 2][:])
                    mark(("mye", -1, tc), soV)
                for e in range(NE):
                    for fq in range(4):
                        q = e * 4 + fq
                        for tc in range(NCH):
                            wg(v, soT, ("mym", q, tc))
                            cv(v, "scalar_tensor_tensor",
                               y_acc[:, tc * D:(tc + 1) * D], ps_y[tc % 2][:],
                               coef[:, tc * NE + e:tc * NE + e + 1],
                               y_acc[:, tc * D:(tc + 1) * D], ALU.mult, ALU.add)
                            mark(("mye", q, tc), soV)
                for tc in range(NCH):
                    v.wait_ge(sax[tc % 2].h, 16 * (tc // 2 + 1))
                    if tc >= 2:
                        v.wait_ge(sot[tc % 2].h, 16 * ((tc - 2) // 2 + 1))
                    cv(v, "tensor_add", outt[tc % 2][:], axt2[tc % 2][:], y_acc[:, tc * D:(tc + 1) * D])
                    mark(("mfin", tc), soV)

    ctx.close()
    return nc


def build_nc(dbg=False, upto="full", sub=""):
    m1 = {}
    _build(dbg, upto, {}, m1, sub)
    m2 = {}
    nc = _build(dbg, upto, m1, m2, sub)
    assert m1 == m2, "nondeterministic build marks"
    return nc


_CACHE = {}


def _host_prep(inputs):
    x = np.asarray(inputs["L0_attn_x"], dtype=np.float32)
    wqkv = np.asarray(inputs["L0_attn_w_qkv"], dtype=np.float32)
    idx = np.arange(3 * D).reshape(H, 3, 64)
    perm = np.concatenate([idx[:, 0].ravel(), idx[:, 1].ravel(), idx[:, 2].ravel()])
    wqkv_r = np.ascontiguousarray(wqkv[:, perm])
    w_o = np.ascontiguousarray(np.asarray(inputs["L0_attn_w_o"], dtype=np.float32))
    rms_bc = np.ascontiguousarray(np.broadcast_to(
        np.asarray(inputs["L0_ffn_rms_w"], dtype=np.float32), (128, D)))
    gate_t = np.ascontiguousarray(np.asarray(inputs["L0_ffn_gate_w"], dtype=np.float32).T)
    b1 = np.asarray(inputs["L0_ffn_b1"], dtype=np.float32)
    b1_t = np.ascontiguousarray(b1.reshape(NE, 32, 128).transpose(2, 0, 1).reshape(128, NE * 32))
    b2 = np.ascontiguousarray(np.asarray(inputs["L0_ffn_b2"], dtype=np.float32))
    w1 = np.ascontiguousarray(np.asarray(inputs["L0_ffn_W1"], dtype=np.float32).reshape(NE * D, FF))
    w2 = np.ascontiguousarray(np.asarray(inputs["L0_ffn_W2"], dtype=np.float32).reshape(NE * FF, D))
    ident = np.eye(128, dtype=np.float32)
    tri = np.where(np.triu(np.ones((128, 128), np.float32)) > 0, 0.0, -4096.0).astype(np.float32)
    in_maps = []
    rows_all = []
    for c in range(8):
        b_, cc = c // 4, c % 4
        rows = np.concatenate([np.arange((4 * g + cc) * 128, (4 * g + cc) * 128 + 128)
                               for g in range(NG)])
        rows_all.append((b_, rows))
        mk = np.zeros((4, 128, 128), dtype=np.float32)
        for r in range(4):
            if r < cc:
                mk[r] = 0.0
            elif r == cc:
                mk[r] = tri.T
            else:
                mk[r] = -4096.0
        in_maps.append(dict(
            x_own=np.ascontiguousarray(x[b_].reshape(S, D)[rows]),
            x_str=np.ascontiguousarray(x[b_]),
            wqkv=wqkv_r, w_o_in=w_o, rms_bc=rms_bc, gate_t=gate_t, b1_t=b1_t,
            b2_in=b2, w1_2d=w1, w2_2d=w2, masks=np.ascontiguousarray(mk), ident=ident))
    return in_maps, rows_all


def kernel(**inputs):
    if "nc" not in _CACHE:
        _CACHE["nc"] = build_nc(dbg=False, upto="full")
    nc = _CACHE["nc"]
    in_maps, rows_all = _host_prep(inputs)
    res = run_bass_kernel_spmd(nc, in_maps, core_ids=list(range(8)))
    out = np.empty((B, S, D), dtype=np.float32)
    for c in range(8):
        b_, rows = rows_all[c]
        out[b_].reshape(S, D)[rows] = res.results[c]["out"]
    return out


# version 06
# speedup vs baseline: 1.0897x; 1.0897x over previous
import numpy as np
import ml_dtypes
import concourse.bass as bass
import concourse.mybir as mybir
from concourse.bass_utils import run_bass_kernel_spmd
from contextlib import ExitStack

F32 = mybir.dt.float32
BF16 = mybir.dt.bfloat16
AF = mybir.ActivationFunctionType
ALU = mybir.AluOpType
AX = mybir.AxisListType

B, S, D = 2, 8192, 1024
H = 16
FF, NE = 4096, 8
TOWN, NCH = 2048, 16
NG, GSZ = 16, 4
NPAIR = 8
SC = 1.0 / 32.0
EPS_LN, EPS_RMS = 1e-5, 1e-6


def _pieces(g):
    qlo = 128 * g
    out = []
    p = qlo
    fe = min(2048, (qlo // 512 + 1) * 512)
    if fe > p:
        out.append((p, fe))
        p = fe
    while p < 2048:
        out.append((p, p + 512))
        p += 512
    return out


def _lspans(g):
    qlo = 128 * g
    out = []
    p = qlo
    fe = min(TOWN, (qlo // 256 + 1) * 256)
    if fe > p:
        out.append((p, fe))
        p = fe
    while p < TOWN:
        out.append((p, p + 256))
        p += 256
    return out


class Sem:
    def __init__(self, nc, name):
        self.h = nc.alloc_semaphore(name)
        self.n = 0

    def inc(self, instr, k=1):
        instr.then_inc(self.h, k)
        self.n += k
        return self.n


def _build(dbg, upto, use, rec, sub=""):
    nc = bass.Bass("TRN2", target_bir_lowering=False)
    DT = nc.dram_tensor
    x_own = DT("x_own", [TOWN, D], F32, kind="ExternalInput").ap()
    x_str = DT("x_str", [S, D], F32, kind="ExternalInput").ap()
    wqkv = DT("wqkv", [D, 3 * D], F32, kind="ExternalInput").ap()
    w_o_in = DT("w_o_in", [D, D], F32, kind="ExternalInput").ap()
    rms_bc = DT("rms_bc", [128, D], F32, kind="ExternalInput").ap()
    gate_t = DT("gate_t", [D, NE], F32, kind="ExternalInput").ap()
    b1_t = DT("b1_t", [128, NE * 32], F32, kind="ExternalInput").ap()
    b2_in = DT("b2_in", [NE, D], F32, kind="ExternalInput").ap()
    w1_2d = DT("w1_2d", [NE * D, FF], F32, kind="ExternalInput").ap()
    w2_2d = DT("w2_2d", [NE * FF, D], F32, kind="ExternalInput").ap()
    masks = DT("masks", [4, 128, 128], F32, kind="ExternalInput").ap()
    ident = DT("ident", [128, 128], F32, kind="ExternalInput").ap()
    out_t = DT("out", [TOWN, D], F32, kind="ExternalOutput").ap()
    dkind = "ExternalOutput" if dbg else "Internal"
    xn2t_sp = DT("xn2t_sp", [128, 8 * TOWN], BF16, kind=dkind).ap()
    attnx_sp = DT("attnx_sp", [TOWN, D], F32, kind=dkind).ap()
    l_tab = DT("l_tab", [H, NG, TOWN], F32, kind=dkind).ap()
    l_tab2 = l_tab.rearrange("h g q -> (h g) q")
    if dbg:
        d_qt = DT("d_qt", [128, NPAIR * TOWN], BF16, kind="ExternalOutput").ap()
        d_coef = DT("d_coef", [128, NCH * NE], F32, kind="ExternalOutput").ap()
        d_ot = DT("d_ot", [128, 8 * TOWN], F32, kind="ExternalOutput").ap()

    coeft_sp = DT("coeft_sp", [8, NCH * 128], F32, kind="Internal").ap()
    coef_sp = DT("coef_sp", [128, NCH * NE], F32, kind="Internal").ap()
    ctx = ExitStack()
    E = ctx.enter_context
    maskb = E(nc.sbuf_tensor("maskb", [128, 4 * 128], BF16))
    id_b = E(nc.sbuf_tensor("id_b", [128, 128], BF16))
    eps5 = E(nc.sbuf_tensor("eps5", [128, 1], F32))
    ctxB = ExitStack()
    EB = ctxB.enter_context
    wq_kv = EB(nc.sbuf_tensor("wq_kv", [128, 8 * 2048], BF16, side="right"))
    QT = [EB(nc.sbuf_tensor(f"qt{g}", [128, TOWN], BF16, side="right")) for g in range(NPAIR)]

    soV = Sem(nc, "soV")
    soA = Sem(nc, "soA")
    soT = Sem(nc, "soT")
    sw = Sem(nc, "s_w")
    szf = Sem(nc, "s_zf")
    ssp = Sem(nc, "s_sp")
    sldma = Sem(nc, "s_ldma")
    smc = Sem(nc, "s_mc")
    sout_ = Sem(nc, "s_out")
    sxl = Sem(nc, "s_xl")
    sxsb = [Sem(nc, f"s_xs{i}") for i in range(4)]
    slt = [Sem(nc, f"s_lt{i}") for i in range(2)]
    sxo = [Sem(nc, f"s_xo{i}") for i in range(2)]
    sax = [Sem(nc, f"s_ax{i}") for i in range(2)]
    sw1 = [Sem(nc, f"s_w1{i}") for i in range(2)]
    sw2 = [Sem(nc, f"s_w2{i}") for i in range(2)]
    sat = [Sem(nc, f"s_at{i}") for i in range(2)]
    sot = [Sem(nc, f"s_ot{i}") for i in range(2)]
    slp = [Sem(nc, f"s_lp{i}") for i in range(2)]
    stp = [Sem(nc, f"s_tp{i}") for i in range(2)]

    def cv(v, name, *a, **kw):
        v.wait_ge(soV.h, soV.n)
        ins = getattr(v, name)(*a, **kw)
        soV.inc(ins)
        return ins

    def cs(s, name, *a, **kw):
        s.wait_ge(soA.h, soA.n)
        ins = getattr(s, name)(*a, **kw)
        soA.inc(ins)
        return ins

    def mark(key, sem):
        rec[key] = sem.n

    def wg(eng, sem, key):
        eng.wait_ge(sem.h, use.get(key, 0))

    with ExitStack() as sctx:
        zb = sctx.enter_context(nc.sbuf_tensor("zb", [64, 8192], F32))
        with nc.Block("setup") as blk:
            @blk.gpsimd
            def _(g):
                sw.inc(g.dma_start(wq_kv[:].rearrange("p (c j) -> p c j", c=8),
                                   wqkv[:, D:3 * D].rearrange("(c p) j -> p c j", p=128)), 16)
                sw.inc(g.dma_start(maskb[:].rearrange("p (r q) -> p r q", r=4),
                                   masks.rearrange("r k q -> k r q")), 16)
                sw.inc(g.dma_start(id_b[:], ident), 16)

            @blk.vector
            def _(v):
                cv(v, "memset", zb[:], 0.0)
                cv(v, "memset", eps5[:], EPS_LN)
                mark("zb", soV)

            @blk.sync
            def _(sy):
                wg(sy, soV, "zb")
                lt_flat = l_tab.rearrange("h g q -> (h g q)").rearrange("(a b) -> a b", a=64)
                szf.inc(sy.dma_start(lt_flat[:, :], zb[:]), 16)
                sy.wait_ge(szf.h, 16)

    with ExitStack() as actx:
        Ea = actx.enter_context
        wq_q = Ea(nc.sbuf_tensor("wq_q", [128, 8 * D], BF16))
        rmsb = Ea(nc.sbuf_tensor("rmsb", [128, D], F32))
        coefT = Ea(nc.sbuf_tensor("coefT", [8, NCH * 128], F32))
        id_f = Ea(nc.sbuf_tensor("id_f", [128, 128], F32))
        coef = Ea(nc.sbuf_tensor("coef", [128, NCH * NE], F32))
        eps6 = Ea(nc.sbuf_tensor("eps6", [128, 1], F32))
        ghi = Ea(nc.sbuf_tensor("ghi", [128, 8 * NE], BF16))
        glo = Ea(nc.sbuf_tensor("glo", [128, 8 * NE], BF16))
        gf = Ea(nc.sbuf_tensor("gf", [128, 8 * NE], F32))
        xbuf = [Ea(nc.sbuf_tensor(f"a0x{i}", [128, D], F32)) for i in range(4)]
        xnb = [Ea(nc.sbuf_tensor(f"a0n{i}", [128, D], BF16)) for i in range(2)]
        xn2b = [Ea(nc.sbuf_tensor(f"a0n2{i}", [128, D], BF16)) for i in range(2)]
        lob = [Ea(nc.sbuf_tensor(f"a0lo{i}", [128, D], BF16)) for i in range(2)]
        xnf = Ea(nc.sbuf_tensor("a0xnf", [128, D], F32))
        xrf = Ea(nc.sbuf_tensor("a0xrf", [128, D], F32))
        x2f = Ea(nc.sbuf_tensor("a0x2f", [128, D], F32))
        bst = Ea(nc.sbuf_tensor("a0bst", [128, 48], F32))
        mv = Ea(nc.sbuf_tensor("a0mv", [128, 8], F32))
        xnt = Ea(nc.sbuf_tensor("a0xnt", [128, 8 * 512], BF16))
        x2t = Ea(nc.sbuf_tensor("a0x2t", [128, 8 * 512], BF16))
        lot = [Ea(nc.sbuf_tensor(f"a0lot{i}", [128, 8 * 128], BF16)) for i in range(2)]
        cst = {nm: Ea(nc.sbuf_tensor("a0" + nm, [128, 4], F32))
               for nm in "vv rec ms lnv rstd lnm cc".split()}
        g16 = {nm: Ea(nc.sbuf_tensor("a0g" + nm, [128, 16], F32))
               for nm in "gm ngm gs gr".split()}
        Lsb = Ea(nc.sbuf_tensor("a0L", [128, NCH * NE], F32))
        egs = Ea(nc.sbuf_tensor("a0eg", [128, NE], F32))
        ps_tr = [Ea(nc.psum_tensor(f"a0ptr{i}", [128, 1024], BF16)) for i in range(2)]
        ps_q = [Ea(nc.psum_tensor(f"a0pq{i}", [128, 512], F32)) for i in range(2)]
        ps_g = [Ea(nc.psum_tensor(f"a0pg{i}", [128, NE], F32)) for i in range(2)]
        ps_ct = Ea(nc.psum_tensor("a0pct", [8, 128], F32))

        with nc.Block("a0") as blk:
            @blk.gpsimd
            def _(g):
                sw.inc(g.dma_start(wq_q[:].rearrange("p (c j) -> p c j", c=8),
                                   wqkv[:, 0:D].rearrange("(c p) j -> p c j", p=128)), 16)

            @blk.sync
            def _(sy):
                sxl.inc(sy.dma_start(rmsb[:], rms_bc), 16)
                sxl.inc(sy.dma_start(id_f[:], ident), 16)
                sxl.inc(sy.dma_start(gf[:].rearrange("p (c e) -> p c e", c=8),
                                     gate_t.rearrange("(c p) e -> p c e", p=128)), 16)

                def spills(st):
                    if "nospill" in sub:
                        return
                    wg(sy, soA, ("a0te_ch", st * 4 + 3))
                    for cd in range(8):
                        ssp.inc(sy.dma_start(xn2t_sp[:, cd * TOWN + st * 512:cd * TOWN + (st + 1) * 512],
                                             x2t[:, cd * 512:(cd + 1) * 512]), 16)
                for st in range(4):
                    for i in range(4):
                        ch = st * 4 + i
                        if ch >= 4:
                            wg(sy, soV, ("a0ln", ch - 4))
                        sxsb[ch % 4].inc(sy.dma_start(xbuf[ch % 4][:], x_own[ch * 128:(ch + 1) * 128, :]), 16)
                    if st >= 1:
                        spills(st - 1)
                spills(3)

            @blk.vector
            def _(v):
                c = cst
                cv(v, "memset", eps6[:], EPS_RMS)
                v.wait_ge(sxl.h, 48)
                cv(v, "tensor_copy", ghi[:], gf[:])
                cv(v, "scalar_tensor_tensor", glo[:], ghi[:], -1.0, gf[:], ALU.mult, ALU.add)
                mark("gate_hl", soV)
                for st in range(4):
                    for i in range(4):
                        ch = st * 4 + i
                        v.wait_ge(sxsb[ch % 4].h, 16 * (ch // 4 + 1))
                        cv(v, "bn_stats", bst[:, i * 12:i * 12 + 6], xbuf[ch % 4][:, 0:512])
                        cv(v, "bn_stats", bst[:, i * 12 + 6:i * 12 + 12], xbuf[ch % 4][:, 512:1024])
                    for i in range(4):
                        cv(v, "bn_aggr", mv[:, i * 2:i * 2 + 2], bst[:, i * 12:i * 12 + 12])
                    for i in range(4):
                        cv(v, "tensor_scalar", c["vv"][:, i:i + 1], mv[:, 2 * i + 1:2 * i + 2],
                           EPS_LN, None, ALU.add)
                    for i in range(4):
                        cv(v, "reciprocal", c["rec"][:, i:i + 1], c["vv"][:, i:i + 1])
                    for i in range(4):
                        cv(v, "tensor_mul", c["ms"][:, i:i + 1], mv[:, 2 * i + 1:2 * i + 2],
                           c["rec"][:, i:i + 1])
                    mark(("a0var", st), soV)
                    wg(v, soA, ("a0rstd", st))
                    for i in range(4):
                        ch = st * 4 + i
                        if ch >= 2:
                            wg(v, soT, ("a0tr", ch - 2))
                        cv(v, "tensor_scalar", xnf[:], xbuf[ch % 4][:], mv[:, 2 * i:2 * i + 1],
                           c["rstd"][:, i:i + 1], ALU.subtract, ALU.mult)
                        cv(v, "tensor_copy", xnb[ch % 2][:], xnf[:])
                        cv(v, "tensor_mul", xrf[:], xnf[:], rmsb[:])
                        cv(v, "tensor_scalar", x2f[:], xrf[:], c["cc"][:, i:i + 1], None, ALU.mult)
                        cv(v, "tensor_copy", xn2b[ch % 2][:], x2f[:])
                        cv(v, "scalar_tensor_tensor", lob[ch % 2][:], xn2b[ch % 2][:], -1.0, x2f[:],
                           ALU.mult, ALU.add)
                        mark(("a0ln", ch), soV)

            @blk.tensor
            def _(t):
                if "notr" in sub:
                    return
                t.wait_ge(sw.h, 64)
                wg(t, soV, "gate_hl")
                for st in range(4):
                    for i in range(4):
                        ch = st * 4 + i
                        wg(t, soV, ("a0ln", ch))
                        for sj, src in enumerate((xnb[ch % 2], xn2b[ch % 2], lob[ch % 2])):
                            for j in range(8):
                                ntr = 24 * ch + 8 * sj + j
                                if ntr >= 2:
                                    wg(t, soA, ("a0te", ntr - 2))
                                soT.inc(t.transpose(ps_tr[ntr % 2][:, 0:128],
                                                    src[:, j * 128:(j + 1) * 128], id_b[:]))
                                mark(("a0trj", ntr), soT)
                        mark(("a0tr", ch), soT)
                        if "nogate" in sub:
                            continue
                        wg(t, soA, ("a0te_ch", ch))
                        if ch >= 2:
                            wg(t, soA, ("a0gl", ch - 2))
                        nmm = 0
                        ins = None
                        for cd in range(8):
                            for src2 in range(3):
                                if src2 == 0:
                                    lhsT, rh = x2t[:, cd * 512 + i * 128:cd * 512 + i * 128 + 128], ghi
                                elif src2 == 1:
                                    lhsT, rh = x2t[:, cd * 512 + i * 128:cd * 512 + i * 128 + 128], glo
                                else:
                                    lhsT, rh = lot[ch % 2][:, cd * 128:(cd + 1) * 128], ghi
                                nmm += 1
                                ins = t.matmul(ps_g[ch % 2][:], lhsT, rh[:, cd * NE:(cd + 1) * NE],
                                               start=(nmm == 1), stop=(nmm == 24))
                        soT.inc(ins)
                        mark(("a0gate", ch), soT)
                    for gp in range(8):
                        if "noqt" in sub:
                            break
                        k = st * 8 + gp
                        if k >= 2:
                            wg(t, soA, ("a0qe", k - 2))
                        ins = None
                        for cd in range(8):
                            ins = t.matmul(ps_q[k % 2][:], wq_q[:, cd * D + gp * 128:cd * D + (gp + 1) * 128],
                                           xnt[:, cd * 512:(cd + 1) * 512], start=(cd == 0), stop=(cd == 7))
                        soT.inc(ins)
                        mark(("a0qt", k), soT)
                if "nogate" not in sub:
                    wg(t, soV, ("a0coef", NCH - 1))
                    for ch in range(NCH):
                        if ch >= 1:
                            wg(t, soV, ("a0ctv", ch - 1))
                        soT.inc(t.transpose(ps_ct[:], coef[:, ch * NE:(ch + 1) * NE], id_f[:]))
                        mark(("a0ct", ch), soT)

            @blk.scalar
            def _(s):
                c = cst
                for st in range(4):
                    wg(s, soV, ("a0var", st))
                    for i in range(4):
                        cs(s, "activation", c["lnv"][:, i:i + 1], mv[:, 2 * i + 1:2 * i + 2], AF.Ln, bias=eps5[:])
                    for i in range(4):
                        cs(s, "activation", c["lnm"][:, i:i + 1], c["ms"][:, i:i + 1], AF.Ln, bias=eps6[:])
                    for i in range(4):
                        cs(s, "activation", c["rstd"][:, i:i + 1], c["lnv"][:, i:i + 1], AF.Exp, scale=-0.5)
                    for i in range(4):
                        cs(s, "activation", c["cc"][:, i:i + 1], c["lnm"][:, i:i + 1], AF.Exp, scale=-0.5)
                    mark(("a0rstd", st), soA)
                    if "notr" in sub or "noev" in sub:
                        continue
                    for i in range(4):
                        ch = st * 4 + i
                        if ch >= 2:
                            wg(s, soT, ("a0gate", ch - 2))
                        if i == 0 and st >= 1:
                            wg(s, soT, ("a0qt", st * 8 - 1))
                            s.wait_ge(ssp.h, 128 * st)
                        for sj, dst in enumerate(("xnt", "x2t", "lot")):
                            for j in range(8):
                                ntr = 24 * ch + 8 * sj + j
                                wg(s, soT, ("a0trj", ntr))
                                psl = ps_tr[ntr % 2][:, 0:128]
                                if dst == "xnt":
                                    cs(s, "activation", xnt[:, j * 512 + i * 128:j * 512 + i * 128 + 128],
                                       psl, AF.Copy)
                                elif dst == "x2t":
                                    cs(s, "activation", x2t[:, j * 512 + i * 128:j * 512 + i * 128 + 128],
                                       psl, AF.Copy)
                                else:
                                    cs(s, "activation", lot[ch % 2][:, j * 128:(j + 1) * 128], psl, AF.Copy)
                                mark(("a0te", ntr), soA)
                        mark(("a0te_ch", ch), soA)
                        if "nogate" not in sub:
                            wg(s, soT, ("a0gate", ch))
                            cs(s, "activation", Lsb[:, ch * NE:(ch + 1) * NE], ps_g[ch % 2][:], AF.Copy)
                            mark(("a0gl", ch), soA)
                    for gp in range(8):
                        if "noqt" in sub:
                            break
                        k = st * 8 + gp
                        wg(s, soT, ("a0qt", k))
                        cs(s, "activation", QT[gp][:, st * 512:(st + 1) * 512], ps_q[k % 2][:], AF.Copy)
                        mark(("a0qe", k), soA)
                if "notr" not in sub and "nogate" not in sub:
                    for ch in range(NCH):
                        wg(s, soV, ("a0ngm", ch))
                        cs(s, "activation", egs[:], Lsb[:, ch * NE:(ch + 1) * NE], AF.Exp,
                           bias=g16["ngm"][:, ch:ch + 1], accum_out=g16["gs"][:, ch:ch + 1])
                        mark(("a0ga", ch), soA)

            @blk.vector
            def _(v):
                if "notr" in sub or "nogate" in sub:
                    return
                for ch in range(NCH):
                    wg(v, soA, ("a0gl", ch))
                    cv(v, "reduce_max", g16["gm"][:, ch:ch + 1], Lsb[:, ch * NE:(ch + 1) * NE], axis=AX.X)
                for ch in range(NCH):
                    cv(v, "tensor_scalar", g16["ngm"][:, ch:ch + 1], g16["gm"][:, ch:ch + 1],
                       -1.0, None, ALU.mult)
                    mark(("a0ngm", ch), soV)
                for ch in range(NCH):
                    wg(v, soA, ("a0ga", ch))
                    cv(v, "reciprocal", g16["gr"][:, ch:ch + 1], g16["gs"][:, ch:ch + 1])
                for ch in range(NCH):
                    cv(v, "tensor_scalar", coef[:, ch * NE:(ch + 1) * NE],
                       Lsb[:, ch * NE:(ch + 1) * NE],
                       g16["gm"][:, ch:ch + 1], g16["gr"][:, ch:ch + 1], ALU.is_equal, ALU.mult)
                    mark(("a0coef", ch), soV)
                for ch in range(NCH):
                    wg(v, soT, ("a0ct", ch))
                    cv(v, "tensor_copy", coefT[:, ch * 128:(ch + 1) * 128], ps_ct[:])
                    mark(("a0ctv", ch), soV)

            @blk.sync
            def _(sy):
                if "nogate" not in sub:
                    wg(sy, soV, ("a0ctv", NCH - 1))
                    ssp.inc(sy.dma_start(coeft_sp, coefT[:]), 16)
                    wg(sy, soV, ("a0coef", NCH - 1))
                    ssp.inc(sy.dma_start(coef_sp, coef[:]), 16)
                if dbg and "notr" not in sub:
                    wg(sy, soA, ("a0qe", 31))
                    for gp in range(NPAIR):
                        ssp.inc(sy.dma_start(d_qt[:, gp * TOWN:(gp + 1) * TOWN], QT[gp][:]), 16)
                    wg(sy, soV, ("a0coef", NCH - 1))
                    ssp.inc(sy.dma_start(d_coef, coef[:]), 16)
                sy.wait_ge(ssp.h, ssp.n)

    if upto == "a0":
        with nc.Block("fin0") as blk:
            @blk.sync
            def _(sy):
                sout_.inc(sy.dma_start(out_t, x_own), 16)
                sy.wait_ge(sout_.h, 16)
        ctx.close()
        return nc

    scg = []
    tot = 0
    for g in range(NG):
        tot += H * 4 * len(_pieces(g))
        scg.append(tot)

    with ExitStack() as wctx:
        OTacc = wctx.enter_context(nc.sbuf_tensor("OTacc", [128, 8 * TOWN], F32))
        with ExitStack() as a1c:
            E1 = a1c.enter_context
            xs = [E1(nc.sbuf_tensor(f"a1x{i}", [128, D], F32)) for i in range(4)]
            xnb1 = [E1(nc.sbuf_tensor(f"a1n{i}", [128, D], BF16)) for i in range(2)]
            bst1 = E1(nc.sbuf_tensor("a1bst", [128, 48], F32))
            mv1 = E1(nc.sbuf_tensor("a1mv", [128, 8], F32))
            xnt1 = E1(nc.sbuf_tensor("a1xnt", [128, 8 * 512], BF16))
            ktg = E1(nc.sbuf_tensor("a1kt", [128, 8 * 512], BF16))
            vg = [E1(nc.sbuf_tensor(f"a1v{i}", [128, H * 65], BF16)) for i in range(4)]
            PT = [[E1(nc.sbuf_tensor(f"a1pt{s0}_{k}", [128, TOWN], BF16)) for k in range(4)]
                  for s0 in range(2)]
            tmpl = [E1(nc.sbuf_tensor(f"a1tl{i}", [1, 256], F32)) for i in range(2)]
            c1 = {nm: E1(nc.sbuf_tensor("a1" + nm, [128, 4], F32))
                  for nm in "lnv rstd".split()}
            ps_s = E1(nc.psum_tensor("a1ps", [128, 2048], F32))
            ps_big = E1(nc.psum_tensor("a1pb", [128, 2048], F32))

            nld_tab = {}
            nld = 0
            for g in range(NG):
                for h in range(H):
                    for li in range(len(_lspans(g))):
                        nld_tab[(g, h, li)] = nld
                        nld += 1
            TOT_L = nld

            with nc.Block("a1") as blk:
                @blk.sync
                def _(sy):
                    def transposes(j):
                        if j >= 1:
                            wg(sy, soT, ("a1vm", j - 1, 3))
                        for i in range(4):
                            s_ = 4 * j + i
                            wg(sy, soV, ("a1xnb", s_))
                            for b_ in range(8):
                                stp[s_ % 2].inc(sy.dma_start(
                                    xnt1[:, b_ * 512 + i * 128:b_ * 512 + i * 128 + 128],
                                    xnb1[s_ % 2][:, b_ * 128:(b_ + 1) * 128],
                                    transpose=True), 16)

                    for g in range(NG):
                        for i in range(4):
                            s_ = 4 * g + i
                            if s_ >= 4:
                                wg(sy, soV, ("a1xnb", s_ - 4))
                            sxsb[s_ % 4].inc(sy.dma_start(xs[s_ % 4][:], x_str[s_ * 128:(s_ + 1) * 128, :]), 16)
                        transposes(g)

                @blk.gpsimd
                def _(g_):
                    def lstores(g):
                        for h in range(H):
                            for li, (a, b) in enumerate(_lspans(g)):
                                nl = nld_tab[(g, h, li)]
                                wg(g_, soV, ("a1lev", g, h, li))
                                slp[nl % 2].inc(g_.dma_start(
                                    l_tab2[h * NG + g:h * NG + g + 1, a:b],
                                    tmpl[nl % 2][0:1, 0:b - a]), 16)

                    for g in range(NG):
                        lstores(g)
                    g_.wait_ge(slp[0].h, slp[0].n)
                    g_.wait_ge(slp[1].h, slp[1].n)

                @blk.vector
                def _(v):
                    for i in range(4):
                        cv(v, "memset", vg[i][:].rearrange("p (h j) -> p h j", j=65)[:, :, 64:65], 1.0)

                    def stats(g):
                        for i in range(4):
                            s_ = 4 * g + i
                            v.wait_ge(sxsb[s_ % 4].h, 16 * (4 + s_ // 4 + 1))
                            cv(v, "bn_stats", bst1[:, i * 12:i * 12 + 6], xs[s_ % 4][:, 0:512])
                            cv(v, "bn_stats", bst1[:, i * 12 + 6:i * 12 + 12], xs[s_ % 4][:, 512:1024])
                        for i in range(4):
                            cv(v, "bn_aggr", mv1[:, i * 2:i * 2 + 2], bst1[:, i * 12:i * 12 + 12])
                        mark(("a1var", g), soV)

                    def xnb_one(g, i):
                        if i == 0:
                            wg(v, soA, ("a1rstd", g))
                        s_ = 4 * g + i
                        if s_ >= 2:
                            v.wait_ge(stp[s_ % 2].h, 16 * 8 * (s_ // 2))
                        cv(v, "tensor_scalar", xnb1[s_ % 2][:], xs[s_ % 4][:], mv1[:, 2 * i:2 * i + 1],
                           c1["rstd"][:, i:i + 1], ALU.subtract, ALU.mult)
                        mark(("a1xnb", s_), soV)

                    def kt_ev(j):
                        for p8 in range(8):
                            rb = (p8 % 2) * 512
                            wg(v, soT, ("a1ktm", j, p8))
                            if j >= 1 and p8 == 0:
                                wg(v, soT, ("a1smm", scg[j - 1] - 1))
                            cv(v, "tensor_copy", ktg[:, p8 * 512:(p8 + 1) * 512],
                               ps_big[:, rb:rb + 512])
                            mark(("a1kt", j, p8), soV)

                    stats(0)
                    for i in range(4):
                        xnb_one(0, i)
                    kt_ev(0)
                    for g in range(NG):
                        for i in range(4):
                            wg(v, soT, ("a1vm", g, i))
                            if g >= 1:
                                wg(v, soT, ("a1pv", g - 1, 15))
                            rb = 1024 * (i % 2)
                            cv(v, "tensor_copy",
                               vg[i][:].rearrange("p (h j) -> p h j", j=65)[:, :, 0:64],
                               ps_big[:, rb:rb + 1024].rearrange("p (h j) -> p h j", j=64))
                            mark(("a1ve", g, i), soV)
                        if g + 1 < NG:
                            stats(g + 1)
                            for i in range(4):
                                xnb_one(g + 1, i)
                        qlo = 128 * g
                        for h in range(H):
                            wg(v, soT, ("a1pv", g, h))
                            hc, r0 = h // 2, 64 * (h % 2)
                            dstap = OTacc[r0:r0 + 64, hc * TOWN + qlo:(hc + 1) * TOWN]
                            if g == 0:
                                cv(v, "tensor_copy", dstap, ps_big[0:64, qlo:TOWN])
                            else:
                                cv(v, "tensor_add", dstap, dstap, ps_big[0:64, qlo:TOWN])
                            for li, (a, b) in enumerate(_lspans(g)):
                                nl = nld_tab[(g, h, li)]
                                if nl >= 2:
                                    v.wait_ge(slp[nl % 2].h, 16 * (nl // 2))
                                cv(v, "tensor_copy", tmpl[nl % 2][0:1, 0:b - a], ps_big[64:65, a:b])
                                mark(("a1lev", g, h, li), soV)
                            mark(("a1ote", g, h), soV)
                        if g + 1 < NG:
                            kt_ev(g + 1)

                @blk.tensor
                def _(t):
                    t.wait_ge(sw.h, 64)

                    def kv(j):
                        t.wait_ge(stp[0].h, 16 * 16 * (j + 1))
                        t.wait_ge(stp[1].h, 16 * 16 * (j + 1))
                        if j >= 1:
                            wg(t, soV, ("a1ote", j - 1, 15))
                        for p8 in range(8):
                            rb = (p8 % 2) * 512
                            if p8 >= 2:
                                wg(t, soV, ("a1kt", j, p8 - 2))
                            ins = None
                            for cd in range(8):
                                ins = t.matmul(ps_big[:, rb:rb + 512],
                                               wq_kv[:, cd * 2048 + p8 * 128:cd * 2048 + (p8 + 1) * 128],
                                               xnt1[:, cd * 512:(cd + 1) * 512],
                                               start=(cd == 0), stop=(cd == 7))
                            soT.inc(ins)
                            mark(("a1ktm", j, p8), soT)
                        wg(t, soV, ("a1kt", j, 7))
                        for i in range(4):
                            rb = 1024 * (i % 2)
                            if i >= 2:
                                wg(t, soV, ("a1ve", j, i - 2))
                            ins = None
                            for cd in range(8):
                                lhsT = xnt1[:, cd * 512 + i * 128:cd * 512 + i * 128 + 128]
                                ins = t.matmul(ps_big[:, rb:rb + 512], lhsT,
                                               wq_kv[:, cd * 2048 + 1024:cd * 2048 + 1536],
                                               start=(cd == 0), stop=(cd == 7))
                                ins = t.matmul(ps_big[:, rb + 512:rb + 1024], lhsT,
                                               wq_kv[:, cd * 2048 + 1536:cd * 2048 + 2048],
                                               start=(cd == 0), stop=(cd == 7))
                            soT.inc(ins)
                            mark(("a1vm", j, i), soT)

                    nsc = [0]

                    def scores(g, h):
                        gp, r0 = h // 2, 64 * (h % 2)
                        for k in range(4):
                            for pi, (a, b) in enumerate(_pieces(g)):
                                jj = nsc[0]
                                nsc[0] += 1
                                if jj >= 4:
                                    wg(t, soA, ("a1expj", jj - 4))
                                ins = t.matmul(ps_s[:, (jj % 4) * 512:(jj % 4) * 512 + (b - a)],
                                               ktg[r0:r0 + 64, gp * 512 + k * 128:gp * 512 + (k + 1) * 128],
                                               QT[gp][r0:r0 + 64, a:b], start=True, stop=(pi != 0))
                                if pi == 0:
                                    ins = t.matmul(ps_s[:, (jj % 4) * 512:(jj % 4) * 512 + 128],
                                                   maskb[:, k * 128:(k + 1) * 128], id_b[:],
                                                   start=False, stop=True)
                                soT.inc(ins)
                                mark(("a1smm", jj), soT)

                    def pv(g, h):
                        wg(t, soA, ("a1exp", g, h))
                        if h == 0:
                            wg(t, soV, ("a1ve", g, 3))
                        if h >= 1:
                            wg(t, soV, ("a1ote", g, h - 1))
                        ins = None
                        for (a, b) in _pieces(g):
                            for k in range(4):
                                ins = t.matmul(ps_big[0:65, a:b], vg[k][:, h * 65:(h + 1) * 65],
                                               PT[h % 2][k][:, a:b], start=(k == 0), stop=(k == 3))
                        soT.inc(ins)
                        mark(("a1pv", g, h), soT)

                    kv(0)
                    for g in range(NG):
                        scores(g, 0)
                        for h in range(H):
                            if h + 1 < H:
                                scores(g, h + 1)
                            pv(g, h)
                        if g + 1 < NG:
                            kv(g + 1)

                @blk.scalar
                def _(s):
                    def rstd_chain(g):
                        wg(s, soV, ("a1var", g))
                        for i in range(4):
                            cs(s, "activation", c1["lnv"][:, i:i + 1], mv1[:, 2 * i + 1:2 * i + 2], AF.Ln, bias=eps5[:])
                        for i in range(4):
                            cs(s, "activation", c1["rstd"][:, i:i + 1], c1["lnv"][:, i:i + 1], AF.Exp, scale=-0.5)
                        mark(("a1rstd", g), soA)

                    rstd_chain(0)
                    nsc = 0
                    for g in range(NG):
                        if g + 1 < NG:
                            rstd_chain(g + 1)
                        for h in range(H):
                            if h >= 2:
                                wg(s, soT, ("a1pv", g, h - 2))
                            elif g >= 1:
                                wg(s, soT, ("a1pv", g - 1, 14 + h))
                            for k in range(4):
                                for (a, b) in _pieces(g):
                                    wg(s, soT, ("a1smm", nsc))
                                    cs(s, "activation", PT[h % 2][k][:, a:b],
                                       ps_s[:, (nsc % 4) * 512:(nsc % 4) * 512 + (b - a)],
                                       AF.Exp, scale=SC)
                                    mark(("a1expj", nsc), soA)
                                    nsc += 1
                            mark(("a1exp", g, h), soA)

        ctxB.close()
        with ExitStack() as woc:
            Ew = woc.enter_context
            OTb = Ew(nc.sbuf_tensor("OTb", [128, 8 * TOWN], BF16))
            on16 = Ew(nc.sbuf_tensor("on16", [16, 128], F32))
            wos = Ew(nc.sbuf_tensor("wos", [128, 8 * D], BF16))
            lts = [Ew(nc.sbuf_tensor(f"lts{i}", [16, TOWN], F32)) for i in range(2)]
            rlb = Ew(nc.sbuf_tensor("rlb", [128, TOWN], F32))
            xo = [Ew(nc.sbuf_tensor(f"xo{i}", [128, D], F32)) for i in range(2)]
            axt = [Ew(nc.sbuf_tensor(f"axt{i}", [128, D], F32)) for i in range(2)]
            ps_lb = Ew(nc.psum_tensor("ps_lb", [128, TOWN], F32))
            ps_wo = [Ew(nc.psum_tensor(f"ps_wo{i}", [128, D], F32)) for i in range(2)]

            A0SP = ssp.n

            with nc.Block("wo") as blk:
                @blk.gpsimd
                def _(g_):
                    sw.inc(g_.dma_start(wos[:].rearrange("p (c j) -> p c j", c=8),
                                        w_o_in.rearrange("(c p) j -> p c j", p=128)), 16)

                @blk.sync
                def _(sy):
                    for h in range(H):
                        if h >= 2:
                            wg(sy, soT, ("wolb", h - 2))
                        slt[h % 2].inc(sy.dma_start(lts[h % 2][:], l_tab2[h * NG:(h + 1) * NG, :]), 16)
                    for tc in range(NCH):
                        if tc >= 2:
                            wg(sy, soV, ("woax", tc - 2))
                        sxo[tc % 2].inc(sy.dma_start(xo[tc % 2][:], x_own[tc * 128:(tc + 1) * 128, :]), 16)
                        if tc >= 2:
                            wg(sy, soV, ("woax", tc - 2))
                            sat[tc % 2].inc(sy.dma_start(attnx_sp[(tc - 2) * 128:(tc - 1) * 128, :],
                                                         axt[tc % 2][:]), 16)
                    for tc in range(NCH - 2, NCH):
                        wg(sy, soV, ("woax", tc))
                        sat[tc % 2].inc(sy.dma_start(attnx_sp[tc * 128:(tc + 1) * 128, :], axt[tc % 2][:]), 16)
                    if dbg:
                        wg(sy, soV, ("wootb", H - 1))
                        ssp.inc(sy.dma_start(d_ot, OTacc[:]), 16)
                    sy.wait_ge(ssp.h, ssp.n)
                    sy.wait_ge(sat[0].h, sat[0].n)
                    sy.wait_ge(sat[1].h, sat[1].n)

                @blk.tensor
                def _(t):
                    t.wait_ge(sw.h, 80)
                    wg(t, soV, "on16")
                    for h in range(H):
                        t.wait_ge(slt[h % 2].h, 16 * (h // 2 + 1))
                        if h >= 1:
                            wg(t, soV, ("worl", h - 1))
                        ins = None
                        for pc in range(4):
                            ins = t.matmul(ps_lb[:, pc * 512:(pc + 1) * 512], on16[:],
                                           lts[h % 2][:, pc * 512:(pc + 1) * 512], start=True, stop=True)
                        soT.inc(ins)
                        mark(("wolb", h), soT)
                    for tc in range(NCH):
                        wg(t, soV, ("wootb", H - 1))
                        if tc >= 2:
                            wg(t, soV, ("woax", tc - 2))
                        ins = None
                        for cd in range(8):
                            lhsT = OTb[:, cd * TOWN + tc * 128:cd * TOWN + (tc + 1) * 128]
                            ins = t.matmul(ps_wo[tc % 2][:, 0:512], lhsT, wos[:, cd * D:cd * D + 512],
                                           start=(cd == 0), stop=(cd == 7))
                            ins = t.matmul(ps_wo[tc % 2][:, 512:1024], lhsT, wos[:, cd * D + 512:(cd + 1) * D],
                                           start=(cd == 0), stop=(cd == 7))
                        soT.inc(ins)
                        mark(("womm", tc), soT)

                @blk.vector
                def _(v):
                    cv(v, "memset", on16[:], 1.0)
                    mark("on16", soV)
                    for h in range(H):
                        wg(v, soT, ("wolb", h))
                        cv(v, "reciprocal", rlb[:], ps_lb[:])
                        mark(("worl", h), soV)
                        hc, r0 = h // 2, 64 * (h % 2)
                        cv(v, "tensor_mul", OTb[r0:r0 + 64, hc * TOWN:(hc + 1) * TOWN],
                           OTacc[r0:r0 + 64, hc * TOWN:(hc + 1) * TOWN], rlb[r0:r0 + 64, :])
                        mark(("wootb", h), soV)
                    for tc in range(NCH):
                        wg(v, soT, ("womm", tc))
                        v.wait_ge(sxo[tc % 2].h, 16 * (tc // 2 + 1))
                        if tc >= 2:
                            v.wait_ge(sat[tc % 2].h, 16 * ((tc - 2) // 2 + 1))
                        cv(v, "tensor_add", axt[tc % 2][:], xo[tc % 2][:], ps_wo[tc % 2][:])
                        mark(("woax", tc), soV)

    A1SP = ssp.n
    with ExitStack() as mctx:
        Em = mctx.enter_context
        y_acc = Em(nc.sbuf_tensor("y_acc", [128, NCH * D], F32))
        X2 = Em(nc.sbuf_tensor("mX2", [128, 8 * TOWN], BF16))
        hsl = Em(nc.sbuf_tensor("mh", [128, 8 * TOWN], BF16))
        w2s = [Em(nc.sbuf_tensor(f"mw2{i}", [128, 8 * D], BF16)) for i in range(2)]
        w1t = [Em(nc.sbuf_tensor(f"mw1{i}", [128, 8 * 128], BF16)) for i in range(2)]
        b1s = Em(nc.sbuf_tensor("mb1", [128, NE * 32], F32))
        b2s = Em(nc.sbuf_tensor("mb2", [8, D], F32))
        coefT = Em(nc.sbuf_tensor("mcoefT", [8, NCH * 128], F32))
        coef = Em(nc.sbuf_tensor("mcoef", [128, NCH * NE], F32))
        axt2 = [Em(nc.sbuf_tensor(f"max{i}", [128, D], F32)) for i in range(2)]
        outt = [Em(nc.sbuf_tensor(f"mo{i}", [128, D], F32)) for i in range(2)]
        ps_h = [Em(nc.psum_tensor(f"mph{i}", [128, 1024], F32)) for i in range(2)]
        ps_y = [Em(nc.psum_tensor(f"mpy{i}", [128, 1024], F32)) for i in range(2)]

        with nc.Block("moe") as blk:
            @blk.sync
            def _(sy):
                sy.wait_ge(ssp.h, A1SP)
                sy.wait_ge(sat[0].h, sat[0].n)
                sy.wait_ge(sat[1].h, sat[1].n)
                smc.inc(sy.dma_start(X2[:], xn2t_sp), 16)
                smc.inc(sy.dma_start(b2s[:], b2_in), 16)
                smc.inc(sy.dma_start(b1s[:], b1_t), 16)
                smc.inc(sy.dma_start(coefT[:], coeft_sp), 16)
                smc.inc(sy.dma_start(coef[:], coef_sp), 16)
                for tc in range(NCH):
                    if tc >= 2:
                        wg(sy, soV, ("mfin", tc - 2))
                    sax[tc % 2].inc(sy.dma_start(axt2[tc % 2][:], attnx_sp[tc * 128:(tc + 1) * 128, :]), 16)
                    if tc >= 2:
                        wg(sy, soV, ("mfin", tc - 2))
                        sot[tc % 2].inc(sy.dma_start(out_t[(tc - 2) * 128:(tc - 1) * 128, :], outt[tc % 2][:]), 16)
                for tc in range(NCH - 2, NCH):
                    wg(sy, soV, ("mfin", tc))
                    sot[tc % 2].inc(sy.dma_start(out_t[tc * 128:(tc + 1) * 128, :], outt[tc % 2][:]), 16)
                sy.wait_ge(sot[0].h, 16 * (NCH // 2))
                sy.wait_ge(sot[1].h, 16 * (NCH // 2))

            @blk.gpsimd
            def _(g_):
                ti = 0
                for e in range(NE):
                    for fq in range(4):
                        q = e * 4 + fq
                        if q >= 2:
                            wg(g_, soT, ("mym", q - 2, NCH - 1))
                        r0 = e * FF + fq * 1024
                        sw2[q % 2].inc(g_.dma_start(w2s[q % 2][:].rearrange("p (c j) -> p c j", c=8),
                                                    w2_2d[r0:r0 + 1024, :].rearrange("(c p) j -> p c j", p=128)), 16)
                        for fc in range(8):
                            if ti >= 2:
                                wg(g_, soT, ("mhm", ti - 2, 1))
                            cl = fq * 1024 + fc * 128
                            sw1[ti % 2].inc(g_.dma_start(
                                w1t[ti % 2][:].rearrange("p (c f) -> p c f", c=8),
                                w1_2d[e * D:(e + 1) * D, cl:cl + 128].rearrange("(c p) f -> p c f", p=128)), 16)
                            ti += 1

            @blk.tensor
            def _(t):
                t.wait_ge(smc.h, 80)
                for tc in range(NCH):
                    if tc >= 2:
                        wg(t, soV, ("mye", -1, tc - 2))
                    ins = t.matmul(ps_y[tc % 2][:, 0:512], coefT[:, tc * 128:(tc + 1) * 128],
                                   b2s[:, 0:512], start=True, stop=True)
                    ins = t.matmul(ps_y[tc % 2][:, 512:1024], coefT[:, tc * 128:(tc + 1) * 128],
                                   b2s[:, 512:1024], start=True, stop=True)
                    soT.inc(ins)
                    mark(("myi", tc), soT)
                ti = 0
                for e in range(NE):
                    for fq in range(4):
                        q = e * 4 + fq
                        for fc in range(8):
                            t.wait_ge(sw1[ti % 2].h, 16 * (ti // 2 + 1))
                            for sub in range(2):
                                if ti > 0:
                                    wg(t, soA, ("msi", ti - 1, sub))
                                ins = None
                                for cd in range(8):
                                    co = cd * TOWN + sub * 1024
                                    ins = t.matmul(ps_h[sub][:, 0:512], w1t[ti % 2][:, cd * 128:(cd + 1) * 128],
                                                   X2[:, co:co + 512], start=(cd == 0), stop=(cd == 7))
                                    ins = t.matmul(ps_h[sub][:, 512:1024], w1t[ti % 2][:, cd * 128:(cd + 1) * 128],
                                                   X2[:, co + 512:co + 1024], start=(cd == 0), stop=(cd == 7))
                                soT.inc(ins)
                                mark(("mhm", ti, sub), soT)
                            ti += 1
                        wg(t, soA, ("msi", q * 8 + 7, 1))
                        t.wait_ge(sw2[q % 2].h, 16 * (q // 2 + 1))
                        for tc in range(NCH):
                            if tc >= 2:
                                wg(t, soV, ("mye", q, tc - 2))
                            elif q >= 1:
                                wg(t, soV, ("mye", q - 1, 14 + tc))
                            else:
                                wg(t, soV, ("mye", -1, 14 + tc % 2))
                            ins = None
                            for fc in range(8):
                                lhsT = hsl[:, fc * TOWN + tc * 128:fc * TOWN + (tc + 1) * 128]
                                ins = t.matmul(ps_y[tc % 2][:, 0:512], lhsT, w2s[q % 2][:, fc * D:fc * D + 512],
                                               start=(fc == 0), stop=(fc == 7))
                                ins = t.matmul(ps_y[tc % 2][:, 512:1024], lhsT,
                                               w2s[q % 2][:, fc * D + 512:(fc + 1) * D],
                                               start=(fc == 0), stop=(fc == 7))
                            soT.inc(ins)
                            mark(("mym", q, tc), soT)

            @blk.scalar
            def _(s):
                ti = 0
                for e in range(NE):
                    for fq in range(4):
                        q = e * 4 + fq
                        for fc in range(8):
                            if q >= 1 and fc == 0:
                                wg(s, soT, ("mym", q - 1, NCH - 1))
                            for sub in range(2):
                                wg(s, soT, ("mhm", ti, sub))
                                cs(s, "activation", hsl[:, fc * TOWN + sub * 1024:fc * TOWN + sub * 1024 + 1024],
                                   ps_h[sub][:], AF.Silu,
                                   bias=b1s[:, e * 32 + fq * 8 + fc:e * 32 + fq * 8 + fc + 1])
                                mark(("msi", ti, sub), soA)
                            ti += 1

            @blk.vector
            def _(v):
                for tc in range(NCH):
                    wg(v, soT, ("myi", tc))
                    cv(v, "tensor_copy", y_acc[:, tc * D:(tc + 1) * D], ps_y[tc % 2][:])
                    mark(("mye", -1, tc), soV)
                for e in range(NE):
                    for fq in range(4):
                        q = e * 4 + fq
                        for tc in range(NCH):
                            wg(v, soT, ("mym", q, tc))
                            cv(v, "scalar_tensor_tensor",
                               y_acc[:, tc * D:(tc + 1) * D], ps_y[tc % 2][:],
                               coef[:, tc * NE + e:tc * NE + e + 1],
                               y_acc[:, tc * D:(tc + 1) * D], ALU.mult, ALU.add)
                            mark(("mye", q, tc), soV)
                for tc in range(NCH):
                    v.wait_ge(sax[tc % 2].h, 16 * (tc // 2 + 1))
                    if tc >= 2:
                        v.wait_ge(sot[tc % 2].h, 16 * ((tc - 2) // 2 + 1))
                    cv(v, "tensor_add", outt[tc % 2][:], axt2[tc % 2][:], y_acc[:, tc * D:(tc + 1) * D])
                    mark(("mfin", tc), soV)

    ctx.close()
    return nc


def build_nc(dbg=False, upto="full", sub=""):
    m1 = {}
    _build(dbg, upto, {}, m1, sub)
    m2 = {}
    nc = _build(dbg, upto, m1, m2, sub)
    assert m1 == m2, "nondeterministic build marks"
    return nc


_CACHE = {}


def _host_prep(inputs):
    x = np.asarray(inputs["L0_attn_x"], dtype=np.float32)
    wqkv = np.asarray(inputs["L0_attn_w_qkv"], dtype=np.float32)
    idx = np.arange(3 * D).reshape(H, 3, 64)
    perm = np.concatenate([idx[:, 0].ravel(), idx[:, 1].ravel(), idx[:, 2].ravel()])
    wqkv_r = np.ascontiguousarray(wqkv[:, perm])
    w_o = np.ascontiguousarray(np.asarray(inputs["L0_attn_w_o"], dtype=np.float32))
    rms_bc = np.ascontiguousarray(np.broadcast_to(
        np.asarray(inputs["L0_ffn_rms_w"], dtype=np.float32), (128, D)))
    gate_t = np.ascontiguousarray(np.asarray(inputs["L0_ffn_gate_w"], dtype=np.float32).T)
    b1 = np.asarray(inputs["L0_ffn_b1"], dtype=np.float32)
    b1_t = np.ascontiguousarray(b1.reshape(NE, 32, 128).transpose(2, 0, 1).reshape(128, NE * 32))
    b2 = np.ascontiguousarray(np.asarray(inputs["L0_ffn_b2"], dtype=np.float32))
    w1 = np.ascontiguousarray(np.asarray(inputs["L0_ffn_W1"], dtype=np.float32).reshape(NE * D, FF))
    w2 = np.ascontiguousarray(np.asarray(inputs["L0_ffn_W2"], dtype=np.float32).reshape(NE * FF, D))
    ident = np.eye(128, dtype=np.float32)
    tri = np.where(np.triu(np.ones((128, 128), np.float32)) > 0, 0.0, -4096.0).astype(np.float32)
    in_maps = []
    rows_all = []
    for c in range(8):
        b_, cc = c // 4, c % 4
        rows = np.concatenate([np.arange((4 * g + cc) * 128, (4 * g + cc) * 128 + 128)
                               for g in range(NG)])
        rows_all.append((b_, rows))
        mk = np.zeros((4, 128, 128), dtype=np.float32)
        for r in range(4):
            if r < cc:
                mk[r] = 0.0
            elif r == cc:
                mk[r] = tri.T
            else:
                mk[r] = -4096.0
        in_maps.append(dict(
            x_own=np.ascontiguousarray(x[b_].reshape(S, D)[rows]),
            x_str=np.ascontiguousarray(x[b_]),
            wqkv=wqkv_r, w_o_in=w_o, rms_bc=rms_bc, gate_t=gate_t, b1_t=b1_t,
            b2_in=b2, w1_2d=w1, w2_2d=w2, masks=np.ascontiguousarray(mk), ident=ident))
    return in_maps, rows_all


def kernel(**inputs):
    if "nc" not in _CACHE:
        _CACHE["nc"] = build_nc(dbg=False, upto="full")
    nc = _CACHE["nc"]
    in_maps, rows_all = _host_prep(inputs)
    res = run_bass_kernel_spmd(nc, in_maps, core_ids=list(range(8)))
    out = np.empty((B, S, D), dtype=np.float32)
    for c in range(8):
        b_, rows = rows_all[c]
        out[b_].reshape(S, D)[rows] = res.results[c]["out"]
    return out


# version 08
# speedup vs baseline: 1.4348x; 1.3167x over previous
import numpy as np
import ml_dtypes
import concourse.bass as bass
import concourse.mybir as mybir
from concourse.bass import IndirectOffsetOnAxis
from concourse.bass_utils import run_bass_kernel_spmd
from contextlib import ExitStack

F32 = mybir.dt.float32
BF16 = mybir.dt.bfloat16
I32 = mybir.dt.int32
AF = mybir.ActivationFunctionType
ALU = mybir.AluOpType
AX = mybir.AxisListType

B, S, D = 2, 8192, 1024
H = 16
FF, NE = 4096, 8
TOWN, NCH = 2048, 16
NG, GSZ = 16, 4
NPAIR = 8
SC = 1.0 / 32.0
EPS_LN, EPS_RMS = 1e-5, 1e-6


def _pieces(g):
    qlo = 128 * g
    out = []
    p = qlo
    fe = min(2048, (qlo // 512 + 1) * 512)
    if fe > p:
        out.append((p, fe))
        p = fe
    while p < 2048:
        out.append((p, p + 512))
        p += 512
    return out


def _lspans(g):
    qlo = 128 * g
    out = []
    p = qlo
    fe = min(TOWN, (qlo // 256 + 1) * 256)
    if fe > p:
        out.append((p, fe))
        p = fe
    while p < TOWN:
        out.append((p, p + 256))
        p += 256
    return out


class Sem:
    def __init__(self, nc, name):
        self.h = nc.alloc_semaphore(name)
        self.n = 0

    def inc(self, instr, k=1):
        instr.then_inc(self.h, k)
        self.n += k
        return self.n


def _build(dbg, upto, use, rec, sub=""):
    nc = bass.Bass("TRN2", target_bir_lowering=False)
    DT = nc.dram_tensor
    x_own = DT("x_own", [TOWN, D], F32, kind="ExternalInput").ap()
    x_str = DT("x_str", [S, D], F32, kind="ExternalInput").ap()
    wqkv = DT("wqkv", [D, 3 * D], F32, kind="ExternalInput").ap()
    w_o_in = DT("w_o_in", [D, D], F32, kind="ExternalInput").ap()
    rms_bc = DT("rms_bc", [128, D], F32, kind="ExternalInput").ap()
    gate_t = DT("gate_t", [D, NE], F32, kind="ExternalInput").ap()
    b2_in = DT("b2_in", [NE, D], F32, kind="ExternalInput").ap()
    w1q = DT("w1q", [NE * 4 * 128, 8 * 1024], F32, kind="ExternalInput").ap()
    w2q = DT("w2q", [NE * 4 * 128, 8 * 1024], F32, kind="ExternalInput").ap()
    b1r = DT("b1r", [NE * 128, 32], F32, kind="ExternalInput").ap()
    tritab_in = DT("tritab_in", [128, 1152], BF16, kind="ExternalInput").ap()
    tri8_in = DT("tri8_in", [8, 8], BF16, kind="ExternalInput").ap()
    rt8_in = DT("rt8_in", [8, 24], F32, kind="ExternalInput").ap()
    iq_in = DT("iq_in", [128, 4], F32, kind="ExternalInput").ap()
    onesk1_in = DT("onesk1_in", [1, 128], F32, kind="ExternalInput").ap()
    masks = DT("masks", [4, 128, 128], F32, kind="ExternalInput").ap()
    ident = DT("ident", [128, 128], F32, kind="ExternalInput").ap()
    out_t = DT("out", [TOWN, D], F32, kind="ExternalOutput").ap()
    dkind = "ExternalOutput" if dbg else "Internal"
    NSLOT = 15 * 256
    xs2_sp = DT("xs2_sp", [TOWN, D], BF16, kind=dkind).ap()
    xsort_sp = DT("xsort_sp", [NSLOT, D], BF16, kind="Internal").ap()
    ysort_sp = DT("ysort_sp", [NSLOT, D], F32, kind="Internal").ap()
    attnx_sp = DT("attnx_sp", [TOWN, D], F32, kind=dkind).ap()
    l_tab = DT("l_tab", [H, NG, TOWN], F32, kind=dkind).ap()
    l_tab2 = l_tab.rearrange("h g q -> (h g) q")
    if dbg:
        d_qt = DT("d_qt", [128, NPAIR * TOWN], BF16, kind="ExternalOutput").ap()
        d_coef = DT("d_coef", [128, NCH * NE], F32, kind="ExternalOutput").ap()
        d_ot = DT("d_ot", [128, 8 * TOWN], F32, kind="ExternalOutput").ap()
        d_rt = DT("d_rt", [128, 64], F32, kind="ExternalOutput").ap()

    coeft_sp = DT("coeft_sp", [8, NCH * 128], F32, kind="Internal").ap()
    ohb_sp = DT("ohb_sp", [128, NCH * NE], BF16, kind="Internal").ap()
    coef_sp = DT("coef_sp", [128, NCH * NE], F32, kind="Internal").ap()
    ctx = ExitStack()
    E = ctx.enter_context
    maskb = E(nc.sbuf_tensor("maskb", [128, 4 * 128], BF16))
    id_b = E(nc.sbuf_tensor("id_b", [128, 128], BF16))
    eps5 = E(nc.sbuf_tensor("eps5", [128, 1], F32))
    ctxB = ExitStack()
    EB = ctxB.enter_context
    wq_kv = EB(nc.sbuf_tensor("wq_kv", [128, 8 * 2048], BF16, side="right"))
    QT = [EB(nc.sbuf_tensor(f"qt{g}", [128, TOWN], BF16, side="right")) for g in range(NPAIR)]

    soV = Sem(nc, "soV")
    soA = Sem(nc, "soA")
    soT = Sem(nc, "soT")
    sw = Sem(nc, "s_w")
    szf = Sem(nc, "s_zf")
    ssp = Sem(nc, "s_sp")
    sldma = Sem(nc, "s_ldma")
    smc = Sem(nc, "s_mc")
    sout_ = Sem(nc, "s_out")
    sxl = Sem(nc, "s_xl")
    sxsb = [Sem(nc, f"s_xs{i}") for i in range(4)]
    slt = [Sem(nc, f"s_lt{i}") for i in range(2)]
    sxo = [Sem(nc, f"s_xo{i}") for i in range(2)]
    sax = [Sem(nc, f"s_ax{i}") for i in range(2)]
    sw1 = [Sem(nc, f"s_w1{i}") for i in range(2)]
    sw2 = [Sem(nc, f"s_w2{i}") for i in range(2)]
    sat = [Sem(nc, f"s_at{i}") for i in range(2)]
    sot = [Sem(nc, f"s_ot{i}") for i in range(2)]
    slp = [Sem(nc, f"s_lp{i}") for i in range(2)]
    stp = [Sem(nc, f"s_tp{i}") for i in range(2)]
    sxn = [Sem(nc, f"s_xn{i}") for i in range(2)]
    smx = [Sem(nc, f"s_mx{i}") for i in range(2)]
    spf = Sem(nc, "s_pf")
    ssc = [Sem(nc, f"s_sc{i}") for i in range(2)]
    sxg = [Sem(nc, f"s_xg{i}") for i in range(2)]
    sgw1 = [Sem(nc, f"s_g1{i}") for i in range(2)]
    sgw2 = [Sem(nc, f"s_g2{i}") for i in range(2)]
    sgb1 = [Sem(nc, f"s_gb{i}") for i in range(2)]
    syw = [Sem(nc, f"s_yw{i}") for i in range(2)]
    syg = [Sem(nc, f"s_yg{i}") for i in range(2)]

    def cv(v, name, *a, **kw):
        v.wait_ge(soV.h, soV.n)
        ins = getattr(v, name)(*a, **kw)
        soV.inc(ins)
        return ins

    def cs(s, name, *a, **kw):
        s.wait_ge(soA.h, soA.n)
        ins = getattr(s, name)(*a, **kw)
        soA.inc(ins)
        return ins

    def mark(key, sem):
        rec[key] = sem.n

    def wg(eng, sem, key):
        eng.wait_ge(sem.h, use.get(key, 0))

    with ExitStack() as sctx:
        zb = sctx.enter_context(nc.sbuf_tensor("zb", [64, 8192], F32))
        with nc.Block("setup") as blk:
            @blk.gpsimd
            def _(g):
                sw.inc(g.dma_start(wq_kv[:].rearrange("p (c j) -> p c j", c=8),
                                   wqkv[:, D:3 * D].rearrange("(c p) j -> p c j", p=128)), 16)
                sw.inc(g.dma_start(maskb[:].rearrange("p (r q) -> p r q", r=4),
                                   masks.rearrange("r k q -> k r q")), 16)
                sw.inc(g.dma_start(id_b[:], ident), 16)

            @blk.vector
            def _(v):
                cv(v, "memset", zb[:], 0.0)
                cv(v, "memset", eps5[:], EPS_LN)
                mark("zb", soV)

            @blk.sync
            def _(sy):
                wg(sy, soV, "zb")
                lt_flat = l_tab.rearrange("h g q -> (h g q)").rearrange("(a b) -> a b", a=64)
                szf.inc(sy.dma_start(lt_flat[:, :], zb[:]), 16)
                sy.wait_ge(szf.h, 16)

    with ExitStack() as actx:
        Ea = actx.enter_context
        wq_q = Ea(nc.sbuf_tensor("wq_q", [128, 8 * D], BF16))
        rmsb = Ea(nc.sbuf_tensor("rmsb", [128, D], F32))
        coefT = Ea(nc.sbuf_tensor("coefT", [8, NCH * 128], F32))
        id_f = Ea(nc.sbuf_tensor("id_f", [128, 128], F32))
        coef = Ea(nc.sbuf_tensor("coef", [128, NCH * NE], F32))
        eps6 = Ea(nc.sbuf_tensor("eps6", [128, 1], F32))
        ghi = Ea(nc.sbuf_tensor("ghi", [128, 8 * NE], BF16))
        glo = Ea(nc.sbuf_tensor("glo", [128, 8 * NE], BF16))
        gf = Ea(nc.sbuf_tensor("gf", [128, 8 * NE], F32))
        xbuf = [Ea(nc.sbuf_tensor(f"a0x{i}", [128, D], F32)) for i in range(4)]
        xnb = [Ea(nc.sbuf_tensor(f"a0n{i}", [128, D], BF16)) for i in range(2)]
        xn2b = [Ea(nc.sbuf_tensor(f"a0n2{i}", [128, D], BF16)) for i in range(2)]
        lob = [Ea(nc.sbuf_tensor(f"a0lo{i}", [128, D], BF16)) for i in range(2)]
        xnf = Ea(nc.sbuf_tensor("a0xnf", [128, D], F32))
        xrf = Ea(nc.sbuf_tensor("a0xrf", [128, D], F32))
        x2f = Ea(nc.sbuf_tensor("a0x2f", [128, D], F32))
        bst = Ea(nc.sbuf_tensor("a0bst", [128, 48], F32))
        mv = Ea(nc.sbuf_tensor("a0mv", [128, 8], F32))
        xnt = Ea(nc.sbuf_tensor("a0xnt", [128, 8 * 512], BF16))
        x2t = Ea(nc.sbuf_tensor("a0x2t", [128, 8 * 512], BF16))
        lot = [Ea(nc.sbuf_tensor(f"a0lot{i}", [128, 8 * 128], BF16)) for i in range(2)]
        cst = {nm: Ea(nc.sbuf_tensor("a0" + nm, [128, 4], F32))
               for nm in "vv rec ms lnv rstd lnm cc".split()}
        g16 = {nm: Ea(nc.sbuf_tensor("a0g" + nm, [128, 16], F32))
               for nm in "gm ngm gs gr".split()}
        Lsb = Ea(nc.sbuf_tensor("a0L", [128, NCH * NE], F32))
        ohb = Ea(nc.sbuf_tensor("a0oh", [128, NCH * NE], BF16))
        egs = Ea(nc.sbuf_tensor("a0eg", [128, NE], F32))
        ps_tr = [Ea(nc.psum_tensor(f"a0ptr{i}", [128, 1024], BF16)) for i in range(2)]
        ps_q = [Ea(nc.psum_tensor(f"a0pq{i}", [128, 512], F32)) for i in range(2)]
        ps_g = [Ea(nc.psum_tensor(f"a0pg{i}", [128, NE], F32)) for i in range(2)]
        ps_ct = Ea(nc.psum_tensor("a0pct", [8, 128], F32))

        with nc.Block("a0") as blk:
            @blk.gpsimd
            def _(g):
                sw.inc(g.dma_start(wq_q[:].rearrange("p (c j) -> p c j", c=8),
                                   wqkv[:, 0:D].rearrange("(c p) j -> p c j", p=128)), 16)

            @blk.sync
            def _(sy):
                sxl.inc(sy.dma_start(rmsb[:], rms_bc), 16)
                sxl.inc(sy.dma_start(id_f[:], ident), 16)
                sxl.inc(sy.dma_start(gf[:].rearrange("p (c e) -> p c e", c=8),
                                     gate_t.rearrange("(c p) e -> p c e", p=128)), 16)

                def spill(ch):
                    if "nospill" in sub:
                        return
                    sxn[ch % 2].inc(sy.dma_start(xs2_sp[ch * 128:(ch + 1) * 128, :],
                                                 xn2b[ch % 2][:]), 16)
                for st in range(4):
                    for i in range(4):
                        ch = st * 4 + i
                        if ch >= 4:
                            wg(sy, soV, ("a0ln", ch - 4))
                        sxsb[ch % 4].inc(sy.dma_start(xbuf[ch % 4][:], x_own[ch * 128:(ch + 1) * 128, :]), 16)
                        if ch >= 4:
                            spill(ch - 4)
                for ch in range(12, 16):
                    wg(sy, soV, ("a0ln", ch))
                    spill(ch)

            @blk.vector
            def _(v):
                c = cst
                cv(v, "memset", eps6[:], EPS_RMS)
                v.wait_ge(sxl.h, 48)
                cv(v, "tensor_copy", ghi[:], gf[:])
                cv(v, "scalar_tensor_tensor", glo[:], ghi[:], -1.0, gf[:], ALU.mult, ALU.add)
                mark("gate_hl", soV)
                for st in range(4):
                    for i in range(4):
                        ch = st * 4 + i
                        v.wait_ge(sxsb[ch % 4].h, 16 * (ch // 4 + 1))
                        cv(v, "bn_stats", bst[:, i * 12:i * 12 + 6], xbuf[ch % 4][:, 0:512])
                        cv(v, "bn_stats", bst[:, i * 12 + 6:i * 12 + 12], xbuf[ch % 4][:, 512:1024])
                    for i in range(4):
                        cv(v, "bn_aggr", mv[:, i * 2:i * 2 + 2], bst[:, i * 12:i * 12 + 12])
                    for i in range(4):
                        cv(v, "tensor_scalar", c["vv"][:, i:i + 1], mv[:, 2 * i + 1:2 * i + 2],
                           EPS_LN, None, ALU.add)
                    for i in range(4):
                        cv(v, "reciprocal", c["rec"][:, i:i + 1], c["vv"][:, i:i + 1])
                    for i in range(4):
                        cv(v, "tensor_mul", c["ms"][:, i:i + 1], mv[:, 2 * i + 1:2 * i + 2],
                           c["rec"][:, i:i + 1])
                    mark(("a0var", st), soV)
                    wg(v, soA, ("a0rstd", st))
                    for i in range(4):
                        ch = st * 4 + i
                        if ch >= 2:
                            wg(v, soT, ("a0tr", ch - 2))
                            v.wait_ge(sxn[ch % 2].h, 16 * (ch // 2))
                        cv(v, "tensor_scalar", xnf[:], xbuf[ch % 4][:], mv[:, 2 * i:2 * i + 1],
                           c["rstd"][:, i:i + 1], ALU.subtract, ALU.mult)
                        cv(v, "tensor_copy", xnb[ch % 2][:], xnf[:])
                        cv(v, "tensor_mul", xrf[:], xnf[:], rmsb[:])
                        cv(v, "tensor_scalar", x2f[:], xrf[:], c["cc"][:, i:i + 1], None, ALU.mult)
                        cv(v, "tensor_copy", xn2b[ch % 2][:], x2f[:])
                        cv(v, "scalar_tensor_tensor", lob[ch % 2][:], xn2b[ch % 2][:], -1.0, x2f[:],
                           ALU.mult, ALU.add)
                        mark(("a0ln", ch), soV)

            @blk.tensor
            def _(t):
                if "notr" in sub:
                    return
                t.wait_ge(sw.h, 64)
                wg(t, soV, "gate_hl")
                for st in range(4):
                    for i in range(4):
                        ch = st * 4 + i
                        wg(t, soV, ("a0ln", ch))
                        for sj, src in enumerate((xnb[ch % 2], xn2b[ch % 2], lob[ch % 2])):
                            for j in range(8):
                                ntr = 24 * ch + 8 * sj + j
                                if ntr >= 2:
                                    wg(t, soA, ("a0te", ntr - 2))
                                soT.inc(t.transpose(ps_tr[ntr % 2][:, 0:128],
                                                    src[:, j * 128:(j + 1) * 128], id_b[:]))
                                mark(("a0trj", ntr), soT)
                        mark(("a0tr", ch), soT)
                        if "nogate" in sub:
                            continue
                        wg(t, soA, ("a0te_ch", ch))
                        if ch >= 2:
                            wg(t, soA, ("a0gl", ch - 2))
                        nmm = 0
                        ins = None
                        for cd in range(8):
                            for src2 in range(3):
                                if src2 == 0:
                                    lhsT, rh = x2t[:, cd * 512 + i * 128:cd * 512 + i * 128 + 128], ghi
                                elif src2 == 1:
                                    lhsT, rh = x2t[:, cd * 512 + i * 128:cd * 512 + i * 128 + 128], glo
                                else:
                                    lhsT, rh = lot[ch % 2][:, cd * 128:(cd + 1) * 128], ghi
                                nmm += 1
                                ins = t.matmul(ps_g[ch % 2][:], lhsT, rh[:, cd * NE:(cd + 1) * NE],
                                               start=(nmm == 1), stop=(nmm == 24))
                        soT.inc(ins)
                        mark(("a0gate", ch), soT)
                    for gp in range(8):
                        if "noqt" in sub:
                            break
                        k = st * 8 + gp
                        if k >= 2:
                            wg(t, soA, ("a0qe", k - 2))
                        ins = None
                        for cd in range(8):
                            ins = t.matmul(ps_q[k % 2][:], wq_q[:, cd * D + gp * 128:cd * D + (gp + 1) * 128],
                                           xnt[:, cd * 512:(cd + 1) * 512], start=(cd == 0), stop=(cd == 7))
                        soT.inc(ins)
                        mark(("a0qt", k), soT)
                if "nogate" not in sub:
                    wg(t, soV, ("a0coef", NCH - 1))
                    for ch in range(NCH):
                        if ch >= 1:
                            wg(t, soV, ("a0ctv", ch - 1))
                        soT.inc(t.transpose(ps_ct[:], coef[:, ch * NE:(ch + 1) * NE], id_f[:]))
                        mark(("a0ct", ch), soT)

            @blk.scalar
            def _(s):
                c = cst
                for st in range(4):
                    wg(s, soV, ("a0var", st))
                    for i in range(4):
                        cs(s, "activation", c["lnv"][:, i:i + 1], mv[:, 2 * i + 1:2 * i + 2], AF.Ln, bias=eps5[:])
                    for i in range(4):
                        cs(s, "activation", c["lnm"][:, i:i + 1], c["ms"][:, i:i + 1], AF.Ln, bias=eps6[:])
                    for i in range(4):
                        cs(s, "activation", c["rstd"][:, i:i + 1], c["lnv"][:, i:i + 1], AF.Exp, scale=-0.5)
                    for i in range(4):
                        cs(s, "activation", c["cc"][:, i:i + 1], c["lnm"][:, i:i + 1], AF.Exp, scale=-0.5)
                    mark(("a0rstd", st), soA)
                    if "notr" in sub or "noev" in sub:
                        continue
                    for i in range(4):
                        ch = st * 4 + i
                        if ch >= 2:
                            wg(s, soT, ("a0gate", ch - 2))
                        if i == 0 and st >= 1:
                            wg(s, soT, ("a0qt", st * 8 - 1))
                            if "nogate" not in sub:
                                wg(s, soT, ("a0gate", st * 4 - 1))
                        for sj, dst in enumerate(("xnt", "x2t", "lot")):
                            for j in range(8):
                                ntr = 24 * ch + 8 * sj + j
                                wg(s, soT, ("a0trj", ntr))
                                psl = ps_tr[ntr % 2][:, 0:128]
                                if dst == "xnt":
                                    cs(s, "activation", xnt[:, j * 512 + i * 128:j * 512 + i * 128 + 128],
                                       psl, AF.Copy)
                                elif dst == "x2t":
                                    cs(s, "activation", x2t[:, j * 512 + i * 128:j * 512 + i * 128 + 128],
                                       psl, AF.Copy)
                                else:
                                    cs(s, "activation", lot[ch % 2][:, j * 128:(j + 1) * 128], psl, AF.Copy)
                                mark(("a0te", ntr), soA)
                        mark(("a0te_ch", ch), soA)
                        if "nogate" not in sub:
                            wg(s, soT, ("a0gate", ch))
                            cs(s, "activation", Lsb[:, ch * NE:(ch + 1) * NE], ps_g[ch % 2][:], AF.Copy)
                            mark(("a0gl", ch), soA)
                    for gp in range(8):
                        if "noqt" in sub:
                            break
                        k = st * 8 + gp
                        wg(s, soT, ("a0qt", k))
                        cs(s, "activation", QT[gp][:, st * 512:(st + 1) * 512], ps_q[k % 2][:], AF.Copy)
                        mark(("a0qe", k), soA)
                if "notr" not in sub and "nogate" not in sub:
                    for ch in range(NCH):
                        wg(s, soV, ("a0ngm", ch))
                        cs(s, "activation", egs[:], Lsb[:, ch * NE:(ch + 1) * NE], AF.Exp,
                           bias=g16["ngm"][:, ch:ch + 1], accum_out=g16["gs"][:, ch:ch + 1])
                        mark(("a0ga", ch), soA)

            @blk.vector
            def _(v):
                if "notr" in sub or "nogate" in sub:
                    return
                for ch in range(NCH):
                    wg(v, soA, ("a0gl", ch))
                    cv(v, "reduce_max", g16["gm"][:, ch:ch + 1], Lsb[:, ch * NE:(ch + 1) * NE], axis=AX.X)
                for ch in range(NCH):
                    cv(v, "tensor_scalar", g16["ngm"][:, ch:ch + 1], g16["gm"][:, ch:ch + 1],
                       -1.0, None, ALU.mult)
                    mark(("a0ngm", ch), soV)
                for ch in range(NCH):
                    wg(v, soA, ("a0ga", ch))
                    cv(v, "reciprocal", g16["gr"][:, ch:ch + 1], g16["gs"][:, ch:ch + 1])
                for ch in range(NCH):
                    cv(v, "tensor_scalar", coef[:, ch * NE:(ch + 1) * NE],
                       Lsb[:, ch * NE:(ch + 1) * NE],
                       g16["gm"][:, ch:ch + 1], g16["gr"][:, ch:ch + 1], ALU.is_equal, ALU.mult)
                    mark(("a0coef", ch), soV)
                    cv(v, "tensor_scalar", ohb[:, ch * NE:(ch + 1) * NE],
                       Lsb[:, ch * NE:(ch + 1) * NE], g16["gm"][:, ch:ch + 1], None, ALU.is_equal)
                    mark(("a0oh", ch), soV)
                for ch in range(NCH):
                    wg(v, soT, ("a0ct", ch))
                    cv(v, "tensor_copy", coefT[:, ch * 128:(ch + 1) * 128], ps_ct[:])
                    mark(("a0ctv", ch), soV)

            @blk.sync
            def _(sy):
                if "nogate" not in sub:
                    wg(sy, soV, ("a0ctv", NCH - 1))
                    ssp.inc(sy.dma_start(coeft_sp, coefT[:]), 16)
                    wg(sy, soV, ("a0coef", NCH - 1))
                    ssp.inc(sy.dma_start(coef_sp, coef[:]), 16)
                    wg(sy, soV, ("a0oh", NCH - 1))
                    ssp.inc(sy.dma_start(ohb_sp, ohb[:]), 16)
                    sy.wait_ge(sxn[0].h, sxn[0].n)
                    sy.wait_ge(sxn[1].h, sxn[1].n)
                if dbg and "notr" not in sub:
                    wg(sy, soA, ("a0qe", 31))
                    for gp in range(NPAIR):
                        ssp.inc(sy.dma_start(d_qt[:, gp * TOWN:(gp + 1) * TOWN], QT[gp][:]), 16)
                    wg(sy, soV, ("a0coef", NCH - 1))
                    ssp.inc(sy.dma_start(d_coef, coef[:]), 16)
                sy.wait_ge(ssp.h, ssp.n)

    if upto == "a0":
        with nc.Block("fin0") as blk:
            @blk.sync
            def _(sy):
                sout_.inc(sy.dma_start(out_t, x_own), 16)
                sy.wait_ge(sout_.h, 16)
        ctx.close()
        return nc

    scg = []
    tot = 0
    for g in range(NG):
        tot += H * 4 * len(_pieces(g))
        scg.append(tot)

    with ExitStack() as wctx:
        OTacc = wctx.enter_context(nc.sbuf_tensor("OTacc", [128, 8 * TOWN], F32))
        with ExitStack() as a1c:
            E1 = a1c.enter_context
            xs = [E1(nc.sbuf_tensor(f"a1x{i}", [128, D], F32)) for i in range(4)]
            xnb1 = [E1(nc.sbuf_tensor(f"a1n{i}", [128, D], BF16)) for i in range(2)]
            bst1 = E1(nc.sbuf_tensor("a1bst", [128, 48], F32))
            mv1 = E1(nc.sbuf_tensor("a1mv", [128, 8], F32))
            xnt1 = E1(nc.sbuf_tensor("a1xnt", [128, 8 * 512], BF16))
            ktg = E1(nc.sbuf_tensor("a1kt", [128, 8 * 512], BF16))
            vg = [E1(nc.sbuf_tensor(f"a1v{i}", [128, H * 65], BF16)) for i in range(4)]
            PT = [[E1(nc.sbuf_tensor(f"a1pt{s0}_{k}", [128, TOWN], BF16)) for k in range(4)]
                  for s0 in range(2)]
            tmpl = [E1(nc.sbuf_tensor(f"a1tl{i}", [1, 256], F32)) for i in range(2)]
            c1 = {nm: E1(nc.sbuf_tensor("a1" + nm, [128, 4], F32))
                  for nm in "lnv rstd".split()}
            ps_s = E1(nc.psum_tensor("a1ps", [128, 2048], F32))
            ps_big = E1(nc.psum_tensor("a1pb", [128, 2048], F32))

            nld_tab = {}
            nld = 0
            for g in range(NG):
                for h in range(H):
                    for li in range(len(_lspans(g))):
                        nld_tab[(g, h, li)] = nld
                        nld += 1
            TOT_L = nld

            with nc.Block("a1") as blk:
                @blk.sync
                def _(sy):
                    def transposes(j):
                        if j >= 1:
                            wg(sy, soT, ("a1vm", j - 1, 3))
                        for i in range(4):
                            s_ = 4 * j + i
                            wg(sy, soV, ("a1xnb", s_))
                            for b_ in range(8):
                                stp[s_ % 2].inc(sy.dma_start(
                                    xnt1[:, b_ * 512 + i * 128:b_ * 512 + i * 128 + 128],
                                    xnb1[s_ % 2][:, b_ * 128:(b_ + 1) * 128],
                                    transpose=True), 16)

                    for g in range(NG):
                        for i in range(4):
                            s_ = 4 * g + i
                            if s_ >= 4:
                                wg(sy, soV, ("a1xnb", s_ - 4))
                            sxsb[s_ % 4].inc(sy.dma_start(xs[s_ % 4][:], x_str[s_ * 128:(s_ + 1) * 128, :]), 16)
                        transposes(g)

                @blk.gpsimd
                def _(g_):
                    def lstores(g):
                        for h in range(H):
                            for li, (a, b) in enumerate(_lspans(g)):
                                nl = nld_tab[(g, h, li)]
                                wg(g_, soV, ("a1lev", g, h, li))
                                slp[nl % 2].inc(g_.dma_start(
                                    l_tab2[h * NG + g:h * NG + g + 1, a:b],
                                    tmpl[nl % 2][0:1, 0:b - a]), 16)

                    for g in range(NG):
                        lstores(g)
                    g_.wait_ge(slp[0].h, slp[0].n)
                    g_.wait_ge(slp[1].h, slp[1].n)

                @blk.vector
                def _(v):
                    for i in range(4):
                        cv(v, "memset", vg[i][:].rearrange("p (h j) -> p h j", j=65)[:, :, 64:65], 1.0)

                    def stats(g):
                        for i in range(4):
                            s_ = 4 * g + i
                            v.wait_ge(sxsb[s_ % 4].h, 16 * (4 + s_ // 4 + 1))
                            cv(v, "bn_stats", bst1[:, i * 12:i * 12 + 6], xs[s_ % 4][:, 0:512])
                            cv(v, "bn_stats", bst1[:, i * 12 + 6:i * 12 + 12], xs[s_ % 4][:, 512:1024])
                        for i in range(4):
                            cv(v, "bn_aggr", mv1[:, i * 2:i * 2 + 2], bst1[:, i * 12:i * 12 + 12])
                        mark(("a1var", g), soV)

                    def xnb_one(g, i):
                        if i == 0:
                            wg(v, soA, ("a1rstd", g))
                        s_ = 4 * g + i
                        if s_ >= 2:
                            v.wait_ge(stp[s_ % 2].h, 16 * 8 * (s_ // 2))
                        cv(v, "tensor_scalar", xnb1[s_ % 2][:], xs[s_ % 4][:], mv1[:, 2 * i:2 * i + 1],
                           c1["rstd"][:, i:i + 1], ALU.subtract, ALU.mult)
                        mark(("a1xnb", s_), soV)

                    def kt_ev(j):
                        for p8 in range(8):
                            rb = (p8 % 2) * 512
                            wg(v, soT, ("a1ktm", j, p8))
                            if j >= 1 and p8 == 0:
                                wg(v, soT, ("a1smm", scg[j - 1] - 1))
                            cv(v, "tensor_copy", ktg[:, p8 * 512:(p8 + 1) * 512],
                               ps_big[:, rb:rb + 512])
                            mark(("a1kt", j, p8), soV)

                    stats(0)
                    for i in range(4):
                        xnb_one(0, i)
                    kt_ev(0)
                    for g in range(NG):
                        for i in range(4):
                            wg(v, soT, ("a1vm", g, i))
                            if g >= 1:
                                wg(v, soT, ("a1pv", g - 1, 15))
                            rb = 1024 * (i % 2)
                            cv(v, "tensor_copy",
                               vg[i][:].rearrange("p (h j) -> p h j", j=65)[:, :, 0:64],
                               ps_big[:, rb:rb + 1024].rearrange("p (h j) -> p h j", j=64))
                            mark(("a1ve", g, i), soV)
                        if g + 1 < NG:
                            stats(g + 1)
                            for i in range(4):
                                xnb_one(g + 1, i)
                        qlo = 128 * g
                        for h in range(H):
                            wg(v, soT, ("a1pv", g, h))
                            hc, r0 = h // 2, 64 * (h % 2)
                            dstap = OTacc[r0:r0 + 64, hc * TOWN + qlo:(hc + 1) * TOWN]
                            if g == 0:
                                cv(v, "tensor_copy", dstap, ps_big[0:64, qlo:TOWN])
                            else:
                                cv(v, "tensor_add", dstap, dstap, ps_big[0:64, qlo:TOWN])
                            for li, (a, b) in enumerate(_lspans(g)):
                                nl = nld_tab[(g, h, li)]
                                if nl >= 2:
                                    v.wait_ge(slp[nl % 2].h, 16 * (nl // 2))
                                cv(v, "tensor_copy", tmpl[nl % 2][0:1, 0:b - a], ps_big[64:65, a:b])
                                mark(("a1lev", g, h, li), soV)
                            mark(("a1ote", g, h), soV)
                        if g + 1 < NG:
                            kt_ev(g + 1)

                @blk.tensor
                def _(t):
                    t.wait_ge(sw.h, 64)

                    def kv(j):
                        t.wait_ge(stp[0].h, 16 * 16 * (j + 1))
                        t.wait_ge(stp[1].h, 16 * 16 * (j + 1))
                        if j >= 1:
                            wg(t, soV, ("a1ote", j - 1, 15))
                        for p8 in range(8):
                            rb = (p8 % 2) * 512
                            if p8 >= 2:
                                wg(t, soV, ("a1kt", j, p8 - 2))
                            ins = None
                            for cd in range(8):
                                ins = t.matmul(ps_big[:, rb:rb + 512],
                                               wq_kv[:, cd * 2048 + p8 * 128:cd * 2048 + (p8 + 1) * 128],
                                               xnt1[:, cd * 512:(cd + 1) * 512],
                                               start=(cd == 0), stop=(cd == 7))
                            soT.inc(ins)
                            mark(("a1ktm", j, p8), soT)
                        wg(t, soV, ("a1kt", j, 7))
                        for i in range(4):
                            rb = 1024 * (i % 2)
                            if i >= 2:
                                wg(t, soV, ("a1ve", j, i - 2))
                            ins = None
                            for cd in range(8):
                                lhsT = xnt1[:, cd * 512 + i * 128:cd * 512 + i * 128 + 128]
                                ins = t.matmul(ps_big[:, rb:rb + 512], lhsT,
                                               wq_kv[:, cd * 2048 + 1024:cd * 2048 + 1536],
                                               start=(cd == 0), stop=(cd == 7))
                                ins = t.matmul(ps_big[:, rb + 512:rb + 1024], lhsT,
                                               wq_kv[:, cd * 2048 + 1536:cd * 2048 + 2048],
                                               start=(cd == 0), stop=(cd == 7))
                            soT.inc(ins)
                            mark(("a1vm", j, i), soT)

                    nsc = [0]

                    def scores(g, h):
                        gp, r0 = h // 2, 64 * (h % 2)
                        for k in range(4):
                            for pi, (a, b) in enumerate(_pieces(g)):
                                jj = nsc[0]
                                nsc[0] += 1
                                if jj >= 4:
                                    wg(t, soA, ("a1expj", jj - 4))
                                ins = t.matmul(ps_s[:, (jj % 4) * 512:(jj % 4) * 512 + (b - a)],
                                               ktg[r0:r0 + 64, gp * 512 + k * 128:gp * 512 + (k + 1) * 128],
                                               QT[gp][r0:r0 + 64, a:b], start=True, stop=(pi != 0))
                                if pi == 0:
                                    ins = t.matmul(ps_s[:, (jj % 4) * 512:(jj % 4) * 512 + 128],
                                                   maskb[:, k * 128:(k + 1) * 128], id_b[:],
                                                   start=False, stop=True)
                                soT.inc(ins)
                                mark(("a1smm", jj), soT)

                    def pv(g, h):
                        wg(t, soA, ("a1exp", g, h))
                        if h == 0:
                            wg(t, soV, ("a1ve", g, 3))
                        if h >= 1:
                            wg(t, soV, ("a1ote", g, h - 1))
                        ins = None
                        for (a, b) in _pieces(g):
                            for k in range(4):
                                ins = t.matmul(ps_big[0:65, a:b], vg[k][:, h * 65:(h + 1) * 65],
                                               PT[h % 2][k][:, a:b], start=(k == 0), stop=(k == 3))
                        soT.inc(ins)
                        mark(("a1pv", g, h), soT)

                    kv(0)
                    for g in range(NG):
                        scores(g, 0)
                        for h in range(H):
                            if h + 1 < H:
                                scores(g, h + 1)
                            pv(g, h)
                        if g + 1 < NG:
                            kv(g + 1)

                @blk.scalar
                def _(s):
                    def rstd_chain(g):
                        wg(s, soV, ("a1var", g))
                        for i in range(4):
                            cs(s, "activation", c1["lnv"][:, i:i + 1], mv1[:, 2 * i + 1:2 * i + 2], AF.Ln, bias=eps5[:])
                        for i in range(4):
                            cs(s, "activation", c1["rstd"][:, i:i + 1], c1["lnv"][:, i:i + 1], AF.Exp, scale=-0.5)
                        mark(("a1rstd", g), soA)

                    rstd_chain(0)
                    nsc = 0
                    for g in range(NG):
                        if g + 1 < NG:
                            rstd_chain(g + 1)
                        for h in range(H):
                            if h >= 2:
                                wg(s, soT, ("a1pv", g, h - 2))
                            elif g >= 1:
                                wg(s, soT, ("a1pv", g - 1, 14 + h))
                            for k in range(4):
                                for (a, b) in _pieces(g):
                                    wg(s, soT, ("a1smm", nsc))
                                    cs(s, "activation", PT[h % 2][k][:, a:b],
                                       ps_s[:, (nsc % 4) * 512:(nsc % 4) * 512 + (b - a)],
                                       AF.Exp, scale=SC)
                                    mark(("a1expj", nsc), soA)
                                    nsc += 1
                            mark(("a1exp", g, h), soA)

        ctxB.close()
        with ExitStack() as woc:
            Ew = woc.enter_context
            OTb = Ew(nc.sbuf_tensor("OTb", [128, 8 * TOWN], BF16))
            on16 = Ew(nc.sbuf_tensor("on16", [16, 128], F32))
            wos = Ew(nc.sbuf_tensor("wos", [128, 8 * D], BF16))
            lts = [Ew(nc.sbuf_tensor(f"lts{i}", [16, TOWN], F32)) for i in range(2)]
            rlb = Ew(nc.sbuf_tensor("rlb", [128, TOWN], F32))
            xo = [Ew(nc.sbuf_tensor(f"xo{i}", [128, D], F32)) for i in range(2)]
            axt = [Ew(nc.sbuf_tensor(f"axt{i}", [128, D], F32)) for i in range(2)]
            ps_lb = Ew(nc.psum_tensor("ps_lb", [128, TOWN], F32))
            ps_wo = [Ew(nc.psum_tensor(f"ps_wo{i}", [128, D], F32)) for i in range(2)]

            A0SP = ssp.n

            with nc.Block("wo") as blk:
                @blk.gpsimd
                def _(g_):
                    sw.inc(g_.dma_start(wos[:].rearrange("p (c j) -> p c j", c=8),
                                        w_o_in.rearrange("(c p) j -> p c j", p=128)), 16)

                @blk.sync
                def _(sy):
                    for h in range(H):
                        if h >= 2:
                            wg(sy, soT, ("wolb", h - 2))
                        slt[h % 2].inc(sy.dma_start(lts[h % 2][:], l_tab2[h * NG:(h + 1) * NG, :]), 16)
                    for tc in range(NCH):
                        if tc >= 2:
                            wg(sy, soV, ("woax", tc - 2))
                        sxo[tc % 2].inc(sy.dma_start(xo[tc % 2][:], x_own[tc * 128:(tc + 1) * 128, :]), 16)
                        if tc >= 2:
                            wg(sy, soV, ("woax", tc - 2))
                            sat[tc % 2].inc(sy.dma_start(attnx_sp[(tc - 2) * 128:(tc - 1) * 128, :],
                                                         axt[tc % 2][:]), 16)
                    for tc in range(NCH - 2, NCH):
                        wg(sy, soV, ("woax", tc))
                        sat[tc % 2].inc(sy.dma_start(attnx_sp[tc * 128:(tc + 1) * 128, :], axt[tc % 2][:]), 16)
                    if dbg:
                        wg(sy, soV, ("wootb", H - 1))
                        ssp.inc(sy.dma_start(d_ot, OTacc[:]), 16)
                    sy.wait_ge(ssp.h, ssp.n)
                    sy.wait_ge(sat[0].h, sat[0].n)
                    sy.wait_ge(sat[1].h, sat[1].n)

                @blk.tensor
                def _(t):
                    t.wait_ge(sw.h, 80)
                    wg(t, soV, "on16")
                    for h in range(H):
                        t.wait_ge(slt[h % 2].h, 16 * (h // 2 + 1))
                        if h >= 1:
                            wg(t, soV, ("worl", h - 1))
                        ins = None
                        for pc in range(4):
                            ins = t.matmul(ps_lb[:, pc * 512:(pc + 1) * 512], on16[:],
                                           lts[h % 2][:, pc * 512:(pc + 1) * 512], start=True, stop=True)
                        soT.inc(ins)
                        mark(("wolb", h), soT)
                    for tc in range(NCH):
                        wg(t, soV, ("wootb", H - 1))
                        if tc >= 2:
                            wg(t, soV, ("woax", tc - 2))
                        ins = None
                        for cd in range(8):
                            lhsT = OTb[:, cd * TOWN + tc * 128:cd * TOWN + (tc + 1) * 128]
                            ins = t.matmul(ps_wo[tc % 2][:, 0:512], lhsT, wos[:, cd * D:cd * D + 512],
                                           start=(cd == 0), stop=(cd == 7))
                            ins = t.matmul(ps_wo[tc % 2][:, 512:1024], lhsT, wos[:, cd * D + 512:(cd + 1) * D],
                                           start=(cd == 0), stop=(cd == 7))
                        soT.inc(ins)
                        mark(("womm", tc), soT)

                @blk.vector
                def _(v):
                    cv(v, "memset", on16[:], 1.0)
                    mark("on16", soV)
                    for h in range(H):
                        wg(v, soT, ("wolb", h))
                        cv(v, "reciprocal", rlb[:], ps_lb[:])
                        mark(("worl", h), soV)
                        hc, r0 = h // 2, 64 * (h % 2)
                        cv(v, "tensor_mul", OTb[r0:r0 + 64, hc * TOWN:(hc + 1) * TOWN],
                           OTacc[r0:r0 + 64, hc * TOWN:(hc + 1) * TOWN], rlb[r0:r0 + 64, :])
                        mark(("wootb", h), soV)
                    for tc in range(NCH):
                        wg(v, soT, ("womm", tc))
                        v.wait_ge(sxo[tc % 2].h, 16 * (tc // 2 + 1))
                        if tc >= 2:
                            v.wait_ge(sat[tc % 2].h, 16 * ((tc - 2) // 2 + 1))
                        cv(v, "tensor_add", axt[tc % 2][:], xo[tc % 2][:], ps_wo[tc % 2][:])
                        mark(("woax", tc), soV)

    A1SP = ssp.n
    NB = 15
    with ExitStack() as mctx:
        Em = mctx.enter_context
        w1s = [Em(nc.sbuf_tensor(f"mw1{i}", [128, 8 * 1024], BF16)) for i in range(2)]
        w2s = [Em(nc.sbuf_tensor(f"mw2{i}", [128, 8 * 1024], BF16)) for i in range(2)]
        b1blk = [Em(nc.sbuf_tensor(f"mb1{i}", [128, 32], F32)) for i in range(2)]
        b2s = Em(nc.sbuf_tensor("mb2", [8, D], F32))
        coefT = Em(nc.sbuf_tensor("mcoefT", [8, NCH * 128], F32))
        coef = Em(nc.sbuf_tensor("mcoef", [128, NCH * NE], F32))
        xgs = [Em(nc.sbuf_tensor(f"mxg{i}", [128, 2048], BF16)) for i in range(2)]
        xsT = [Em(nc.sbuf_tensor(f"mxt{i}", [128, 2048], BF16)) for i in range(2)]
        hslq = [Em(nc.sbuf_tensor(f"mhs{i}", [128, 2048], BF16)) for i in range(2)]
        ysv = [Em(nc.sbuf_tensor(f"mys{i}", [128, 2048], F32)) for i in range(2)]
        ygb = [Em(nc.sbuf_tensor(f"myg{i}", [128, D], F32)) for i in range(2)]
        axt2 = [Em(nc.sbuf_tensor(f"max{i}", [128, D], F32)) for i in range(2)]
        outt = [Em(nc.sbuf_tensor(f"mo{i}", [128, D], F32)) for i in range(2)]
        slotf = Em(nc.sbuf_tensor("mslotf", [128, NCH], F32))
        slot_i = Em(nc.sbuf_tensor("msloti", [128, NCH], I32))
        wif = Em(nc.sbuf_tensor("mwif", [128, 60], F32))
        wi32 = Em(nc.sbuf_tensor("mwi32", [128, 60], I32))
        bif = Em(nc.sbuf_tensor("mbif", [128, 15], F32))
        bi32 = Em(nc.sbuf_tensor("mbi32", [128, 15], I32))
        sc16 = Em(nc.sbuf_tensor("msc16", [128, NCH], F32))
        ebc = Em(nc.sbuf_tensor("mebc", [128, 15], F32))
        iq = Em(nc.sbuf_tensor("miq", [128, 4], F32))
        zbm = Em(nc.sbuf_tensor("mzbm", [128, D], BF16))

        with ExitStack() as rctx:
            Er = rctx.enter_context
            idm = Er(nc.sbuf_tensor("midm", [128, 128], F32))
            ohb = Er(nc.sbuf_tensor("mohb", [128, NCH * NE], BF16))
            trit = Er(nc.sbuf_tensor("mtrit", [128, 1152], BF16))
            tri8b = Er(nc.sbuf_tensor("mtri8", [8, 8], BF16))
            rt8 = Er(nc.sbuf_tensor("mrt8", [8, 24], F32))
            ok1 = Er(nc.sbuf_tensor("mok1", [1, 128], F32))
            Cc = Er(nc.sbuf_tensor("mCc", [8, TOWN], F32))
            Ctok = Er(nc.sbuf_tensor("mCtok", [128, NCH * NE], F32))
            tmp8 = Er(nc.sbuf_tensor("mtmp8", [8, 8], F32))
            bsum = Er(nc.sbuf_tensor("mbsum", [8, 1], F32))
            bs256b = Er(nc.sbuf_tensor("mbs256", [8, 1], BF16))
            padoffs = Er(nc.sbuf_tensor("mpoffs", [8, 1], F32))
            porow = Er(nc.sbuf_tensor("mporow", [1, 8], F32))
            padb = Er(nc.sbuf_tensor("mpadb", [128, 8], F32))
            cmp_ = Er(nc.sbuf_tensor("mcmpb", [8, 15], F32))
            ebrE = Er(nc.sbuf_tensor("mebrE", [1, 15], F32))
            ebr = Er(nc.sbuf_tensor("mebr", [1, 15], F32))
            tmpA = Er(nc.sbuf_tensor("mtmpA", [128, 8], F32))
            xsb = [Er(nc.sbuf_tensor(f"mxsb{i}", [128, D], BF16)) for i in range(2)]
            ps_c = Er(nc.psum_tensor("mpc", [8, 512], F32))
            ps_t8 = [Er(nc.psum_tensor(f"mpt8{i}", [128, 8], F32)) for i in range(2)]
            ps_sm = Er(nc.psum_tensor("mpsm", [8, 128], F32))
            ps_bc = Er(nc.psum_tensor("mpbc", [128, 32], F32))

            with nc.Block("mrt") as blk:
                @blk.sync
                def _(sy):
                    sy.wait_ge(ssp.h, A1SP)
                    smc.inc(sy.dma_start(idm[:], ident), 16)
                    smc.inc(sy.dma_start(ohb[:], ohb_sp), 16)
                    smc.inc(sy.dma_start(trit[:], tritab_in), 16)
                    smc.inc(sy.dma_start(tri8b[:], tri8_in), 16)
                    smc.inc(sy.dma_start(rt8[:], rt8_in), 16)
                    smc.inc(sy.dma_start(ok1[:], onesk1_in), 16)
                    smc.inc(sy.dma_start(iq[:], iq_in), 16)
                    smc.inc(sy.dma_start(b2s[:], b2_in), 16)
                    smc.inc(sy.dma_start(coefT[:], coeft_sp), 16)
                    smc.inc(sy.dma_start(coef[:], coef_sp), 16)
                    wg(sy, soV, "mzb")
                    for c_ in range(NB * 2):
                        spf.inc(sy.dma_start(xsort_sp[c_ * 128:(c_ + 1) * 128, :], zbm[:]), 16)
                    sy.wait_ge(sxn[0].h, sxn[0].n)
                    sy.wait_ge(sxn[1].h, sxn[1].n)
                    for ch in range(NCH):
                        if ch >= 2:
                            sy.wait_ge(ssc[ch % 2].h, 16 * (ch // 2))
                        smx[ch % 2].inc(sy.dma_start(xsb[ch % 2][:], xs2_sp[ch * 128:(ch + 1) * 128, :]), 16)
                    if dbg:
                        wg(sy, soV, "mslot")
                        wg(sy, soV, "mwi")
                        wg(sy, soV, "msc")
                        smc.inc(sy.dma_start(d_rt[:, 0:16], slotf[:]), 16)
                        smc.inc(sy.dma_start(d_rt[:, 16:31], ebc[:]), 16)
                        smc.inc(sy.dma_start(d_rt[:, 32:47], bif[:]), 16)
                        smc.inc(sy.dma_start(d_rt[:, 48:64], sc16[:]), 16)

                @blk.tensor
                def _(t):
                    t.wait_ge(smc.h, 16 * 10)
                    wg(t, soV, ("woax", NCH - 1))
                    for n in range(4):
                        if n >= 1:
                            wg(t, soA, ("mcse", n - 1))
                        ins = None
                        for c_ in range(4 * n + 4):
                            j = c_ - 4 * n
                            rhs = trit[:, 512:1024] if j < 0 else trit[:, 384 - 128 * j:896 - 128 * j]
                            ins = t.matmul(ps_c[0:8, 0:512], ohb[:, c_ * NE:(c_ + 1) * NE], rhs,
                                           start=(c_ == 0), stop=(c_ == 4 * n + 3))
                        soT.inc(ins)
                        mark(("mcs", n), soT)
                    for ch in range(NCH):
                        wg(t, soA, ("mcse", ch // 4))
                        if ch >= 2:
                            wg(t, soA, ("mcte", ch - 2))
                        soT.inc(t.transpose(ps_t8[ch % 2][:, 0:8], Cc[0:8, ch * 128:(ch + 1) * 128],
                                            idm[0:8, 0:8]))
                        mark(("mct", ch), soT)
                    wg(t, soV, "mbs")
                    soT.inc(t.matmul(ps_sm[0:8, 0:1], tri8b[0:8, 0:8], bs256b[0:8, 0:1],
                                     start=True, stop=True))
                    mark("mpo", soT)
                    wg(t, soA, "mpoe")
                    soT.inc(t.transpose(ps_sm[0:1, 64:72], padoffs[0:8, 0:1], idm[0:8, 0:8]))
                    mark("mpot", soT)
                    wg(t, soA, "mpote")
                    soT.inc(t.matmul(ps_bc[:, 0:8], ok1[0:1, :], porow[0:1, 0:8], start=True, stop=True))
                    mark("mpb", soT)
                    wg(t, soV, "mcmp")
                    soT.inc(t.matmul(ps_sm[0:1, 32:47], rt8[0:8, 23:24], cmp_[0:8, 0:15],
                                     start=True, stop=True))
                    mark("meb", soT)
                    wg(t, soV, "mebr")
                    wg(t, soA, "mpbe")
                    soT.inc(t.matmul(ps_bc[:, 16:31], ok1[0:1, :], ebr[0:1, 0:15], start=True, stop=True))
                    mark("mebc", soT)

                @blk.scalar
                def _(s):
                    for n in range(4):
                        wg(s, soT, ("mcs", n))
                        cs(s, "activation", Cc[0:8, n * 512:(n + 1) * 512], ps_c[0:8, 0:512], AF.Copy)
                        mark(("mcse", n), soA)
                    for ch in range(NCH):
                        wg(s, soT, ("mct", ch))
                        cs(s, "activation", Ctok[:, ch * NE:(ch + 1) * NE], ps_t8[ch % 2][:, 0:8], AF.Copy)
                        mark(("mcte", ch), soA)
                    wg(s, soT, "mpo")
                    cs(s, "activation", padoffs[0:8, 0:1], ps_sm[0:8, 0:1], AF.Copy)
                    mark("mpoe", soA)
                    wg(s, soT, "mpot")
                    cs(s, "activation", porow[0:1, 0:8], ps_sm[0:1, 64:72], AF.Copy)
                    mark("mpote", soA)
                    wg(s, soT, "mpb")
                    cs(s, "activation", padb[:, 0:8], ps_bc[:, 0:8], AF.Copy)
                    mark("mpbe", soA)
                    wg(s, soT, "meb")
                    cs(s, "activation", ebrE[0:1, 0:15], ps_sm[0:1, 32:47], AF.Copy)
                    mark("mebe", soA)
                    wg(s, soT, "mebc")
                    cs(s, "activation", ebc[:, 0:15], ps_bc[:, 16:31], AF.Copy)
                    mark("mrtdone", soA)

                @blk.vector
                def _(v):
                    cv(v, "memset", zbm[:], 0.0)
                    mark("mzb", soV)
                    wg(v, soA, ("mcse", 3))
                    cv(v, "tensor_scalar", tmp8[0:8, 0:8], rt8[0:8, 0:8], Cc[0:8, 2047:2048],
                       None, ALU.is_le)
                    cv(v, "reduce_sum", bsum[0:8, 0:1], tmp8[0:8, 0:8], axis=AX.X)
                    cv(v, "tensor_scalar", bs256b[0:8, 0:1], bsum[0:8, 0:1], 256.0, None, ALU.mult)
                    mark("mbs", soV)
                    wg(v, soA, "mpoe")
                    cv(v, "tensor_scalar", cmp_[0:8, 0:15], rt8[0:8, 8:23], padoffs[0:8, 0:1],
                       None, ALU.is_ge)
                    mark("mcmp", soV)
                    wg(v, soA, "mebe")
                    cv(v, "tensor_scalar", ebr[0:1, 0:15], ebrE[0:1, 0:15], -1.0, None, ALU.add)
                    mark("mebr", soV)
                    wg(v, soA, "mrtdone")
                    for q in range(4):
                        cv(v, "tensor_scalar", wif[:, q * 15:(q + 1) * 15], ebc[:, 0:15],
                           512.0, iq[:, q:q + 1], ALU.mult, ALU.add)
                    cv(v, "tensor_copy", wi32[:], wif[:])
                    cv(v, "tensor_scalar", bif[:, 0:15], ebc[:, 0:15], 128.0, iq[:, 0:1],
                       ALU.mult, ALU.add)
                    cv(v, "tensor_copy", bi32[:], bif[:])
                    mark("mwi", soV)
                    wg(v, soA, "mpbe")
                    for ch in range(NCH):
                        wg(v, soA, ("mcte", ch))
                        cv(v, "tensor_add", tmpA[:, 0:8], Ctok[:, ch * NE:(ch + 1) * NE], padb[:, 0:8])
                        cv(v, "tensor_mul", tmpA[:, 0:8], tmpA[:, 0:8], ohb[:, ch * NE:(ch + 1) * NE])
                        cv(v, "reduce_sum", slotf[:, ch:ch + 1], tmpA[:, 0:8], axis=AX.X)
                    cv(v, "tensor_scalar", slotf[:], slotf[:], 1.0, None, ALU.subtract)
                    cv(v, "tensor_copy", slot_i[:], slotf[:])
                    mark("mslot", soV)
                    v.wait_ge(smc.h, 16 * 10)
                    for ch in range(NCH):
                        cv(v, "reduce_sum", sc16[:, ch:ch + 1], coef[:, ch * NE:(ch + 1) * NE],
                           axis=AX.X)
                    mark("msc", soV)

                @blk.gpsimd
                def _(g_):
                    wg(g_, soV, "mslot")
                    g_.wait_ge(spf.h, 16 * NB * 2)
                    for ch in range(NCH):
                        g_.wait_ge(smx[ch % 2].h, 16 * (ch // 2 + 1))
                        ssc[ch % 2].inc(g_.indirect_dma_start(
                            xsort_sp[:, :], IndirectOffsetOnAxis(slot_i[:, ch:ch + 1], 0),
                            xsb[ch % 2][:], None), 16)

        ps_trm = [Em(nc.psum_tensor(f"mptr{i}", [128, 128], BF16)) for i in range(2)]
        ps_h = [Em(nc.psum_tensor(f"mph{i}", [128, 256], F32)) for i in range(2)]
        ps_y = [Em(nc.psum_tensor(f"mpy{i}", [128, 1024], F32)) for i in range(2)]

        with nc.Block("moe") as blk:
            @blk.sync
            def _(sy):
                sy.wait_ge(ssc[0].h, 16 * (NCH // 2))
                sy.wait_ge(ssc[1].h, 16 * (NCH // 2))
                for b in range(NB):
                    if b >= 2:
                        wg(sy, soT, ("mtr", b - 2))
                    for th in range(2):
                        sxg[b % 2].inc(sy.dma_start(
                            xgs[b % 2][:, th * 1024:(th + 1) * 1024],
                            xsort_sp[256 * b + 128 * th:256 * b + 128 * (th + 1), :]), 16)
                    if b >= 1:
                        wg(sy, soV, ("mye", b - 1))
                        for th in range(2):
                            syw[(b - 1) % 2].inc(sy.dma_start(
                                ysort_sp[256 * (b - 1) + 128 * th:256 * (b - 1) + 128 * (th + 1), :],
                                ysv[(b - 1) % 2][:, th * 1024:(th + 1) * 1024]), 16)
                wg(sy, soV, ("mye", NB - 1))
                for th in range(2):
                    syw[(NB - 1) % 2].inc(sy.dma_start(
                        ysort_sp[256 * (NB - 1) + 128 * th:256 * (NB - 1) + 128 * (th + 1), :],
                        ysv[(NB - 1) % 2][:, th * 1024:(th + 1) * 1024]), 16)
                sy.wait_ge(sat[0].h, sat[0].n)
                sy.wait_ge(sat[1].h, sat[1].n)
                for tc in range(NCH):
                    if tc >= 2:
                        wg(sy, soV, ("mfin", tc - 2))
                    sax[tc % 2].inc(sy.dma_start(axt2[tc % 2][:], attnx_sp[tc * 128:(tc + 1) * 128, :]), 16)
                    if tc >= 2:
                        sot[tc % 2].inc(sy.dma_start(out_t[(tc - 2) * 128:(tc - 1) * 128, :], outt[tc % 2][:]), 16)
                for tc in range(NCH - 2, NCH):
                    wg(sy, soV, ("mfin", tc))
                    sot[tc % 2].inc(sy.dma_start(out_t[tc * 128:(tc + 1) * 128, :], outt[tc % 2][:]), 16)
                sy.wait_ge(sot[0].h, 16 * (NCH // 2))
                sy.wait_ge(sot[1].h, 16 * (NCH // 2))

            @blk.gpsimd
            def _(g_):
                wg(g_, soV, "mwi")
                for b in range(NB):
                    if b >= 2:
                        wg(g_, soA, ("msiB", b - 2))
                    sgb1[b % 2].inc(g_.indirect_dma_start(
                        b1blk[b % 2][:], None, b1r[:, :],
                        IndirectOffsetOnAxis(bi32[:, b:b + 1], 0)), 16)
                    for q in range(4):
                        ti = 4 * b + q
                        if ti >= 2:
                            wg(g_, soT, ("mhmq", ti - 2))
                        sgw1[ti % 2].inc(g_.indirect_dma_start(
                            w1s[ti % 2][:], None, w1q[:, :],
                            IndirectOffsetOnAxis(wi32[:, q * 15 + b:q * 15 + b + 1], 0)), 16)
                        if ti >= 2:
                            wg(g_, soT, ("mymq", ti - 2))
                        sgw2[ti % 2].inc(g_.indirect_dma_start(
                            w2s[ti % 2][:], None, w2q[:, :],
                            IndirectOffsetOnAxis(wi32[:, q * 15 + b:q * 15 + b + 1], 0)), 16)
                g_.wait_ge(syw[0].h, syw[0].n)
                g_.wait_ge(syw[1].h, syw[1].n)
                for tc in range(NCH):
                    if tc >= 2:
                        wg(g_, soV, ("mfin", tc - 2))
                    syg[tc % 2].inc(g_.indirect_dma_start(
                        ygb[tc % 2][:], None, ysort_sp[:, :],
                        IndirectOffsetOnAxis(slot_i[:, tc:tc + 1], 0)), 16)

            @blk.tensor
            def _(t):
                wg(t, soA, "mrtdone")

                def y_mms(b, q):
                    ti = 4 * b + q
                    t.wait_ge(sgw2[ti % 2].h, 16 * (ti // 2 + 1))
                    if q == 0 and b >= 1:
                        wg(t, soV, ("mye", b - 1))
                    ins = None
                    for ffc in range(8):
                        wg(t, soA, ("msi", b * 32 + q * 8 + ffc))
                        for th in range(2):
                            for dn in range(2):
                                ins = t.matmul(
                                    ps_y[th][:, dn * 512:(dn + 1) * 512],
                                    hslq[q % 2][:, ffc * 256 + th * 128:ffc * 256 + th * 128 + 128],
                                    w2s[ti % 2][:, ffc * 1024 + dn * 512:ffc * 1024 + (dn + 1) * 512],
                                    start=(q == 0 and ffc == 0), stop=(q == 3 and ffc == 7))
                    soT.inc(ins)
                    mark(("mymq", ti), soT)
                    if q == 3:
                        mark(("mym", b), soT)

                nt = 0
                for b in range(NB):
                    t.wait_ge(sxg[b % 2].h, 32 * (b // 2 + 1))
                    for th in range(2):
                        for dc in range(8):
                            if nt >= 2:
                                wg(t, soA, ("mtre", nt - 2))
                            soT.inc(t.transpose(
                                ps_trm[nt % 2][:, 0:128],
                                xgs[b % 2][:, th * 1024 + dc * 128:th * 1024 + (dc + 1) * 128],
                                id_b[:]))
                            mark(("mtrj", nt), soT)
                            nt += 1
                    mark(("mtr", b), soT)
                    for q in range(4):
                        ti = 4 * b + q
                        t.wait_ge(sgw1[ti % 2].h, 16 * (ti // 2 + 1))
                        for ffc in range(8):
                            nh = b * 32 + q * 8 + ffc
                            if nh >= 2:
                                wg(t, soA, ("msi", nh - 2))
                            if ffc == 0:
                                wg(t, soA, ("mtreB", b))
                            ins = None
                            for dc in range(8):
                                ins = t.matmul(
                                    ps_h[nh % 2][:, 0:256],
                                    w1s[ti % 2][:, dc * 1024 + ffc * 128:dc * 1024 + ffc * 128 + 128],
                                    xsT[b % 2][:, dc * 256:(dc + 1) * 256],
                                    start=(dc == 0), stop=(dc == 7))
                            soT.inc(ins)
                            mark(("mhm", nh), soT)
                        mark(("mhmq", ti), soT)
                        if q >= 1:
                            y_mms(b, q - 1)
                    y_mms(b, 3)
                for tc in range(NCH):
                    if tc >= 2:
                        wg(t, soV, ("mfin", tc - 2))
                    else:
                        wg(t, soV, ("mye", NB - 1))
                    ins = t.matmul(ps_y[tc % 2][:, 0:512], coefT[:, tc * 128:(tc + 1) * 128],
                                   b2s[:, 0:512], start=True, stop=True)
                    ins = t.matmul(ps_y[tc % 2][:, 512:1024], coefT[:, tc * 128:(tc + 1) * 128],
                                   b2s[:, 512:1024], start=True, stop=True)
                    soT.inc(ins)
                    mark(("myi", tc), soT)

            @blk.scalar
            def _(s):
                nt = 0
                for b in range(NB):
                    s.wait_ge(sgb1[b % 2].h, 16 * (b // 2 + 1))
                    for th in range(2):
                        for dc in range(8):
                            wg(s, soT, ("mtrj", nt))
                            if b >= 2 and nt % 16 == 0:
                                wg(s, soT, ("mhmq", 4 * (b - 2) + 3))
                            cs(s, "activation",
                               xsT[b % 2][:, dc * 256 + th * 128:dc * 256 + th * 128 + 128],
                               ps_trm[nt % 2][:, 0:128], AF.Copy)
                            mark(("mtre", nt), soA)
                            nt += 1
                    mark(("mtreB", b), soA)
                    for q in range(4):
                        for ffc in range(8):
                            nh = b * 32 + q * 8 + ffc
                            wg(s, soT, ("mhm", nh))
                            ti = 4 * b + q
                            if ti >= 2:
                                wg(s, soT, ("mymq", ti - 2))
                            cs(s, "activation", hslq[q % 2][:, ffc * 256:(ffc + 1) * 256],
                               ps_h[nh % 2][:, 0:256], AF.Silu,
                               bias=b1blk[b % 2][:, q * 8 + ffc:q * 8 + ffc + 1])
                            mark(("msi", nh), soA)
                    mark(("msiB", b), soA)

            @blk.vector
            def _(v):
                for b in range(NB):
                    wg(v, soT, ("mym", b))
                    if b >= 2:
                        v.wait_ge(syw[b % 2].h, 32 * (b // 2))
                    cv(v, "tensor_copy", ysv[b % 2][:, 0:1024], ps_y[0][:])
                    cv(v, "tensor_copy", ysv[b % 2][:, 1024:2048], ps_y[1][:])
                    mark(("mye", b), soV)
                wg(v, soV, "msc")
                for tc in range(NCH):
                    wg(v, soT, ("myi", tc))
                    v.wait_ge(syg[tc % 2].h, 16 * (tc // 2 + 1))
                    v.wait_ge(sax[tc % 2].h, 16 * (tc // 2 + 1))
                    if tc >= 2:
                        v.wait_ge(sot[tc % 2].h, 16 * ((tc - 2) // 2 + 1))
                    cv(v, "scalar_tensor_tensor", outt[tc % 2][:], ygb[tc % 2][:],
                       sc16[:, tc:tc + 1], ps_y[tc % 2][:], ALU.mult, ALU.add)
                    cv(v, "tensor_add", outt[tc % 2][:], outt[tc % 2][:], axt2[tc % 2][:])
                    mark(("mfin", tc), soV)

    ctx.close()
    return nc


def build_nc(dbg=False, upto="full", sub=""):
    m1 = {}
    _build(dbg, upto, {}, m1, sub)
    m2 = {}
    nc = _build(dbg, upto, m1, m2, sub)
    assert m1 == m2, "nondeterministic build marks"
    return nc


_CACHE = {}


def _host_prep(inputs):
    x = np.asarray(inputs["L0_attn_x"], dtype=np.float32)
    wqkv = np.asarray(inputs["L0_attn_w_qkv"], dtype=np.float32)
    idx = np.arange(3 * D).reshape(H, 3, 64)
    perm = np.concatenate([idx[:, 0].ravel(), idx[:, 1].ravel(), idx[:, 2].ravel()])
    wqkv_r = np.ascontiguousarray(wqkv[:, perm])
    w_o = np.ascontiguousarray(np.asarray(inputs["L0_attn_w_o"], dtype=np.float32))
    rms_bc = np.ascontiguousarray(np.broadcast_to(
        np.asarray(inputs["L0_ffn_rms_w"], dtype=np.float32), (128, D)))
    gate_t = np.ascontiguousarray(np.asarray(inputs["L0_ffn_gate_w"], dtype=np.float32).T)
    b1 = np.asarray(inputs["L0_ffn_b1"], dtype=np.float32)
    b1r = np.ascontiguousarray(b1.reshape(NE, 4, 8, 128).transpose(0, 3, 1, 2).reshape(NE * 128, 32))
    b2 = np.ascontiguousarray(np.asarray(inputs["L0_ffn_b2"], dtype=np.float32))
    w1 = np.asarray(inputs["L0_ffn_W1"], dtype=np.float32)
    w2 = np.asarray(inputs["L0_ffn_W2"], dtype=np.float32)
    w1q = np.ascontiguousarray(
        w1.reshape(NE, 8, 128, 4, 1024).transpose(0, 3, 2, 1, 4).reshape(NE * 4 * 128, 8 * 1024))
    w2q = np.ascontiguousarray(
        w2.reshape(NE, 4, 8, 128, 1024).transpose(0, 1, 3, 2, 4).reshape(NE * 4 * 128, 8 * 1024))
    ident = np.eye(128, dtype=np.float32)
    tritab = np.zeros((128, 1152), dtype=np.float32)
    tritab[:, 512:] = 1.0
    tritab[:, 384:512] = (np.arange(128)[:, None] <= np.arange(128)[None, :])
    tritab = tritab.astype(ml_dtypes.bfloat16)
    tri8 = (np.arange(8)[:, None] < np.arange(8)[None, :]).astype(ml_dtypes.bfloat16)
    rt8 = np.zeros((8, 24), dtype=np.float32)
    rt8[:, 0:8] = 256.0 * np.arange(8)[None, :] + 1.0
    rt8[:, 8:23] = 256.0 * np.arange(15)[None, :]
    rt8[:, 23] = 1.0
    iq_c = (np.arange(128)[:, None] + 128.0 * np.arange(4)[None, :]).astype(np.float32)
    onesk1 = np.ones((1, 128), dtype=np.float32)
    tri = np.where(np.triu(np.ones((128, 128), np.float32)) > 0, 0.0, -4096.0).astype(np.float32)
    in_maps = []
    rows_all = []
    for c in range(8):
        b_, cc = c // 4, c % 4
        rows = np.concatenate([np.arange((4 * g + cc) * 128, (4 * g + cc) * 128 + 128)
                               for g in range(NG)])
        rows_all.append((b_, rows))
        mk = np.zeros((4, 128, 128), dtype=np.float32)
        for r in range(4):
            if r < cc:
                mk[r] = 0.0
            elif r == cc:
                mk[r] = tri.T
            else:
                mk[r] = -4096.0
        in_maps.append(dict(
            x_own=np.ascontiguousarray(x[b_].reshape(S, D)[rows]),
            x_str=np.ascontiguousarray(x[b_]),
            wqkv=wqkv_r, w_o_in=w_o, rms_bc=rms_bc, gate_t=gate_t,
            b2_in=b2, w1q=w1q, w2q=w2q, b1r=b1r, tritab_in=tritab, tri8_in=tri8,
            rt8_in=rt8, iq_in=iq_c, onesk1_in=onesk1,
            masks=np.ascontiguousarray(mk), ident=ident))
    return in_maps, rows_all


def kernel(**inputs):
    if "nc" not in _CACHE:
        _CACHE["nc"] = build_nc(dbg=False, upto="full")
    nc = _CACHE["nc"]
    in_maps, rows_all = _host_prep(inputs)
    res = run_bass_kernel_spmd(nc, in_maps, core_ids=list(range(8)))
    out = np.empty((B, S, D), dtype=np.float32)
    for c in range(8):
        b_, rows = rows_all[c]
        out[b_].reshape(S, D)[rows] = res.results[c]["out"]
    return out


# version 10 (final)
# speedup vs baseline: 1.5307x; 1.0668x over previous
import numpy as np
import ml_dtypes
import concourse.bass as bass
import concourse.mybir as mybir
from concourse.bass import IndirectOffsetOnAxis
from concourse.bass_utils import run_bass_kernel_spmd
from contextlib import ExitStack

F32 = mybir.dt.float32
BF16 = mybir.dt.bfloat16
I32 = mybir.dt.int32
AF = mybir.ActivationFunctionType
ALU = mybir.AluOpType
AX = mybir.AxisListType

B, S, D = 2, 8192, 1024
H = 16
FF, NE = 4096, 8
TOWN, NCH = 2048, 16
NG, GSZ = 16, 4
NPAIR = 8
SC = 1.0 / 32.0
EPS_LN, EPS_RMS = 1e-5, 1e-6


def _pieces(g):
    qlo = 128 * g
    out = []
    p = qlo
    fe = min(2048, (qlo // 512 + 1) * 512)
    if fe > p:
        out.append((p, fe))
        p = fe
    while p < 2048:
        out.append((p, p + 512))
        p += 512
    return out


def _lspans(g):
    qlo = 128 * g
    out = []
    p = qlo
    fe = min(TOWN, (qlo // 256 + 1) * 256)
    if fe > p:
        out.append((p, fe))
        p = fe
    while p < TOWN:
        out.append((p, p + 256))
        p += 256
    return out


class Sem:
    def __init__(self, nc, name):
        self.h = nc.alloc_semaphore(name)
        self.n = 0

    def inc(self, instr, k=1):
        instr.then_inc(self.h, k)
        self.n += k
        return self.n


def _build(dbg, upto, use, rec, sub=""):
    nc = bass.Bass("TRN2", target_bir_lowering=False)
    DT = nc.dram_tensor
    x_own = DT("x_own", [TOWN, D], F32, kind="ExternalInput").ap()
    x_str = DT("x_str", [S, D], F32, kind="ExternalInput").ap()
    wqkv = DT("wqkv", [D, 3 * D], F32, kind="ExternalInput").ap()
    w_o_in = DT("w_o_in", [D, D], F32, kind="ExternalInput").ap()
    rms_bc = DT("rms_bc", [128, D], F32, kind="ExternalInput").ap()
    gate_t = DT("gate_t", [D, NE], F32, kind="ExternalInput").ap()
    b2_in = DT("b2_in", [NE, D], F32, kind="ExternalInput").ap()
    w1q = DT("w1q", [NE * 4 * 128, 8 * 1024], F32, kind="ExternalInput").ap()
    w2q = DT("w2q", [NE * 4 * 128, 8 * 1024], F32, kind="ExternalInput").ap()
    b1r = DT("b1r", [NE * 128, 32], F32, kind="ExternalInput").ap()
    tritab_in = DT("tritab_in", [128, 1152], BF16, kind="ExternalInput").ap()
    tri8_in = DT("tri8_in", [8, 8], BF16, kind="ExternalInput").ap()
    rt8_in = DT("rt8_in", [8, 24], F32, kind="ExternalInput").ap()
    iq_in = DT("iq_in", [128, 4], F32, kind="ExternalInput").ap()
    onesk1_in = DT("onesk1_in", [1, 128], F32, kind="ExternalInput").ap()
    masks = DT("masks", [4, 128, 128], F32, kind="ExternalInput").ap()
    ident = DT("ident", [128, 128], F32, kind="ExternalInput").ap()
    out_t = DT("out", [TOWN, D], F32, kind="ExternalOutput").ap()
    dkind = "ExternalOutput" if dbg else "Internal"
    NSLOT = 15 * 256
    xs2_sp = DT("xs2_sp", [TOWN, D], BF16, kind=dkind).ap()
    xsort_sp = DT("xsort_sp", [NSLOT, D], BF16, kind="Internal").ap()
    ysort_sp = DT("ysort_sp", [NSLOT, D], F32, kind="Internal").ap()
    attnx_sp = DT("attnx_sp", [TOWN, D], F32, kind=dkind).ap()
    l_tab = DT("l_tab", [H, NG, TOWN], F32, kind=dkind).ap()
    l_tab2 = l_tab.rearrange("h g q -> (h g) q")
    if dbg:
        d_qt = DT("d_qt", [128, NPAIR * TOWN], BF16, kind="ExternalOutput").ap()
        d_coef = DT("d_coef", [128, NCH * NE], F32, kind="ExternalOutput").ap()
        d_ot = DT("d_ot", [128, 8 * TOWN], F32, kind="ExternalOutput").ap()
        d_rt = DT("d_rt", [128, 64], F32, kind="ExternalOutput").ap()

    coeft_sp = DT("coeft_sp", [8, NCH * 128], F32, kind="Internal").ap()
    ohb_sp = DT("ohb_sp", [128, NCH * NE], BF16, kind="Internal").ap()
    w1qb = DT("w1qb", [NE * 4 * 128, 8 * 1024], BF16, kind="Internal").ap()
    w2qb = DT("w2qb", [NE * 4 * 128, 8 * 1024], BF16, kind="Internal").ap()
    coef_sp = DT("coef_sp", [128, NCH * NE], F32, kind="Internal").ap()
    ctx = ExitStack()
    E = ctx.enter_context
    maskb = E(nc.sbuf_tensor("maskb", [128, 4 * 128], BF16))
    id_b = E(nc.sbuf_tensor("id_b", [128, 128], BF16))
    eps5 = E(nc.sbuf_tensor("eps5", [128, 1], F32))
    ctxB = ExitStack()
    EB = ctxB.enter_context
    wq_kv = EB(nc.sbuf_tensor("wq_kv", [128, 8 * 2048], BF16, side="right"))
    QT = [EB(nc.sbuf_tensor(f"qt{g}", [128, TOWN], BF16, side="right")) for g in range(NPAIR)]

    soV = Sem(nc, "soV")
    soA = Sem(nc, "soA")
    soT = Sem(nc, "soT")
    sw = Sem(nc, "s_w")
    szf = Sem(nc, "s_zf")
    ssp = Sem(nc, "s_sp")
    sldma = Sem(nc, "s_ldma")
    smc = Sem(nc, "s_mc")
    sout_ = Sem(nc, "s_out")
    sxl = Sem(nc, "s_xl")
    sxsb = [Sem(nc, f"s_xs{i}") for i in range(4)]
    slt = [Sem(nc, f"s_lt{i}") for i in range(2)]
    sxo = [Sem(nc, f"s_xo{i}") for i in range(2)]
    sax = [Sem(nc, f"s_ax{i}") for i in range(2)]
    sw1 = [Sem(nc, f"s_w1{i}") for i in range(2)]
    sw2 = [Sem(nc, f"s_w2{i}") for i in range(2)]
    sat = [Sem(nc, f"s_at{i}") for i in range(2)]
    sot = [Sem(nc, f"s_ot{i}") for i in range(2)]
    slp = [Sem(nc, f"s_lp{i}") for i in range(2)]
    stp = [Sem(nc, f"s_tp{i}") for i in range(2)]
    sxn = [Sem(nc, f"s_xn{i}") for i in range(2)]
    smx = [Sem(nc, f"s_mx{i}") for i in range(2)]
    spf = Sem(nc, "s_pf")
    ssc = [Sem(nc, f"s_sc{i}") for i in range(2)]
    sxg = [Sem(nc, f"s_xg{i}") for i in range(2)]
    sgw1 = [Sem(nc, f"s_g1{i}") for i in range(2)]
    sgw2 = [Sem(nc, f"s_g2{i}") for i in range(2)]
    sgb1 = [Sem(nc, f"s_gb{i}") for i in range(2)]
    syw = [Sem(nc, f"s_yw{i}") for i in range(2)]
    syg = [Sem(nc, f"s_yg{i}") for i in range(2)]
    spp = Sem(nc, "s_pp")

    def cv(v, name, *a, **kw):
        v.wait_ge(soV.h, soV.n)
        ins = getattr(v, name)(*a, **kw)
        soV.inc(ins)
        return ins

    def cs(s, name, *a, **kw):
        s.wait_ge(soA.h, soA.n)
        ins = getattr(s, name)(*a, **kw)
        soA.inc(ins)
        return ins

    def mark(key, sem):
        rec[key] = sem.n

    def wg(eng, sem, key):
        eng.wait_ge(sem.h, use.get(key, 0))

    with ExitStack() as sctx:
        zb = sctx.enter_context(nc.sbuf_tensor("zb", [64, 8192], F32))
        with nc.Block("setup") as blk:
            @blk.gpsimd
            def _(g):
                sw.inc(g.dma_start(wq_kv[:].rearrange("p (c j) -> p c j", c=8),
                                   wqkv[:, D:3 * D].rearrange("(c p) j -> p c j", p=128)), 16)
                sw.inc(g.dma_start(maskb[:].rearrange("p (r q) -> p r q", r=4),
                                   masks.rearrange("r k q -> k r q")), 16)
                sw.inc(g.dma_start(id_b[:], ident), 16)

            @blk.vector
            def _(v):
                cv(v, "memset", zb[:], 0.0)
                cv(v, "memset", eps5[:], EPS_LN)
                mark("zb", soV)

            @blk.sync
            def _(sy):
                wg(sy, soV, "zb")
                lt_flat = l_tab.rearrange("h g q -> (h g q)").rearrange("(a b) -> a b", a=64)
                szf.inc(sy.dma_start(lt_flat[:, :], zb[:]), 16)
                sy.wait_ge(szf.h, 16)

    with ExitStack() as actx:
        Ea = actx.enter_context
        wq_q = Ea(nc.sbuf_tensor("wq_q", [128, 8 * D], BF16))
        rmsb = Ea(nc.sbuf_tensor("rmsb", [128, D], F32))
        coefT = Ea(nc.sbuf_tensor("coefT", [8, NCH * 128], F32))
        id_f = Ea(nc.sbuf_tensor("id_f", [128, 128], F32))
        coef = Ea(nc.sbuf_tensor("coef", [128, NCH * NE], F32))
        eps6 = Ea(nc.sbuf_tensor("eps6", [128, 1], F32))
        ghi = Ea(nc.sbuf_tensor("ghi", [128, 8 * NE], BF16))
        glo = Ea(nc.sbuf_tensor("glo", [128, 8 * NE], BF16))
        gf = Ea(nc.sbuf_tensor("gf", [128, 8 * NE], F32))
        xbuf = [Ea(nc.sbuf_tensor(f"a0x{i}", [128, D], F32)) for i in range(4)]
        xnb = [Ea(nc.sbuf_tensor(f"a0n{i}", [128, D], BF16)) for i in range(2)]
        xn2b = [Ea(nc.sbuf_tensor(f"a0n2{i}", [128, D], BF16)) for i in range(2)]
        lob = [Ea(nc.sbuf_tensor(f"a0lo{i}", [128, D], BF16)) for i in range(2)]
        xnf = Ea(nc.sbuf_tensor("a0xnf", [128, D], F32))
        xrf = Ea(nc.sbuf_tensor("a0xrf", [128, D], F32))
        x2f = Ea(nc.sbuf_tensor("a0x2f", [128, D], F32))
        bst = Ea(nc.sbuf_tensor("a0bst", [128, 48], F32))
        mv = Ea(nc.sbuf_tensor("a0mv", [128, 8], F32))
        xnt = Ea(nc.sbuf_tensor("a0xnt", [128, 8 * 512], BF16))
        x2t = Ea(nc.sbuf_tensor("a0x2t", [128, 8 * 512], BF16))
        lot = [Ea(nc.sbuf_tensor(f"a0lot{i}", [128, 8 * 128], BF16)) for i in range(2)]
        cst = {nm: Ea(nc.sbuf_tensor("a0" + nm, [128, 4], F32))
               for nm in "vv rec ms lnv rstd lnm cc".split()}
        g16 = {nm: Ea(nc.sbuf_tensor("a0g" + nm, [128, 16], F32))
               for nm in "gm ngm gs gr".split()}
        Lsb = Ea(nc.sbuf_tensor("a0L", [128, NCH * NE], F32))
        ohb = Ea(nc.sbuf_tensor("a0oh", [128, NCH * NE], BF16))
        egs = Ea(nc.sbuf_tensor("a0eg", [128, NE], F32))
        ps_tr = [Ea(nc.psum_tensor(f"a0ptr{i}", [128, 1024], BF16)) for i in range(2)]
        ps_q = [Ea(nc.psum_tensor(f"a0pq{i}", [128, 512], F32)) for i in range(2)]
        ps_g = [Ea(nc.psum_tensor(f"a0pg{i}", [128, NE], F32)) for i in range(2)]
        ps_ct = Ea(nc.psum_tensor("a0pct", [8, 128], F32))

        with nc.Block("a0") as blk:
            @blk.gpsimd
            def _(g):
                sw.inc(g.dma_start(wq_q[:].rearrange("p (c j) -> p c j", c=8),
                                   wqkv[:, 0:D].rearrange("(c p) j -> p c j", p=128)), 16)

            @blk.sync
            def _(sy):
                sxl.inc(sy.dma_start(rmsb[:], rms_bc), 16)
                sxl.inc(sy.dma_start(id_f[:], ident), 16)
                sxl.inc(sy.dma_start(gf[:].rearrange("p (c e) -> p c e", c=8),
                                     gate_t.rearrange("(c p) e -> p c e", p=128)), 16)

                def spill(ch):
                    if "nospill" in sub:
                        return
                    sxn[ch % 2].inc(sy.dma_start(xs2_sp[ch * 128:(ch + 1) * 128, :],
                                                 xn2b[ch % 2][:]), 16)
                for st in range(4):
                    for i in range(4):
                        ch = st * 4 + i
                        if ch >= 4:
                            wg(sy, soV, ("a0ln", ch - 4))
                        sxsb[ch % 4].inc(sy.dma_start(xbuf[ch % 4][:], x_own[ch * 128:(ch + 1) * 128, :]), 16)
                        if ch >= 4:
                            spill(ch - 4)
                for ch in range(12, 16):
                    wg(sy, soV, ("a0ln", ch))
                    spill(ch)

            @blk.vector
            def _(v):
                c = cst
                cv(v, "memset", eps6[:], EPS_RMS)
                v.wait_ge(sxl.h, 48)
                cv(v, "tensor_copy", ghi[:], gf[:])
                cv(v, "scalar_tensor_tensor", glo[:], ghi[:], -1.0, gf[:], ALU.mult, ALU.add)
                mark("gate_hl", soV)
                for st in range(4):
                    for i in range(4):
                        ch = st * 4 + i
                        v.wait_ge(sxsb[ch % 4].h, 16 * (ch // 4 + 1))
                        cv(v, "bn_stats", bst[:, i * 12:i * 12 + 6], xbuf[ch % 4][:, 0:512])
                        cv(v, "bn_stats", bst[:, i * 12 + 6:i * 12 + 12], xbuf[ch % 4][:, 512:1024])
                    for i in range(4):
                        cv(v, "bn_aggr", mv[:, i * 2:i * 2 + 2], bst[:, i * 12:i * 12 + 12])
                    for i in range(4):
                        cv(v, "tensor_scalar", c["vv"][:, i:i + 1], mv[:, 2 * i + 1:2 * i + 2],
                           EPS_LN, None, ALU.add)
                    for i in range(4):
                        cv(v, "reciprocal", c["rec"][:, i:i + 1], c["vv"][:, i:i + 1])
                    for i in range(4):
                        cv(v, "tensor_mul", c["ms"][:, i:i + 1], mv[:, 2 * i + 1:2 * i + 2],
                           c["rec"][:, i:i + 1])
                    mark(("a0var", st), soV)
                    wg(v, soA, ("a0rstd", st))
                    for i in range(4):
                        ch = st * 4 + i
                        if ch >= 2:
                            wg(v, soT, ("a0tr", ch - 2))
                            v.wait_ge(sxn[ch % 2].h, 16 * (ch // 2))
                        cv(v, "tensor_scalar", xnf[:], xbuf[ch % 4][:], mv[:, 2 * i:2 * i + 1],
                           c["rstd"][:, i:i + 1], ALU.subtract, ALU.mult)
                        cv(v, "tensor_copy", xnb[ch % 2][:], xnf[:])
                        cv(v, "tensor_mul", xrf[:], xnf[:], rmsb[:])
                        cv(v, "tensor_scalar", x2f[:], xrf[:], c["cc"][:, i:i + 1], None, ALU.mult)
                        cv(v, "tensor_copy", xn2b[ch % 2][:], x2f[:])
                        cv(v, "scalar_tensor_tensor", lob[ch % 2][:], xn2b[ch % 2][:], -1.0, x2f[:],
                           ALU.mult, ALU.add)
                        mark(("a0ln", ch), soV)

            @blk.tensor
            def _(t):
                if "notr" in sub:
                    return
                t.wait_ge(sw.h, 64)
                wg(t, soV, "gate_hl")
                for st in range(4):
                    for i in range(4):
                        ch = st * 4 + i
                        wg(t, soV, ("a0ln", ch))
                        for sj, src in enumerate((xnb[ch % 2], xn2b[ch % 2], lob[ch % 2])):
                            for j in range(8):
                                ntr = 24 * ch + 8 * sj + j
                                if ntr >= 2:
                                    wg(t, soA, ("a0te", ntr - 2))
                                soT.inc(t.transpose(ps_tr[ntr % 2][:, 0:128],
                                                    src[:, j * 128:(j + 1) * 128], id_b[:]))
                                mark(("a0trj", ntr), soT)
                        mark(("a0tr", ch), soT)
                        if "nogate" in sub:
                            continue
                        wg(t, soA, ("a0te_ch", ch))
                        if ch >= 2:
                            wg(t, soA, ("a0gl", ch - 2))
                        nmm = 0
                        ins = None
                        for cd in range(8):
                            for src2 in range(3):
                                if src2 == 0:
                                    lhsT, rh = x2t[:, cd * 512 + i * 128:cd * 512 + i * 128 + 128], ghi
                                elif src2 == 1:
                                    lhsT, rh = x2t[:, cd * 512 + i * 128:cd * 512 + i * 128 + 128], glo
                                else:
                                    lhsT, rh = lot[ch % 2][:, cd * 128:(cd + 1) * 128], ghi
                                nmm += 1
                                ins = t.matmul(ps_g[ch % 2][:], lhsT, rh[:, cd * NE:(cd + 1) * NE],
                                               start=(nmm == 1), stop=(nmm == 24))
                        soT.inc(ins)
                        mark(("a0gate", ch), soT)
                    for gp in range(8):
                        if "noqt" in sub:
                            break
                        k = st * 8 + gp
                        if k >= 2:
                            wg(t, soA, ("a0qe", k - 2))
                        ins = None
                        for cd in range(8):
                            ins = t.matmul(ps_q[k % 2][:], wq_q[:, cd * D + gp * 128:cd * D + (gp + 1) * 128],
                                           xnt[:, cd * 512:(cd + 1) * 512], start=(cd == 0), stop=(cd == 7))
                        soT.inc(ins)
                        mark(("a0qt", k), soT)
                if "nogate" not in sub:
                    wg(t, soV, ("a0coef", NCH - 1))
                    for ch in range(NCH):
                        if ch >= 1:
                            wg(t, soV, ("a0ctv", ch - 1))
                        soT.inc(t.transpose(ps_ct[:], coef[:, ch * NE:(ch + 1) * NE], id_f[:]))
                        mark(("a0ct", ch), soT)

            @blk.scalar
            def _(s):
                c = cst
                for st in range(4):
                    wg(s, soV, ("a0var", st))
                    for i in range(4):
                        cs(s, "activation", c["lnv"][:, i:i + 1], mv[:, 2 * i + 1:2 * i + 2], AF.Ln, bias=eps5[:])
                    for i in range(4):
                        cs(s, "activation", c["lnm"][:, i:i + 1], c["ms"][:, i:i + 1], AF.Ln, bias=eps6[:])
                    for i in range(4):
                        cs(s, "activation", c["rstd"][:, i:i + 1], c["lnv"][:, i:i + 1], AF.Exp, scale=-0.5)
                    for i in range(4):
                        cs(s, "activation", c["cc"][:, i:i + 1], c["lnm"][:, i:i + 1], AF.Exp, scale=-0.5)
                    mark(("a0rstd", st), soA)
                    if "notr" in sub or "noev" in sub:
                        continue
                    for i in range(4):
                        ch = st * 4 + i
                        if ch >= 2:
                            wg(s, soT, ("a0gate", ch - 2))
                        if i == 0 and st >= 1:
                            wg(s, soT, ("a0qt", st * 8 - 1))
                            if "nogate" not in sub:
                                wg(s, soT, ("a0gate", st * 4 - 1))
                        for sj, dst in enumerate(("xnt", "x2t", "lot")):
                            for j in range(8):
                                ntr = 24 * ch + 8 * sj + j
                                wg(s, soT, ("a0trj", ntr))
                                psl = ps_tr[ntr % 2][:, 0:128]
                                if dst == "xnt":
                                    cs(s, "activation", xnt[:, j * 512 + i * 128:j * 512 + i * 128 + 128],
                                       psl, AF.Copy)
                                elif dst == "x2t":
                                    cs(s, "activation", x2t[:, j * 512 + i * 128:j * 512 + i * 128 + 128],
                                       psl, AF.Copy)
                                else:
                                    cs(s, "activation", lot[ch % 2][:, j * 128:(j + 1) * 128], psl, AF.Copy)
                                mark(("a0te", ntr), soA)
                        mark(("a0te_ch", ch), soA)
                        if "nogate" not in sub:
                            wg(s, soT, ("a0gate", ch))
                            cs(s, "activation", Lsb[:, ch * NE:(ch + 1) * NE], ps_g[ch % 2][:], AF.Copy)
                            mark(("a0gl", ch), soA)
                    for gp in range(8):
                        if "noqt" in sub:
                            break
                        k = st * 8 + gp
                        wg(s, soT, ("a0qt", k))
                        cs(s, "activation", QT[gp][:, st * 512:(st + 1) * 512], ps_q[k % 2][:], AF.Copy)
                        mark(("a0qe", k), soA)
                if "notr" not in sub and "nogate" not in sub:
                    for ch in range(NCH):
                        wg(s, soV, ("a0ngm", ch))
                        cs(s, "activation", egs[:], Lsb[:, ch * NE:(ch + 1) * NE], AF.Exp,
                           bias=g16["ngm"][:, ch:ch + 1], accum_out=g16["gs"][:, ch:ch + 1])
                        mark(("a0ga", ch), soA)

            @blk.vector
            def _(v):
                if "notr" in sub or "nogate" in sub:
                    return
                for ch in range(NCH):
                    wg(v, soA, ("a0gl", ch))
                    cv(v, "reduce_max", g16["gm"][:, ch:ch + 1], Lsb[:, ch * NE:(ch + 1) * NE], axis=AX.X)
                for ch in range(NCH):
                    cv(v, "tensor_scalar", g16["ngm"][:, ch:ch + 1], g16["gm"][:, ch:ch + 1],
                       -1.0, None, ALU.mult)
                    mark(("a0ngm", ch), soV)
                for ch in range(NCH):
                    wg(v, soA, ("a0ga", ch))
                    cv(v, "reciprocal", g16["gr"][:, ch:ch + 1], g16["gs"][:, ch:ch + 1])
                for ch in range(NCH):
                    cv(v, "tensor_scalar", coef[:, ch * NE:(ch + 1) * NE],
                       Lsb[:, ch * NE:(ch + 1) * NE],
                       g16["gm"][:, ch:ch + 1], g16["gr"][:, ch:ch + 1], ALU.is_equal, ALU.mult)
                    mark(("a0coef", ch), soV)
                    cv(v, "tensor_scalar", ohb[:, ch * NE:(ch + 1) * NE],
                       Lsb[:, ch * NE:(ch + 1) * NE], g16["gm"][:, ch:ch + 1], None, ALU.is_equal)
                    mark(("a0oh", ch), soV)
                for ch in range(NCH):
                    wg(v, soT, ("a0ct", ch))
                    cv(v, "tensor_copy", coefT[:, ch * 128:(ch + 1) * 128], ps_ct[:])
                    mark(("a0ctv", ch), soV)

            @blk.sync
            def _(sy):
                if "nogate" not in sub:
                    wg(sy, soV, ("a0ctv", NCH - 1))
                    ssp.inc(sy.dma_start(coeft_sp, coefT[:]), 16)
                    wg(sy, soV, ("a0coef", NCH - 1))
                    ssp.inc(sy.dma_start(coef_sp, coef[:]), 16)
                    wg(sy, soV, ("a0oh", NCH - 1))
                    ssp.inc(sy.dma_start(ohb_sp, ohb[:]), 16)
                    sy.wait_ge(sxn[0].h, sxn[0].n)
                    sy.wait_ge(sxn[1].h, sxn[1].n)
                if dbg and "notr" not in sub:
                    wg(sy, soA, ("a0qe", 31))
                    for gp in range(NPAIR):
                        ssp.inc(sy.dma_start(d_qt[:, gp * TOWN:(gp + 1) * TOWN], QT[gp][:]), 16)
                    wg(sy, soV, ("a0coef", NCH - 1))
                    ssp.inc(sy.dma_start(d_coef, coef[:]), 16)
                sy.wait_ge(ssp.h, ssp.n)

    if upto == "a0":
        with nc.Block("fin0") as blk:
            @blk.sync
            def _(sy):
                sout_.inc(sy.dma_start(out_t, x_own), 16)
                sy.wait_ge(sout_.h, 16)
        ctx.close()
        return nc

    scg = []
    tot = 0
    for g in range(NG):
        tot += H * 4 * len(_pieces(g))
        scg.append(tot)

    with ExitStack() as wctx:
        OTacc = wctx.enter_context(nc.sbuf_tensor("OTacc", [128, 8 * TOWN], F32))
        with ExitStack() as a1c:
            E1 = a1c.enter_context
            xs = [E1(nc.sbuf_tensor(f"a1x{i}", [128, D], F32)) for i in range(4)]
            xnb1 = [E1(nc.sbuf_tensor(f"a1n{i}", [128, D], BF16)) for i in range(2)]
            bst1 = E1(nc.sbuf_tensor("a1bst", [128, 48], F32))
            mv1 = E1(nc.sbuf_tensor("a1mv", [128, 8], F32))
            xnt1 = E1(nc.sbuf_tensor("a1xnt", [128, 8 * 512], BF16))
            ktg = E1(nc.sbuf_tensor("a1kt", [128, 8 * 512], BF16))
            vg = [E1(nc.sbuf_tensor(f"a1v{i}", [128, H * 65], BF16)) for i in range(4)]
            PT = [[E1(nc.sbuf_tensor(f"a1pt{s0}_{k}", [128, TOWN], BF16)) for k in range(4)]
                  for s0 in range(2)]
            tmpl = [E1(nc.sbuf_tensor(f"a1tl{i}", [1, 256], F32)) for i in range(2)]
            c1 = {nm: E1(nc.sbuf_tensor("a1" + nm, [128, 4], F32))
                  for nm in "lnv rstd".split()}
            ps_s = E1(nc.psum_tensor("a1ps", [128, 2048], F32))
            ps_big = E1(nc.psum_tensor("a1pb", [128, 2048], F32))

            nld_tab = {}
            nld = 0
            for g in range(NG):
                for h in range(H):
                    for li in range(len(_lspans(g))):
                        nld_tab[(g, h, li)] = nld
                        nld += 1
            TOT_L = nld

            with nc.Block("a1") as blk:
                @blk.sync
                def _(sy):
                    def transposes(j):
                        if j >= 1:
                            wg(sy, soT, ("a1vm", j - 1, 3))
                        for i in range(4):
                            s_ = 4 * j + i
                            wg(sy, soV, ("a1xnb", s_))
                            for b_ in range(8):
                                stp[s_ % 2].inc(sy.dma_start(
                                    xnt1[:, b_ * 512 + i * 128:b_ * 512 + i * 128 + 128],
                                    xnb1[s_ % 2][:, b_ * 128:(b_ + 1) * 128],
                                    transpose=True), 16)

                    for g in range(NG):
                        for i in range(4):
                            s_ = 4 * g + i
                            if s_ >= 4:
                                wg(sy, soV, ("a1xnb", s_ - 4))
                            sxsb[s_ % 4].inc(sy.dma_start(xs[s_ % 4][:], x_str[s_ * 128:(s_ + 1) * 128, :]), 16)
                        transposes(g)

                @blk.gpsimd
                def _(g_):
                    def lstores(g):
                        for h in range(H):
                            for li, (a, b) in enumerate(_lspans(g)):
                                nl = nld_tab[(g, h, li)]
                                wg(g_, soV, ("a1lev", g, h, li))
                                slp[nl % 2].inc(g_.dma_start(
                                    l_tab2[h * NG + g:h * NG + g + 1, a:b],
                                    tmpl[nl % 2][0:1, 0:b - a]), 16)

                    for g in range(NG):
                        lstores(g)
                        for u in range(4):
                            ci = g * 4 + u
                            r = (ci % 32) * 128
                            dst, srcw = (w1qb, w1q) if ci < 32 else (w2qb, w2q)
                            spp.inc(g_.dma_start(dst[r:r + 128, :], srcw[r:r + 128, :]), 16)
                    g_.wait_ge(slp[0].h, slp[0].n)
                    g_.wait_ge(slp[1].h, slp[1].n)

                @blk.vector
                def _(v):
                    for i in range(4):
                        cv(v, "memset", vg[i][:].rearrange("p (h j) -> p h j", j=65)[:, :, 64:65], 1.0)

                    def stats(g):
                        for i in range(4):
                            s_ = 4 * g + i
                            v.wait_ge(sxsb[s_ % 4].h, 16 * (4 + s_ // 4 + 1))
                            cv(v, "bn_stats", bst1[:, i * 12:i * 12 + 6], xs[s_ % 4][:, 0:512])
                            cv(v, "bn_stats", bst1[:, i * 12 + 6:i * 12 + 12], xs[s_ % 4][:, 512:1024])
                        for i in range(4):
                            cv(v, "bn_aggr", mv1[:, i * 2:i * 2 + 2], bst1[:, i * 12:i * 12 + 12])
                        mark(("a1var", g), soV)

                    def xnb_one(g, i):
                        if i == 0:
                            wg(v, soA, ("a1rstd", g))
                        s_ = 4 * g + i
                        if s_ >= 2:
                            v.wait_ge(stp[s_ % 2].h, 16 * 8 * (s_ // 2))
                        cv(v, "tensor_scalar", xnb1[s_ % 2][:], xs[s_ % 4][:], mv1[:, 2 * i:2 * i + 1],
                           c1["rstd"][:, i:i + 1], ALU.subtract, ALU.mult)
                        mark(("a1xnb", s_), soV)

                    def kt_ev(j):
                        for p8 in range(8):
                            rb = (p8 % 2) * 512
                            wg(v, soT, ("a1ktm", j, p8))
                            if j >= 1 and p8 == 0:
                                wg(v, soT, ("a1smm", scg[j - 1] - 1))
                            cv(v, "tensor_copy", ktg[:, p8 * 512:(p8 + 1) * 512],
                               ps_big[:, rb:rb + 512])
                            mark(("a1kt", j, p8), soV)

                    stats(0)
                    for i in range(4):
                        xnb_one(0, i)
                    kt_ev(0)
                    for g in range(NG):
                        for i in range(4):
                            wg(v, soT, ("a1vm", g, i))
                            if g >= 1:
                                wg(v, soT, ("a1pv", g - 1, 15))
                            rb = 1024 * (i % 2)
                            cv(v, "tensor_copy",
                               vg[i][:].rearrange("p (h j) -> p h j", j=65)[:, :, 0:64],
                               ps_big[:, rb:rb + 1024].rearrange("p (h j) -> p h j", j=64))
                            mark(("a1ve", g, i), soV)
                        if g + 1 < NG:
                            stats(g + 1)
                            for i in range(4):
                                xnb_one(g + 1, i)
                        qlo = 128 * g
                        for h in range(H):
                            wg(v, soT, ("a1pv", g, h))
                            hc, r0 = h // 2, 64 * (h % 2)
                            dstap = OTacc[r0:r0 + 64, hc * TOWN + qlo:(hc + 1) * TOWN]
                            if g == 0:
                                cv(v, "tensor_copy", dstap, ps_big[0:64, qlo:TOWN])
                            else:
                                cv(v, "tensor_add", dstap, dstap, ps_big[0:64, qlo:TOWN])
                            for li, (a, b) in enumerate(_lspans(g)):
                                nl = nld_tab[(g, h, li)]
                                if nl >= 2:
                                    v.wait_ge(slp[nl % 2].h, 16 * (nl // 2))
                                cv(v, "tensor_copy", tmpl[nl % 2][0:1, 0:b - a], ps_big[64:65, a:b])
                                mark(("a1lev", g, h, li), soV)
                            mark(("a1ote", g, h), soV)
                        if g + 1 < NG:
                            kt_ev(g + 1)

                @blk.tensor
                def _(t):
                    t.wait_ge(sw.h, 64)

                    def kv(j):
                        t.wait_ge(stp[0].h, 16 * 16 * (j + 1))
                        t.wait_ge(stp[1].h, 16 * 16 * (j + 1))
                        if j >= 1:
                            wg(t, soV, ("a1ote", j - 1, 15))
                        for p8 in range(8):
                            rb = (p8 % 2) * 512
                            if p8 >= 2:
                                wg(t, soV, ("a1kt", j, p8 - 2))
                            ins = None
                            for cd in range(8):
                                ins = t.matmul(ps_big[:, rb:rb + 512],
                                               wq_kv[:, cd * 2048 + p8 * 128:cd * 2048 + (p8 + 1) * 128],
                                               xnt1[:, cd * 512:(cd + 1) * 512],
                                               start=(cd == 0), stop=(cd == 7))
                            soT.inc(ins)
                            mark(("a1ktm", j, p8), soT)
                        wg(t, soV, ("a1kt", j, 7))
                        for i in range(4):
                            rb = 1024 * (i % 2)
                            if i >= 2:
                                wg(t, soV, ("a1ve", j, i - 2))
                            ins = None
                            for cd in range(8):
                                lhsT = xnt1[:, cd * 512 + i * 128:cd * 512 + i * 128 + 128]
                                ins = t.matmul(ps_big[:, rb:rb + 512], lhsT,
                                               wq_kv[:, cd * 2048 + 1024:cd * 2048 + 1536],
                                               start=(cd == 0), stop=(cd == 7))
                                ins = t.matmul(ps_big[:, rb + 512:rb + 1024], lhsT,
                                               wq_kv[:, cd * 2048 + 1536:cd * 2048 + 2048],
                                               start=(cd == 0), stop=(cd == 7))
                            soT.inc(ins)
                            mark(("a1vm", j, i), soT)

                    nsc = [0]

                    def scores(g, h):
                        gp, r0 = h // 2, 64 * (h % 2)
                        for k in range(4):
                            for pi, (a, b) in enumerate(_pieces(g)):
                                jj = nsc[0]
                                nsc[0] += 1
                                if jj >= 4:
                                    wg(t, soA, ("a1expj", jj - 4))
                                ins = t.matmul(ps_s[:, (jj % 4) * 512:(jj % 4) * 512 + (b - a)],
                                               ktg[r0:r0 + 64, gp * 512 + k * 128:gp * 512 + (k + 1) * 128],
                                               QT[gp][r0:r0 + 64, a:b], start=True, stop=(pi != 0))
                                if pi == 0:
                                    ins = t.matmul(ps_s[:, (jj % 4) * 512:(jj % 4) * 512 + 128],
                                                   maskb[:, k * 128:(k + 1) * 128], id_b[:],
                                                   start=False, stop=True)
                                soT.inc(ins)
                                mark(("a1smm", jj), soT)

                    def pv(g, h):
                        wg(t, soA, ("a1exp", g, h))
                        if h == 0:
                            wg(t, soV, ("a1ve", g, 3))
                        if h >= 1:
                            wg(t, soV, ("a1ote", g, h - 1))
                        ins = None
                        mw = 128 if h < 15 else 65
                        for (a, b) in _pieces(g):
                            for k in range(4):
                                ins = t.matmul(ps_big[0:mw, a:b], vg[k][:, h * 65:h * 65 + mw],
                                               PT[h % 2][k][:, a:b], start=(k == 0), stop=(k == 3))
                        soT.inc(ins)
                        mark(("a1pv", g, h), soT)

                    kv(0)
                    for g in range(NG):
                        scores(g, 0)
                        for h in range(H):
                            if h + 1 < H:
                                scores(g, h + 1)
                            pv(g, h)
                        if g + 1 < NG:
                            kv(g + 1)

                @blk.scalar
                def _(s):
                    def rstd_chain(g):
                        wg(s, soV, ("a1var", g))
                        for i in range(4):
                            cs(s, "activation", c1["lnv"][:, i:i + 1], mv1[:, 2 * i + 1:2 * i + 2], AF.Ln, bias=eps5[:])
                        for i in range(4):
                            cs(s, "activation", c1["rstd"][:, i:i + 1], c1["lnv"][:, i:i + 1], AF.Exp, scale=-0.5)
                        mark(("a1rstd", g), soA)

                    rstd_chain(0)
                    nsc = 0
                    for g in range(NG):
                        if g + 1 < NG:
                            rstd_chain(g + 1)
                        for h in range(H):
                            if h >= 2:
                                wg(s, soT, ("a1pv", g, h - 2))
                            elif g >= 1:
                                wg(s, soT, ("a1pv", g - 1, 14 + h))
                            for k in range(4):
                                for (a, b) in _pieces(g):
                                    wg(s, soT, ("a1smm", nsc))
                                    cs(s, "activation", PT[h % 2][k][:, a:b],
                                       ps_s[:, (nsc % 4) * 512:(nsc % 4) * 512 + (b - a)],
                                       AF.Exp, scale=SC)
                                    mark(("a1expj", nsc), soA)
                                    nsc += 1
                            mark(("a1exp", g, h), soA)

        ctxB.close()
        with ExitStack() as woc:
            Ew = woc.enter_context
            OTb = Ew(nc.sbuf_tensor("OTb", [128, 8 * TOWN], BF16))
            on16 = Ew(nc.sbuf_tensor("on16", [16, 128], F32))
            wos = Ew(nc.sbuf_tensor("wos", [128, 8 * D], BF16))
            lts = [Ew(nc.sbuf_tensor(f"lts{i}", [16, TOWN], F32)) for i in range(2)]
            rlb = Ew(nc.sbuf_tensor("rlb", [128, TOWN], F32))
            xo = [Ew(nc.sbuf_tensor(f"xo{i}", [128, D], F32)) for i in range(2)]
            axt = [Ew(nc.sbuf_tensor(f"axt{i}", [128, D], F32)) for i in range(2)]
            ps_lb = Ew(nc.psum_tensor("ps_lb", [128, TOWN], F32))
            ps_wo = [Ew(nc.psum_tensor(f"ps_wo{i}", [128, D], F32)) for i in range(2)]

            A0SP = ssp.n

            with nc.Block("wo") as blk:
                @blk.gpsimd
                def _(g_):
                    sw.inc(g_.dma_start(wos[:].rearrange("p (c j) -> p c j", c=8),
                                        w_o_in.rearrange("(c p) j -> p c j", p=128)), 16)

                @blk.sync
                def _(sy):
                    for h in range(H):
                        if h >= 2:
                            wg(sy, soT, ("wolb", h - 2))
                        slt[h % 2].inc(sy.dma_start(lts[h % 2][:], l_tab2[h * NG:(h + 1) * NG, :]), 16)
                    for tc in range(NCH):
                        if tc >= 2:
                            wg(sy, soV, ("woax", tc - 2))
                        sxo[tc % 2].inc(sy.dma_start(xo[tc % 2][:], x_own[tc * 128:(tc + 1) * 128, :]), 16)
                        if tc >= 2:
                            wg(sy, soV, ("woax", tc - 2))
                            sat[tc % 2].inc(sy.dma_start(attnx_sp[(tc - 2) * 128:(tc - 1) * 128, :],
                                                         axt[tc % 2][:]), 16)
                    for tc in range(NCH - 2, NCH):
                        wg(sy, soV, ("woax", tc))
                        sat[tc % 2].inc(sy.dma_start(attnx_sp[tc * 128:(tc + 1) * 128, :], axt[tc % 2][:]), 16)
                    if dbg:
                        wg(sy, soV, ("wootb", H - 1))
                        ssp.inc(sy.dma_start(d_ot, OTacc[:]), 16)
                    sy.wait_ge(ssp.h, ssp.n)
                    sy.wait_ge(sat[0].h, sat[0].n)
                    sy.wait_ge(sat[1].h, sat[1].n)

                @blk.tensor
                def _(t):
                    t.wait_ge(sw.h, 80)
                    wg(t, soV, "on16")
                    for h in range(H):
                        t.wait_ge(slt[h % 2].h, 16 * (h // 2 + 1))
                        if h >= 1:
                            wg(t, soV, ("worl", h - 1))
                        ins = None
                        for pc in range(4):
                            ins = t.matmul(ps_lb[:, pc * 512:(pc + 1) * 512], on16[:],
                                           lts[h % 2][:, pc * 512:(pc + 1) * 512], start=True, stop=True)
                        soT.inc(ins)
                        mark(("wolb", h), soT)
                    for tc in range(NCH):
                        wg(t, soV, ("wootb", H - 1))
                        if tc >= 2:
                            wg(t, soV, ("woax", tc - 2))
                        ins = None
                        for cd in range(8):
                            lhsT = OTb[:, cd * TOWN + tc * 128:cd * TOWN + (tc + 1) * 128]
                            ins = t.matmul(ps_wo[tc % 2][:, 0:512], lhsT, wos[:, cd * D:cd * D + 512],
                                           start=(cd == 0), stop=(cd == 7))
                            ins = t.matmul(ps_wo[tc % 2][:, 512:1024], lhsT, wos[:, cd * D + 512:(cd + 1) * D],
                                           start=(cd == 0), stop=(cd == 7))
                        soT.inc(ins)
                        mark(("womm", tc), soT)

                @blk.vector
                def _(v):
                    cv(v, "memset", on16[:], 1.0)
                    mark("on16", soV)
                    for h in range(H):
                        wg(v, soT, ("wolb", h))
                        cv(v, "reciprocal", rlb[:], ps_lb[:])
                        mark(("worl", h), soV)
                        hc, r0 = h // 2, 64 * (h % 2)
                        cv(v, "tensor_mul", OTb[r0:r0 + 64, hc * TOWN:(hc + 1) * TOWN],
                           OTacc[r0:r0 + 64, hc * TOWN:(hc + 1) * TOWN], rlb[r0:r0 + 64, :])
                        mark(("wootb", h), soV)
                    for tc in range(NCH):
                        wg(v, soT, ("womm", tc))
                        v.wait_ge(sxo[tc % 2].h, 16 * (tc // 2 + 1))
                        if tc >= 2:
                            v.wait_ge(sat[tc % 2].h, 16 * ((tc - 2) // 2 + 1))
                        cv(v, "tensor_add", axt[tc % 2][:], xo[tc % 2][:], ps_wo[tc % 2][:])
                        mark(("woax", tc), soV)

    A1SP = ssp.n
    NB = 15
    with ExitStack() as mctx:
        Em = mctx.enter_context
        w1s = [Em(nc.sbuf_tensor(f"mw1{i}", [128, 8 * 1024], BF16)) for i in range(2)]
        w2s = [Em(nc.sbuf_tensor(f"mw2{i}", [128, 8 * 1024], BF16)) for i in range(2)]
        b1blk = [Em(nc.sbuf_tensor(f"mb1{i}", [128, 32], F32)) for i in range(2)]
        b2s = Em(nc.sbuf_tensor("mb2", [8, D], F32))
        coefT = Em(nc.sbuf_tensor("mcoefT", [8, NCH * 128], F32))
        coef = Em(nc.sbuf_tensor("mcoef", [128, NCH * NE], F32))
        xgs = [Em(nc.sbuf_tensor(f"mxg{i}", [128, 2048], BF16)) for i in range(2)]
        xsT = [Em(nc.sbuf_tensor(f"mxt{i}", [128, 2048], BF16)) for i in range(2)]
        hslq = [Em(nc.sbuf_tensor(f"mhs{i}", [128, 2048], BF16)) for i in range(2)]
        ysv = [Em(nc.sbuf_tensor(f"mys{i}", [128, 2048], F32)) for i in range(2)]
        ygb = [Em(nc.sbuf_tensor(f"myg{i}", [128, D], F32)) for i in range(2)]
        axt2 = [Em(nc.sbuf_tensor(f"max{i}", [128, D], F32)) for i in range(2)]
        outt = [Em(nc.sbuf_tensor(f"mo{i}", [128, D], F32)) for i in range(2)]
        slotf = Em(nc.sbuf_tensor("mslotf", [128, NCH], F32))
        slot_i = Em(nc.sbuf_tensor("msloti", [128, NCH], I32))
        wif = Em(nc.sbuf_tensor("mwif", [128, 60], F32))
        wi32 = Em(nc.sbuf_tensor("mwi32", [128, 60], I32))
        bif = Em(nc.sbuf_tensor("mbif", [128, 15], F32))
        bi32 = Em(nc.sbuf_tensor("mbi32", [128, 15], I32))
        sc16 = Em(nc.sbuf_tensor("msc16", [128, NCH], F32))
        ebc = Em(nc.sbuf_tensor("mebc", [128, 15], F32))
        iq = Em(nc.sbuf_tensor("miq", [128, 4], F32))
        zbm = Em(nc.sbuf_tensor("mzbm", [128, D], BF16))

        with ExitStack() as rctx:
            Er = rctx.enter_context
            idm = Er(nc.sbuf_tensor("midm", [128, 128], F32))
            ohb = Er(nc.sbuf_tensor("mohb", [128, NCH * NE], BF16))
            trit = Er(nc.sbuf_tensor("mtrit", [128, 1152], BF16))
            tri8b = Er(nc.sbuf_tensor("mtri8", [8, 8], BF16))
            rt8 = Er(nc.sbuf_tensor("mrt8", [8, 24], F32))
            ok1 = Er(nc.sbuf_tensor("mok1", [1, 128], F32))
            Cc = Er(nc.sbuf_tensor("mCc", [8, TOWN], F32))
            Ctok = Er(nc.sbuf_tensor("mCtok", [128, NCH * NE], F32))
            tmp8 = Er(nc.sbuf_tensor("mtmp8", [8, 8], F32))
            bsum = Er(nc.sbuf_tensor("mbsum", [8, 1], F32))
            bs256b = Er(nc.sbuf_tensor("mbs256", [8, 1], BF16))
            padoffs = Er(nc.sbuf_tensor("mpoffs", [8, 1], F32))
            porow = Er(nc.sbuf_tensor("mporow", [1, 8], F32))
            padb = Er(nc.sbuf_tensor("mpadb", [128, 8], F32))
            cmp_ = Er(nc.sbuf_tensor("mcmpb", [8, 15], F32))
            ebrE = Er(nc.sbuf_tensor("mebrE", [1, 15], F32))
            ebr = Er(nc.sbuf_tensor("mebr", [1, 15], F32))
            tmpA = Er(nc.sbuf_tensor("mtmpA", [128, 8], F32))
            xsb = [Er(nc.sbuf_tensor(f"mxsb{i}", [128, D], BF16)) for i in range(2)]
            ps_c = Er(nc.psum_tensor("mpc", [8, 512], F32))
            ps_t8 = [Er(nc.psum_tensor(f"mpt8{i}", [128, 8], F32)) for i in range(2)]
            ps_sm = Er(nc.psum_tensor("mpsm", [8, 128], F32))
            ps_bc = Er(nc.psum_tensor("mpbc", [128, 32], F32))

            with nc.Block("mrt") as blk:
                @blk.sync
                def _(sy):
                    sy.wait_ge(ssp.h, A1SP)
                    smc.inc(sy.dma_start(idm[:], ident), 16)
                    smc.inc(sy.dma_start(ohb[:], ohb_sp), 16)
                    smc.inc(sy.dma_start(trit[:], tritab_in), 16)
                    smc.inc(sy.dma_start(tri8b[:], tri8_in), 16)
                    smc.inc(sy.dma_start(rt8[:], rt8_in), 16)
                    smc.inc(sy.dma_start(ok1[:], onesk1_in), 16)
                    smc.inc(sy.dma_start(iq[:], iq_in), 16)
                    smc.inc(sy.dma_start(b2s[:], b2_in), 16)
                    smc.inc(sy.dma_start(coefT[:], coeft_sp), 16)
                    smc.inc(sy.dma_start(coef[:], coef_sp), 16)
                    wg(sy, soV, "mzb")
                    for c_ in range(NB * 2):
                        spf.inc(sy.dma_start(xsort_sp[c_ * 128:(c_ + 1) * 128, :], zbm[:]), 16)
                    sy.wait_ge(sxn[0].h, sxn[0].n)
                    sy.wait_ge(sxn[1].h, sxn[1].n)
                    for ch in range(NCH):
                        if ch >= 2:
                            sy.wait_ge(ssc[ch % 2].h, 16 * (ch // 2))
                        smx[ch % 2].inc(sy.dma_start(xsb[ch % 2][:], xs2_sp[ch * 128:(ch + 1) * 128, :]), 16)
                    if dbg:
                        wg(sy, soV, "mslot")
                        wg(sy, soV, "mwi")
                        wg(sy, soV, "msc")
                        smc.inc(sy.dma_start(d_rt[:, 0:16], slotf[:]), 16)
                        smc.inc(sy.dma_start(d_rt[:, 16:31], ebc[:]), 16)
                        smc.inc(sy.dma_start(d_rt[:, 32:47], bif[:]), 16)
                        smc.inc(sy.dma_start(d_rt[:, 48:64], sc16[:]), 16)

                @blk.tensor
                def _(t):
                    t.wait_ge(smc.h, 16 * 10)
                    wg(t, soV, ("woax", NCH - 1))
                    for n in range(4):
                        if n >= 1:
                            wg(t, soA, ("mcse", n - 1))
                        ins = None
                        for c_ in range(4 * n + 4):
                            j = c_ - 4 * n
                            rhs = trit[:, 512:1024] if j < 0 else trit[:, 384 - 128 * j:896 - 128 * j]
                            ins = t.matmul(ps_c[0:8, 0:512], ohb[:, c_ * NE:(c_ + 1) * NE], rhs,
                                           start=(c_ == 0), stop=(c_ == 4 * n + 3))
                        soT.inc(ins)
                        mark(("mcs", n), soT)
                    for ch in range(NCH):
                        wg(t, soA, ("mcse", ch // 4))
                        if ch >= 2:
                            wg(t, soA, ("mcte", ch - 2))
                        soT.inc(t.transpose(ps_t8[ch % 2][:, 0:8], Cc[0:8, ch * 128:(ch + 1) * 128],
                                            idm[0:8, 0:8]))
                        mark(("mct", ch), soT)
                    wg(t, soV, "mbs")
                    soT.inc(t.matmul(ps_sm[0:8, 0:1], tri8b[0:8, 0:8], bs256b[0:8, 0:1],
                                     start=True, stop=True))
                    mark("mpo", soT)
                    wg(t, soA, "mpoe")
                    soT.inc(t.transpose(ps_sm[0:1, 64:72], padoffs[0:8, 0:1], idm[0:8, 0:8]))
                    mark("mpot", soT)
                    wg(t, soA, "mpote")
                    soT.inc(t.matmul(ps_bc[:, 0:8], ok1[0:1, :], porow[0:1, 0:8], start=True, stop=True))
                    mark("mpb", soT)
                    wg(t, soV, "mcmp")
                    soT.inc(t.matmul(ps_sm[0:1, 32:47], rt8[0:8, 23:24], cmp_[0:8, 0:15],
                                     start=True, stop=True))
                    mark("meb", soT)
                    wg(t, soV, "mebr")
                    wg(t, soA, "mpbe")
                    soT.inc(t.matmul(ps_bc[:, 16:31], ok1[0:1, :], ebr[0:1, 0:15], start=True, stop=True))
                    mark("mebc", soT)

                @blk.scalar
                def _(s):
                    for n in range(4):
                        wg(s, soT, ("mcs", n))
                        cs(s, "activation", Cc[0:8, n * 512:(n + 1) * 512], ps_c[0:8, 0:512], AF.Copy)
                        mark(("mcse", n), soA)
                    for ch in range(NCH):
                        wg(s, soT, ("mct", ch))
                        cs(s, "activation", Ctok[:, ch * NE:(ch + 1) * NE], ps_t8[ch % 2][:, 0:8], AF.Copy)
                        mark(("mcte", ch), soA)
                    wg(s, soT, "mpo")
                    cs(s, "activation", padoffs[0:8, 0:1], ps_sm[0:8, 0:1], AF.Copy)
                    mark("mpoe", soA)
                    wg(s, soT, "mpot")
                    cs(s, "activation", porow[0:1, 0:8], ps_sm[0:1, 64:72], AF.Copy)
                    mark("mpote", soA)
                    wg(s, soT, "mpb")
                    cs(s, "activation", padb[:, 0:8], ps_bc[:, 0:8], AF.Copy)
                    mark("mpbe", soA)
                    wg(s, soT, "meb")
                    cs(s, "activation", ebrE[0:1, 0:15], ps_sm[0:1, 32:47], AF.Copy)
                    mark("mebe", soA)
                    wg(s, soT, "mebc")
                    cs(s, "activation", ebc[:, 0:15], ps_bc[:, 16:31], AF.Copy)
                    mark("mrtdone", soA)

                @blk.vector
                def _(v):
                    cv(v, "memset", zbm[:], 0.0)
                    mark("mzb", soV)
                    wg(v, soA, ("mcse", 3))
                    cv(v, "tensor_scalar", tmp8[0:8, 0:8], rt8[0:8, 0:8], Cc[0:8, 2047:2048],
                       None, ALU.is_le)
                    cv(v, "reduce_sum", bsum[0:8, 0:1], tmp8[0:8, 0:8], axis=AX.X)
                    cv(v, "tensor_scalar", bs256b[0:8, 0:1], bsum[0:8, 0:1], 256.0, None, ALU.mult)
                    mark("mbs", soV)
                    wg(v, soA, "mpoe")
                    cv(v, "tensor_scalar", cmp_[0:8, 0:15], rt8[0:8, 8:23], padoffs[0:8, 0:1],
                       None, ALU.is_ge)
                    mark("mcmp", soV)
                    wg(v, soA, "mebe")
                    cv(v, "tensor_scalar", ebr[0:1, 0:15], ebrE[0:1, 0:15], -1.0, None, ALU.add)
                    mark("mebr", soV)
                    wg(v, soA, "mrtdone")
                    for q in range(4):
                        cv(v, "tensor_scalar", wif[:, q * 15:(q + 1) * 15], ebc[:, 0:15],
                           512.0, iq[:, q:q + 1], ALU.mult, ALU.add)
                    cv(v, "tensor_copy", wi32[:], wif[:])
                    cv(v, "tensor_scalar", bif[:, 0:15], ebc[:, 0:15], 128.0, iq[:, 0:1],
                       ALU.mult, ALU.add)
                    cv(v, "tensor_copy", bi32[:], bif[:])
                    mark("mwi", soV)
                    wg(v, soA, "mpbe")
                    for ch in range(NCH):
                        wg(v, soA, ("mcte", ch))
                        cv(v, "tensor_add", tmpA[:, 0:8], Ctok[:, ch * NE:(ch + 1) * NE], padb[:, 0:8])
                        cv(v, "tensor_mul", tmpA[:, 0:8], tmpA[:, 0:8], ohb[:, ch * NE:(ch + 1) * NE])
                        cv(v, "reduce_sum", slotf[:, ch:ch + 1], tmpA[:, 0:8], axis=AX.X)
                    cv(v, "tensor_scalar", slotf[:], slotf[:], 1.0, None, ALU.subtract)
                    cv(v, "tensor_copy", slot_i[:], slotf[:])
                    mark("mslot", soV)
                    v.wait_ge(smc.h, 16 * 10)
                    for ch in range(NCH):
                        cv(v, "reduce_sum", sc16[:, ch:ch + 1], coef[:, ch * NE:(ch + 1) * NE],
                           axis=AX.X)
                    mark("msc", soV)

                @blk.gpsimd
                def _(g_):
                    wg(g_, soV, "mslot")
                    g_.wait_ge(spf.h, 16 * NB * 2)
                    for ch in range(NCH):
                        g_.wait_ge(smx[ch % 2].h, 16 * (ch // 2 + 1))
                        ssc[ch % 2].inc(g_.indirect_dma_start(
                            xsort_sp[:, :], IndirectOffsetOnAxis(slot_i[:, ch:ch + 1], 0),
                            xsb[ch % 2][:], None), 16)

        ps_trm = [Em(nc.psum_tensor(f"mptr{i}", [128, 128], BF16)) for i in range(2)]
        ps_h = [Em(nc.psum_tensor(f"mph{i}", [128, 256], F32)) for i in range(2)]
        ps_y = [Em(nc.psum_tensor(f"mpy{i}", [128, 1024], F32)) for i in range(2)]

        with nc.Block("moe") as blk:
            @blk.sync
            def _(sy):
                sy.wait_ge(ssc[0].h, 16 * (NCH // 2))
                sy.wait_ge(ssc[1].h, 16 * (NCH // 2))
                for b in range(NB):
                    if b >= 2:
                        wg(sy, soT, ("mtr", b - 2))
                    for th in range(2):
                        sxg[b % 2].inc(sy.dma_start(
                            xgs[b % 2][:, th * 1024:(th + 1) * 1024],
                            xsort_sp[256 * b + 128 * th:256 * b + 128 * (th + 1), :]), 16)
                    if b >= 1:
                        wg(sy, soV, ("mye", b - 1))
                        for th in range(2):
                            syw[(b - 1) % 2].inc(sy.dma_start(
                                ysort_sp[256 * (b - 1) + 128 * th:256 * (b - 1) + 128 * (th + 1), :],
                                ysv[(b - 1) % 2][:, th * 1024:(th + 1) * 1024]), 16)
                wg(sy, soV, ("mye", NB - 1))
                for th in range(2):
                    syw[(NB - 1) % 2].inc(sy.dma_start(
                        ysort_sp[256 * (NB - 1) + 128 * th:256 * (NB - 1) + 128 * (th + 1), :],
                        ysv[(NB - 1) % 2][:, th * 1024:(th + 1) * 1024]), 16)
                sy.wait_ge(sat[0].h, sat[0].n)
                sy.wait_ge(sat[1].h, sat[1].n)
                for tc in range(NCH):
                    if tc >= 2:
                        wg(sy, soV, ("mfin", tc - 2))
                    sax[tc % 2].inc(sy.dma_start(axt2[tc % 2][:], attnx_sp[tc * 128:(tc + 1) * 128, :]), 16)
                    if tc >= 2:
                        sot[tc % 2].inc(sy.dma_start(out_t[(tc - 2) * 128:(tc - 1) * 128, :], outt[tc % 2][:]), 16)
                for tc in range(NCH - 2, NCH):
                    wg(sy, soV, ("mfin", tc))
                    sot[tc % 2].inc(sy.dma_start(out_t[tc * 128:(tc + 1) * 128, :], outt[tc % 2][:]), 16)
                sy.wait_ge(sot[0].h, 16 * (NCH // 2))
                sy.wait_ge(sot[1].h, 16 * (NCH // 2))

            @blk.gpsimd
            def _(g_):
                wg(g_, soV, "mwi")
                g_.wait_ge(spp.h, spp.n)
                for b in range(NB):
                    if b >= 2:
                        wg(g_, soA, ("msiB", b - 2))
                    sgb1[b % 2].inc(g_.indirect_dma_start(
                        b1blk[b % 2][:], None, b1r[:, :],
                        IndirectOffsetOnAxis(bi32[:, b:b + 1], 0)), 16)
                    for q in range(4):
                        ti = 4 * b + q
                        if ti >= 2:
                            wg(g_, soT, ("mhmq", ti - 2))
                        sgw1[ti % 2].inc(g_.indirect_dma_start(
                            w1s[ti % 2][:], None, w1qb[:, :],
                            IndirectOffsetOnAxis(wi32[:, q * 15 + b:q * 15 + b + 1], 0)), 16)
                        if ti >= 2:
                            wg(g_, soT, ("mymq", ti - 2))
                        sgw2[ti % 2].inc(g_.indirect_dma_start(
                            w2s[ti % 2][:], None, w2qb[:, :],
                            IndirectOffsetOnAxis(wi32[:, q * 15 + b:q * 15 + b + 1], 0)), 16)
                g_.wait_ge(syw[0].h, syw[0].n)
                g_.wait_ge(syw[1].h, syw[1].n)
                for tc in range(NCH):
                    if tc >= 2:
                        wg(g_, soV, ("mfin", tc - 2))
                    syg[tc % 2].inc(g_.indirect_dma_start(
                        ygb[tc % 2][:], None, ysort_sp[:, :],
                        IndirectOffsetOnAxis(slot_i[:, tc:tc + 1], 0)), 16)

            @blk.tensor
            def _(t):
                wg(t, soA, "mrtdone")

                def y_mms(b, q):
                    ti = 4 * b + q
                    t.wait_ge(sgw2[ti % 2].h, 16 * (ti // 2 + 1))
                    if q == 0 and b >= 1:
                        wg(t, soV, ("mye", b - 1))
                    ins = None
                    for ffc in range(8):
                        wg(t, soA, ("msi", b * 32 + q * 8 + ffc))
                        for th in range(2):
                            for dn in range(2):
                                ins = t.matmul(
                                    ps_y[th][:, dn * 512:(dn + 1) * 512],
                                    hslq[q % 2][:, ffc * 256 + th * 128:ffc * 256 + th * 128 + 128],
                                    w2s[ti % 2][:, ffc * 1024 + dn * 512:ffc * 1024 + (dn + 1) * 512],
                                    start=(q == 0 and ffc == 0), stop=(q == 3 and ffc == 7))
                    soT.inc(ins)
                    mark(("mymq", ti), soT)
                    if q == 3:
                        mark(("mym", b), soT)

                nt = 0
                for b in range(NB):
                    t.wait_ge(sxg[b % 2].h, 32 * (b // 2 + 1))
                    for th in range(2):
                        for dc in range(8):
                            if nt >= 2:
                                wg(t, soA, ("mtre", nt - 2))
                            soT.inc(t.transpose(
                                ps_trm[nt % 2][:, 0:128],
                                xgs[b % 2][:, th * 1024 + dc * 128:th * 1024 + (dc + 1) * 128],
                                id_b[:]))
                            mark(("mtrj", nt), soT)
                            nt += 1
                    mark(("mtr", b), soT)
                    for q in range(4):
                        ti = 4 * b + q
                        t.wait_ge(sgw1[ti % 2].h, 16 * (ti // 2 + 1))
                        for ffc in range(8):
                            nh = b * 32 + q * 8 + ffc
                            if nh >= 2:
                                wg(t, soA, ("msi", nh - 2))
                            if ffc == 0:
                                wg(t, soA, ("mtreB", b))
                            ins = None
                            for dc in range(8):
                                ins = t.matmul(
                                    ps_h[nh % 2][:, 0:256],
                                    w1s[ti % 2][:, dc * 1024 + ffc * 128:dc * 1024 + ffc * 128 + 128],
                                    xsT[b % 2][:, dc * 256:(dc + 1) * 256],
                                    start=(dc == 0), stop=(dc == 7))
                            soT.inc(ins)
                            mark(("mhm", nh), soT)
                        mark(("mhmq", ti), soT)
                        if q >= 1:
                            y_mms(b, q - 1)
                    y_mms(b, 3)
                for tc in range(NCH):
                    if tc >= 2:
                        wg(t, soV, ("mfin", tc - 2))
                    else:
                        wg(t, soV, ("mye", NB - 1))
                    ins = t.matmul(ps_y[tc % 2][:, 0:512], coefT[:, tc * 128:(tc + 1) * 128],
                                   b2s[:, 0:512], start=True, stop=True)
                    ins = t.matmul(ps_y[tc % 2][:, 512:1024], coefT[:, tc * 128:(tc + 1) * 128],
                                   b2s[:, 512:1024], start=True, stop=True)
                    soT.inc(ins)
                    mark(("myi", tc), soT)

            @blk.scalar
            def _(s):
                nt = 0
                for b in range(NB):
                    s.wait_ge(sgb1[b % 2].h, 16 * (b // 2 + 1))
                    for th in range(2):
                        for dc in range(8):
                            wg(s, soT, ("mtrj", nt))
                            if b >= 2 and nt % 16 == 0:
                                wg(s, soT, ("mhmq", 4 * (b - 2) + 3))
                            cs(s, "activation",
                               xsT[b % 2][:, dc * 256 + th * 128:dc * 256 + th * 128 + 128],
                               ps_trm[nt % 2][:, 0:128], AF.Copy)
                            mark(("mtre", nt), soA)
                            nt += 1
                    mark(("mtreB", b), soA)
                    for q in range(4):
                        for ffc in range(8):
                            nh = b * 32 + q * 8 + ffc
                            wg(s, soT, ("mhm", nh))
                            ti = 4 * b + q
                            if ti >= 2:
                                wg(s, soT, ("mymq", ti - 2))
                            cs(s, "activation", hslq[q % 2][:, ffc * 256:(ffc + 1) * 256],
                               ps_h[nh % 2][:, 0:256], AF.Silu,
                               bias=b1blk[b % 2][:, q * 8 + ffc:q * 8 + ffc + 1])
                            mark(("msi", nh), soA)
                    mark(("msiB", b), soA)

            @blk.vector
            def _(v):
                for b in range(NB):
                    wg(v, soT, ("mym", b))
                    if b >= 2:
                        v.wait_ge(syw[b % 2].h, 32 * (b // 2))
                    cv(v, "tensor_copy", ysv[b % 2][:, 0:1024], ps_y[0][:])
                    cv(v, "tensor_copy", ysv[b % 2][:, 1024:2048], ps_y[1][:])
                    mark(("mye", b), soV)
                wg(v, soV, "msc")
                for tc in range(NCH):
                    wg(v, soT, ("myi", tc))
                    v.wait_ge(syg[tc % 2].h, 16 * (tc // 2 + 1))
                    v.wait_ge(sax[tc % 2].h, 16 * (tc // 2 + 1))
                    if tc >= 2:
                        v.wait_ge(sot[tc % 2].h, 16 * ((tc - 2) // 2 + 1))
                    cv(v, "scalar_tensor_tensor", outt[tc % 2][:], ygb[tc % 2][:],
                       sc16[:, tc:tc + 1], ps_y[tc % 2][:], ALU.mult, ALU.add)
                    cv(v, "tensor_add", outt[tc % 2][:], outt[tc % 2][:], axt2[tc % 2][:])
                    mark(("mfin", tc), soV)

    ctx.close()
    return nc


def build_nc(dbg=False, upto="full", sub=""):
    m1 = {}
    _build(dbg, upto, {}, m1, sub)
    m2 = {}
    nc = _build(dbg, upto, m1, m2, sub)
    assert m1 == m2, "nondeterministic build marks"
    return nc


_CACHE = {}


def _host_prep(inputs):
    x = np.asarray(inputs["L0_attn_x"], dtype=np.float32)
    wqkv = np.asarray(inputs["L0_attn_w_qkv"], dtype=np.float32)
    idx = np.arange(3 * D).reshape(H, 3, 64)
    perm = np.concatenate([idx[:, 0].ravel(), idx[:, 1].ravel(), idx[:, 2].ravel()])
    wqkv_r = np.ascontiguousarray(wqkv[:, perm])
    w_o = np.ascontiguousarray(np.asarray(inputs["L0_attn_w_o"], dtype=np.float32))
    rms_bc = np.ascontiguousarray(np.broadcast_to(
        np.asarray(inputs["L0_ffn_rms_w"], dtype=np.float32), (128, D)))
    gate_t = np.ascontiguousarray(np.asarray(inputs["L0_ffn_gate_w"], dtype=np.float32).T)
    b1 = np.asarray(inputs["L0_ffn_b1"], dtype=np.float32)
    b1r = np.ascontiguousarray(b1.reshape(NE, 4, 8, 128).transpose(0, 3, 1, 2).reshape(NE * 128, 32))
    b2 = np.ascontiguousarray(np.asarray(inputs["L0_ffn_b2"], dtype=np.float32))
    w1 = np.asarray(inputs["L0_ffn_W1"], dtype=np.float32)
    w2 = np.asarray(inputs["L0_ffn_W2"], dtype=np.float32)
    w1q = np.ascontiguousarray(
        w1.reshape(NE, 8, 128, 4, 1024).transpose(0, 3, 2, 1, 4).reshape(NE * 4 * 128, 8 * 1024))
    w2q = np.ascontiguousarray(
        w2.reshape(NE, 4, 8, 128, 1024).transpose(0, 1, 3, 2, 4).reshape(NE * 4 * 128, 8 * 1024))
    ident = np.eye(128, dtype=np.float32)
    tritab = np.zeros((128, 1152), dtype=np.float32)
    tritab[:, 512:] = 1.0
    tritab[:, 384:512] = (np.arange(128)[:, None] <= np.arange(128)[None, :])
    tritab = tritab.astype(ml_dtypes.bfloat16)
    tri8 = (np.arange(8)[:, None] < np.arange(8)[None, :]).astype(ml_dtypes.bfloat16)
    rt8 = np.zeros((8, 24), dtype=np.float32)
    rt8[:, 0:8] = 256.0 * np.arange(8)[None, :] + 1.0
    rt8[:, 8:23] = 256.0 * np.arange(15)[None, :]
    rt8[:, 23] = 1.0
    iq_c = (np.arange(128)[:, None] + 128.0 * np.arange(4)[None, :]).astype(np.float32)
    onesk1 = np.ones((1, 128), dtype=np.float32)
    tri = np.where(np.triu(np.ones((128, 128), np.float32)) > 0, 0.0, -4096.0).astype(np.float32)
    in_maps = []
    rows_all = []
    for c in range(8):
        b_, cc = c // 4, c % 4
        rows = np.concatenate([np.arange((4 * g + cc) * 128, (4 * g + cc) * 128 + 128)
                               for g in range(NG)])
        rows_all.append((b_, rows))
        mk = np.zeros((4, 128, 128), dtype=np.float32)
        for r in range(4):
            if r < cc:
                mk[r] = 0.0
            elif r == cc:
                mk[r] = tri.T
            else:
                mk[r] = -4096.0
        in_maps.append(dict(
            x_own=np.ascontiguousarray(x[b_].reshape(S, D)[rows]),
            x_str=np.ascontiguousarray(x[b_]),
            wqkv=wqkv_r, w_o_in=w_o, rms_bc=rms_bc, gate_t=gate_t,
            b2_in=b2, w1q=w1q, w2q=w2q, b1r=b1r, tritab_in=tritab, tri8_in=tri8,
            rt8_in=rt8, iq_in=iq_c, onesk1_in=onesk1,
            masks=np.ascontiguousarray(mk), ident=ident))
    return in_maps, rows_all


def kernel(**inputs):
    if "nc" not in _CACHE:
        _CACHE["nc"] = build_nc(dbg=False, upto="full")
    nc = _CACHE["nc"]
    in_maps, rows_all = _host_prep(inputs)
    res = run_bass_kernel_spmd(nc, in_maps, core_ids=list(range(8)))
    out = np.empty((B, S, D), dtype=np.float32)
    for c in range(8):
        b_, rows = rows_all[c]
        out[b_].reshape(S, D)[rows] = res.results[c]["out"]
    return out
